# version 03
# baseline (speedup 1.0000x reference)
import os
import numpy as np
import ml_dtypes
import concourse.bass as bass
import concourse.mybir as mybir
from concourse.bass_utils import run_bass_kernel_spmd
from contextlib import ExitStack

F32 = mybir.dt.float32
BF16 = mybir.dt.bfloat16
I32 = mybir.dt.int32
AF = mybir.ActivationFunctionType
ALU = mybir.AluOpType
AX = mybir.AxisListType
bfnp = ml_dtypes.bfloat16

NCORES = 8
SEQ = 2048
DM = 1024
TOK = 2 * SEQ
NT = TOK // 128
LN_EPS = 1e-5
GN_EPS = 1e-6


class Prod:
    def __init__(self, ctx, nc, name, dma=False):
        self.h = ctx.enter_context(nc.semaphore(name))
        self.n = 0
        self.step = 16 if dma else 1

    def inc(self, instr, k=1):
        instr.then_inc(self.h, self.step * k)
        self.n += self.step * k
        return self.n


class KB:
    def __init__(self, nc, ctx, block):
        self.nc = nc
        self.ctx = ctx
        self.mem = ctx
        self.block = block
        self.prods = []
        self.pe = self._prod("pe")
        self.dv = self._prod("dv")
        self.ac = self._prod("ac")
        self.po = self._prod("po")

    def _prod(self, name, dma=False):
        p = Prod(self.ctx, self.nc, name, dma=dma)
        self.prods.append(p)
        return p

    def dsem(self, name, n=1):
        if n == 1:
            return self._prod(name, dma=True)
        return [self._prod(f"{name}{i}", dma=True) for i in range(n)]

    def token(self):
        return [(p, p.n) for p in self.prods if p.n > 0]

    def gate_wait(self, eng, gate):
        if gate:
            for p, n in gate:
                eng.wait_ge(p.h, n)

    def phase(self):
        return _PhaseScope(self)

    def sb(self, name, shape, dtype):
        return self.mem.enter_context(self.nc.sbuf_tensor(name, list(shape), dtype))

    def ps(self, name, shape, dtype):
        return self.mem.enter_context(self.nc.psum_tensor(name, list(shape), dtype))


class _PhaseScope:
    def __init__(self, kb):
        self.kb = kb
        self.stack = ExitStack()

    def __enter__(self):
        self.prev = self.kb.mem
        self.kb.mem = self.stack
        return self

    def __exit__(self, *a):
        self.stack.close()
        self.kb.mem = self.prev
        return False


def host_constants():
    c = {}
    c["c_ident"] = np.eye(128, dtype=bfnp)
    c["c_identf"] = np.eye(128, dtype=np.float32)
    c["c_ones"] = np.ones((1, 128), dtype=np.float32)
    pos = np.arange(SEQ, dtype=np.float32)[:, None]
    ids = np.arange(DM // 2, dtype=np.float32)
    theta = (np.float32(10000.0) ** (np.float32(-2.0) * ids)) / np.float32(DM // 2)
    r = pos * theta[None, :]
    c["c_cos"] = np.cos(r).astype(np.float32)
    c["c_sin"] = np.sin(r).astype(np.float32)
    gind = np.zeros((128, 4), dtype=np.float32)
    for p in range(128):
        gind[p, p // 32] = 1.0
    c["c_gind"] = gind
    c["c_e2"] = np.ascontiguousarray(gind.T)
    return c


def ph_ln(kb, x_dram, xnfm_dram, xntm_dram=None, gate=None, x_done=None, tag=""):
    block = kb.block
    pe, dv, ac = kb.pe, kb.dv, kb.ac

    xt = [kb.sb(f"ln{tag}_x{i}", (128, 1024), F32) for i in range(2)]
    xn32 = [kb.sb(f"ln{tag}_xn32{i}", (128, 1024), F32) for i in range(2)]
    xnbf = [kb.sb(f"ln{tag}_xnbf{i}", (128, 1024), BF16) for i in range(2)]
    stat = kb.sb(f"ln{tag}_stat", (128, 12), F32)
    mv = kb.sb(f"ln{tag}_mv", (128, 2), F32)
    rstd = kb.sb(f"ln{tag}_rstd", (128, 1), F32)
    lnb = kb.sb(f"ln{tag}_lnb", (128, 1), F32)
    fmbuf = [kb.sb(f"ln{tag}_fm{i}", (128, 8, 512), BF16) for i in range(2)]
    ptr = kb.ps(f"ln{tag}_ptr", (128, 1024), BF16)

    xld = kb.dsem(f"ln{tag}_xld", 2)
    tmst = kb.dsem(f"ln{tag}_tmst", 2)
    fmst = kb.dsem(f"ln{tag}_fmst", 2)

    end = [0] * NT
    fm_done = [None] * (NT // 4)
    tm_done = [None] * NT
    for t in range(NT):
        b = t % 2
        sl = t // 4

        def load(s, t=t, b=b):
            if t < 2:
                kb.gate_wait(s, gate)
            if t >= 2:
                s.wait_ge(dv.h, end[t - 2])
            if x_done is not None:
                s.wait_ge(x_done[t][0].h, x_done[t][1])
            xld[b].inc(s.dma_start(out=xt[b][:, :], in_=x_dram[128 * t:128 * (t + 1), :]))
        block.sync(load)
        ld_n = xld[b].n

        def stats(v, t=t, b=b, w=ld_n):
            v.wait_ge(xld[b].h, w)
            if t >= 1:
                v.wait_ge(ac.h, ac.n)
            v.bn_stats(stat[:, 0:6], xt[b][:, 0:512])
            n = dv.inc(v.bn_stats(stat[:, 6:12], xt[b][:, 512:1024]))
            v.wait_ge(dv.h, n)
            dv.inc(v.bn_aggr(mv[:, :], stat[:, :]))
        block.vector(stats)
        aggr_n = dv.n

        def rstd_f(sc, w=aggr_n):
            sc.wait_ge(dv.h, w)
            a = ac.inc(sc.activation(out=rstd[:, :], in_=mv[:, 1:2], func=AF.Ln,
                                     bias=kb.epsln[:, :], scale=1.0))
            sc.wait_ge(ac.h, a)
            ac.inc(sc.activation(out=rstd[:, :], in_=rstd[:, :], func=AF.Exp, scale=-0.5))
        block.scalar(rstd_f)
        rstd_n = ac.n

        def bias_f(v, w=rstd_n):
            v.wait_ge(ac.h, w)
            dv.inc(v.tensor_scalar(out=lnb[:, :], in0=mv[:, 0:1], scalar1=rstd[:, :],
                                   scalar2=-1.0, op0=ALU.mult, op1=ALU.mult))
        block.vector(bias_f)
        bias_n = dv.n

        def apply(sc, t=t, b=b, w=bias_n):
            sc.wait_ge(dv.h, w)
            if xntm_dram is not None:
                if t >= 2:
                    sc.wait_ge(tmst[b].h, tmst[b].n)
                sc.activation(out=xn32[b][:, :], in_=xt[b][:, :], func=AF.Identity,
                              scale=rstd[:, :], bias=lnb[:, :])
            ac.inc(sc.activation(out=xnbf[b][:, :], in_=xt[b][:, :], func=AF.Identity,
                                 scale=rstd[:, :], bias=lnb[:, :]))
        block.scalar(apply)
        xn_n = ac.n

        if xntm_dram is not None:
            def store_tm(s, t=t, b=b, w=xn_n):
                s.wait_ge(ac.h, w)
                tmst[b].inc(s.dma_start(out=xntm_dram[128 * t:128 * (t + 1), :],
                                        in_=xn32[b][:, :]))
            block.sync(store_tm)
            tm_done[t] = (tmst[b], tmst[b].n)

        def transp(te, t=t, b=b, w=xn_n):
            te.wait_ge(ac.h, w)
            if t >= 1:
                te.wait_ge(dv.h, end[t - 1])
            for c in range(8):
                tr = te.transpose(ptr[:, 128 * c:128 * (c + 1)],
                                  xnbf[b][:, 128 * c:128 * (c + 1)], kb.ident[:, :])
            pe.inc(tr)
        block.tensor(transp)
        tr_n = pe.n

        def evac(v, t=t, sl=sl, w=tr_n):
            v.wait_ge(pe.h, w)
            if t % 4 == 0 and sl >= 2:
                v.wait_ge(fmst[sl % 2].h, fmst[sl % 2].n)
            j = t % 4
            fb = fmbuf[sl % 2]
            r = v.tensor_copy(
                fb[:, :, 128 * j:128 * (j + 1)],
                ptr.ap().rearrange("p (c n) -> p c n", n=128))
            dv.inc(r)
            end[t] = dv.n
        block.vector(evac)

        if t % 4 == 3:
            def store_fm(s, sl=sl, t=t, w=end[t]):
                s.wait_ge(dv.h, w)
                fmst[sl % 2].inc(s.dma_start(
                    out=xnfm_dram[:, :, 512 * sl:512 * (sl + 1)].rearrange("c p n -> p c n"),
                    in_=fmbuf[sl % 2][:, :, :]))
            block.scalar(store_fm)
            fm_done[sl] = (fmst[sl % 2], fmst[sl % 2].n)

    return {"xld": xld, "tmst": tmst, "fmst": fmst,
            "fm_done": fm_done, "tm_done": tm_done}


RSC0 = 1.0 / float(np.sqrt(np.float32(16 + 512 + 512)))


def rope_assemble(kb, v, pA, pB, brep_c, brep_r, cs, aq, qtmp, tmp1, tmp2):
    dv = kb.dv
    v.tensor_tensor(out=aq[:, 0:512], in0=pB[:, 0:512], in1=brep_r[:, 0:512], op=ALU.add)
    v.tensor_tensor(out=aq[:, 512:1024], in0=pB[:, 512:1024], in1=brep_r[:, 512:1024], op=ALU.add)
    v.tensor_tensor(out=qtmp[:, 0:512], in0=pA[:, 0:512], in1=brep_c[:, 0:512], op=ALU.add)
    n = dv.inc(v.tensor_tensor(out=qtmp[:, 512:1024], in0=pA[:, 512:1024], in1=brep_c[:, 512:1024], op=ALU.add))
    v.wait_ge(dv.h, n)
    ae = aq[:, 0:1024:2]
    ao = aq[:, 1:1024:2]
    v.tensor_tensor(out=tmp1[:, :], in0=ae, in1=cs[:, 0:512], op=ALU.mult)
    n = dv.inc(v.tensor_tensor(out=tmp2[:, :], in0=ao, in1=cs[:, 512:1024], op=ALU.mult))
    v.wait_ge(dv.h, n)
    n = dv.inc(v.tensor_tensor(out=tmp1[:, :], in0=tmp1[:, :], in1=tmp2[:, :], op=ALU.subtract))
    v.wait_ge(dv.h, n)
    n = dv.inc(v.tensor_tensor(out=qtmp[:, 0:1024:2], in0=qtmp[:, 0:1024:2], in1=tmp1[:, :], op=ALU.add))
    v.wait_ge(dv.h, n)
    v.tensor_tensor(out=tmp1[:, :], in0=ao, in1=cs[:, 0:512], op=ALU.mult)
    n = dv.inc(v.tensor_tensor(out=tmp2[:, :], in0=ae, in1=cs[:, 512:1024], op=ALU.mult))
    v.wait_ge(dv.h, n)
    n = dv.inc(v.tensor_tensor(out=tmp1[:, :], in0=tmp1[:, :], in1=tmp2[:, :], op=ALU.add))
    v.wait_ge(dv.h, n)
    dv.inc(v.tensor_tensor(out=qtmp[:, 1:1024:2], in0=qtmp[:, 1:1024:2], in1=tmp1[:, :], op=ALU.add))


def ph_proj0(kb, xnfm_dram, cos_dram, sin_dram, W, qfm_dram, kfm_dram, v_dram,
             fm_done=None, gate=None, tag="p0"):
    block = kb.block
    pe, dv, ac = kb.pe, kb.dv, kb.ac

    w_sb = {
        "wdq": kb.sb(f"{tag}_wdq", (128, 8, 512), BF16),
        "wdkv": kb.sb(f"{tag}_wdkv", (128, 8, 512), BF16),
        "wkr": kb.sb(f"{tag}_wkr", (128, 8, 1024), BF16),
        "wuq": kb.sb(f"{tag}_wuq", (128, 4, 1024), BF16),
        "wqr": kb.sb(f"{tag}_wqr", (128, 4, 1024), BF16),
        "wuk": kb.sb(f"{tag}_wuk", (128, 4, 1024), BF16),
        "wuv": kb.sb(f"{tag}_wuv", (128, 4, 1024), BF16),
    }
    b_fm = kb.sb(f"{tag}_bfm", (128, 8), F32)
    brep = {nm: kb.sb(f"{tag}_{nm}", (128, 1024), F32)
            for nm in ["brep_uq", "brep_qr", "brep_uk", "brep_kr", "brep_uv"]}

    wld = kb.dsem(f"{tag}_wld")
    bld = kb.dsem(f"{tag}_bld")

    def loadw(g):
        kb.gate_wait(g, gate)
        for nm in ["wdq", "wdkv", "wkr", "wuq", "wqr", "wuk", "wuv"]:
            wld.inc(g.dma_start(out=w_sb[nm][:, :, :],
                                in_=W[nm].rearrange("(k p) j -> p k j", p=128)))
    block.gpsimd(loadw)
    WLD = wld.n

    def loadb(s):
        kb.gate_wait(s, gate)
        bld.inc(s.dma_start(out=b_fm[:, 0:4], in_=W["bdq"]))
        bld.inc(s.dma_start(out=b_fm[:, 4:8], in_=W["bdkv"]))
        for nm in brep:
            bld.inc(s.dma_start(out=brep[nm][:, :], in_=W[nm]))
    block.sync(loadb)
    BLD = bld.n

    xn_sl = [kb.sb(f"{tag}_xn{i}", (128, 8, 512), BF16) for i in range(2)]
    ctq = kb.sb(f"{tag}_ctq", (128, 4, 512), BF16)
    ctkv = kb.sb(f"{tag}_ctkv", (128, 4, 512), BF16)
    cs_t = [kb.sb(f"{tag}_cs{i}", (128, 1024), F32) for i in range(2)]
    aq = kb.sb(f"{tag}_aq", (128, 1024), F32)
    qtmp = kb.sb(f"{tag}_qtmp", (128, 1024), F32)
    tmp1 = kb.sb(f"{tag}_tmp1", (128, 512), F32)
    tmp2 = kb.sb(f"{tag}_tmp2", (128, 512), F32)
    qbf = kb.sb(f"{tag}_qbf", (128, 1024), BF16)
    qfm_buf = [kb.sb(f"{tag}_qfm{i}", (128, 8, 512), BF16) for i in range(2)]
    kfm_buf = [kb.sb(f"{tag}_kfm{i}", (128, 8, 512), BF16) for i in range(2)]
    vt_buf = [kb.sb(f"{tag}_vt{i}", (128, 1040), BF16) for i in range(2)]
    pCT = kb.ps(f"{tag}_pct", (128, 1024), F32)
    pA = kb.ps(f"{tag}_pa", (128, 1024), F32)
    pB = kb.ps(f"{tag}_pb", (128, 1024), F32)
    pTR = kb.ps(f"{tag}_ptrp", (128, 1024), BF16)

    xld = kb.dsem(f"{tag}_xld", 2)
    csld = kb.dsem(f"{tag}_csld", 2)
    qst = kb.dsem(f"{tag}_qst", 2)
    kst = kb.dsem(f"{tag}_kst", 2)
    vst = kb.dsem(f"{tag}_vst", 2)

    def ones_init(v):
        kb.gate_wait(v, gate)
        for i in range(2):
            dv.inc(v.memset(vt_buf[i].ap().rearrange("p (c n) -> p c n", n=65)[:, :, 64:65], 1.0))
    block.vector(ones_init)

    slice_pe_end = [0] * 8
    qkv_done = [None, None]

    for s in range(8):
        b = s % 2
        seq = s // 4
        scol = 512 * (s % 4)

        def loadx(sy, s=s, b=b):
            if s == 0:
                kb.gate_wait(sy, gate)
            if s >= 2:
                sy.wait_ge(pe.h, slice_pe_end[s - 2])
            if fm_done is not None:
                sy.wait_ge(fm_done[s][0].h, fm_done[s][1])
            xld[b].inc(sy.dma_start(
                out=xn_sl[b][:, :, :],
                in_=xnfm_dram[:, :, 512 * s:512 * (s + 1)].rearrange("c p n -> p c n")))
        block.sync(loadx)
        XL = xld[b].n

        ev_at = []
        for half, (wname, csb, bcol) in enumerate([("wdq", ctq, 0), ("wdkv", ctkv, 4)]):
            for m in range(4):
                idx = half * 4 + m

                def ct_mm(t, wname=wname, m=m, b=b, s=s, idx=idx, w=XL):
                    t.wait_ge(xld[b].h, w)
                    if s == 0 and idx == 0:
                        t.wait_ge(wld.h, WLD)
                    if idx >= 2:
                        t.wait_ge(ac.h, ev_at[idx - 2])
                    elif s >= 1:
                        t.wait_ge(ac.h, ct_prev_ev[idx - 2])
                    for kc in range(8):
                        mm = t.matmul(pCT[:, 512 * (idx % 2):512 * (idx % 2 + 1)],
                                      w_sb[wname][:, kc, 128 * m:128 * (m + 1)],
                                      xn_sl[b][:, kc, :],
                                      start=(kc == 0), stop=(kc == 7))
                    pe.inc(mm)
                block.tensor(ct_mm)
                MYW = pe.n

                def ct_ev(sc, half=half, m=m, csb=csb, bcol=bcol, idx=idx, s=s, w=MYW):
                    sc.wait_ge(pe.h, w)
                    if s == 0 and idx == 0:
                        sc.wait_ge(bld.h, BLD)
                    ac.inc(sc.activation(out=csb[:, m, :],
                                         in_=pCT[:, 512 * (idx % 2):512 * (idx % 2 + 1)],
                                         func=AF.Identity,
                                         bias=b_fm[:, bcol + m:bcol + m + 1], scale=1.0))
                block.scalar(ct_ev)
                ev_at.append(ac.n)
        ct_prev_ev = ev_at
        CT_EV = ac.n

        for t4 in range(4):
            t = 4 * s + t4
            bb = t % 2
            prow = 128 * (t % 16)

            def loadcs(sy, bb=bb, prow=prow, t=t):
                if t < 2:
                    kb.gate_wait(sy, gate)
                if t >= 2:
                    sy.wait_ge(dv.h, cs_done[bb])
                csld[bb].inc(sy.dma_start(out=cs_t[bb][:, 0:512],
                                          in_=cos_dram[prow:prow + 128, :]))
                csld[bb].inc(sy.dma_start(out=cs_t[bb][:, 512:1024],
                                          in_=sin_dram[prow:prow + 128, :]))
            block.sync(loadcs)
            CSL = csld[bb].n

            def q_mm(te, b=b, t4=t4, w=CT_EV, s=s):
                te.wait_ge(ac.h, w)
                te.wait_ge(dv.h, dv.n)
                for nb in range(2):
                    for kc in range(4):
                        mm = te.matmul(pA[:, 512 * nb:512 * (nb + 1)],
                                       ctq[:, kc, 128 * t4:128 * (t4 + 1)],
                                       w_sb["wuq"][:, kc, 512 * nb:512 * (nb + 1)],
                                       start=(kc == 0), stop=(kc == 3))
                    for kc in range(4):
                        mm = te.matmul(pB[:, 512 * nb:512 * (nb + 1)],
                                       ctq[:, kc, 128 * t4:128 * (t4 + 1)],
                                       w_sb["wqr"][:, kc, 512 * nb:512 * (nb + 1)],
                                       start=(kc == 0), stop=(kc == 3))
                pe.inc(mm)
            block.tensor(q_mm)
            QMM = pe.n

            def q_rope(v, bb=bb, w=QMM, wcs=CSL):
                v.wait_ge(pe.h, w)
                v.wait_ge(csld[bb].h, wcs)
                if w == kb._first_qmm:
                    v.wait_ge(bld.h, BLD)
                rope_assemble(kb, v, pA, pB, brep["brep_uq"], brep["brep_qr"],
                              cs_t[bb], aq, qtmp, tmp1, tmp2)
            if s == 0 and t4 == 0:
                kb._first_qmm = QMM
            block.vector(q_rope)
            QAS = dv.n

            def q_cast(sc, w=QAS):
                sc.wait_ge(dv.h, w)
                ac.inc(sc.activation(out=qbf[:, :], in_=qtmp[:, :], func=AF.Identity,
                                     scale=RSC0))
            block.scalar(q_cast)
            QC = ac.n

            def q_tr(te, w=QC):
                te.wait_ge(ac.h, w)
                for c in range(8):
                    tr = te.transpose(pTR[:, 128 * c:128 * (c + 1)],
                                      qbf[:, 128 * c:128 * (c + 1)], kb.ident[:, :])
                pe.inc(tr)
            block.tensor(q_tr)
            QTR = pe.n

            def q_ev(v, w=QTR, t4=t4, sl2=s % 2, s=s):
                v.wait_ge(pe.h, w)
                if s >= 2 and t4 == 0:
                    v.wait_ge(qst[sl2].h, qst[sl2].n)
                dv.inc(v.tensor_copy(qfm_buf[sl2][:, :, 128 * t4:128 * (t4 + 1)],
                                     pTR.ap().rearrange("p (c n) -> p c n", n=128)))
            block.vector(q_ev)
            QEV = dv.n

            def k_mm(te, b=b, t4=t4, w=QEV):
                te.wait_ge(dv.h, w)
                for nb in range(2):
                    for kc in range(4):
                        mm = te.matmul(pA[:, 512 * nb:512 * (nb + 1)],
                                       ctkv[:, kc, 128 * t4:128 * (t4 + 1)],
                                       w_sb["wuk"][:, kc, 512 * nb:512 * (nb + 1)],
                                       start=(kc == 0), stop=(kc == 3))
                    for kc in range(8):
                        mm = te.matmul(pB[:, 512 * nb:512 * (nb + 1)],
                                       xn_sl[b][:, kc, 128 * t4:128 * (t4 + 1)],
                                       w_sb["wkr"][:, kc, 512 * nb:512 * (nb + 1)],
                                       start=(kc == 0), stop=(kc == 7))
                pe.inc(mm)
            block.tensor(k_mm)
            KMM = pe.n

            def k_rope(v, bb=bb, w=KMM):
                v.wait_ge(pe.h, w)
                rope_assemble(kb, v, pA, pB, brep["brep_uk"], brep["brep_kr"],
                              cs_t[bb], aq, qtmp, tmp1, tmp2)
            block.vector(k_rope)
            KAS = dv.n

            def k_cast(sc, w=KAS):
                sc.wait_ge(dv.h, w)
                ac.inc(sc.activation(out=qbf[:, :], in_=qtmp[:, :], func=AF.Identity))
            block.scalar(k_cast)
            KC = ac.n

            def k_tr(te, w=KC):
                te.wait_ge(ac.h, w)
                for c in range(8):
                    tr = te.transpose(pTR[:, 128 * c:128 * (c + 1)],
                                      qbf[:, 128 * c:128 * (c + 1)], kb.ident[:, :])
                pe.inc(tr)
            block.tensor(k_tr)
            KTR = pe.n

            def k_ev(v, w=KTR, t4=t4, sl2=s % 2, s=s):
                v.wait_ge(pe.h, w)
                if s >= 2 and t4 == 0:
                    v.wait_ge(kst[sl2].h, kst[sl2].n)
                dv.inc(v.tensor_copy(kfm_buf[sl2][:, :, 128 * t4:128 * (t4 + 1)],
                                     pTR.ap().rearrange("p (c n) -> p c n", n=128)))
            block.vector(k_ev)
            KEV = dv.n

            def v_mm(te, t4=t4, w=KEV):
                te.wait_ge(dv.h, w)
                for nb in range(2):
                    for kc in range(4):
                        mm = te.matmul(pA[:, 512 * nb:512 * (nb + 1)],
                                       ctkv[:, kc, 128 * t4:128 * (t4 + 1)],
                                       w_sb["wuv"][:, kc, 512 * nb:512 * (nb + 1)],
                                       start=(kc == 0), stop=(kc == 3))
                pe.inc(mm)
            block.tensor(v_mm)
            VMM = pe.n

            def v_ev(v, t=t, bb=bb, w=VMM):
                v.wait_ge(pe.h, w)
                if t >= 2:
                    v.wait_ge(vst[bb].h, vst[bb].n)
                v.tensor_tensor(out=qtmp[:, 0:512], in0=pA[:, 0:512],
                                in1=brep["brep_uv"][:, 0:512], op=ALU.add)
                n = dv.inc(v.tensor_tensor(out=qtmp[:, 512:1024], in0=pA[:, 512:1024],
                                           in1=brep["brep_uv"][:, 512:1024], op=ALU.add))
                v.wait_ge(dv.h, n)
                dv.inc(v.tensor_copy(
                    vt_buf[bb].ap().rearrange("p (c n) -> p c n", n=65)[:, :, 0:64],
                    qtmp.ap().rearrange("p (c n) -> p c n", n=64)))
            block.vector(v_ev)
            VEV = dv.n
            if t < 2:
                cs_done = [0, 0]
            cs_done[bb] = VEV

            def v_st(sy, t=t, bb=bb, w=VEV):
                sy.wait_ge(dv.h, w)
                vst[bb].inc(sy.dma_start(
                    out=v_dram[t // 16, 128 * (t % 16):128 * (t % 16) + 128, :],
                    in_=vt_buf[bb][:, :]))
            block.sync(v_st)

        slice_pe_end[s] = pe.n

        def qk_store(sy, b2=s % 2, seq=seq, scol=scol, w=dv.n):
            sy.wait_ge(dv.h, w)
            qst[b2].inc(sy.dma_start(
                out=qfm_dram[seq, :, :, scol:scol + 512].rearrange("c p n -> p c n"),
                in_=qfm_buf[b2][:, :, :]))
            kst[b2].inc(sy.dma_start(
                out=kfm_dram[seq, :, :, scol:scol + 512].rearrange("c p n -> p c n"),
                in_=kfm_buf[b2][:, :, :]))
        block.sync(qk_store)
        if s % 4 == 3:
            qkv_done[seq] = [(qst[0], qst[0].n), (qst[1], qst[1].n),
                             (kst[0], kst[0].n), (kst[1], kst[1].n),
                             (vst[0], vst[0].n), (vst[1], vst[1].n)]

    return {"qst": qst, "kst": kst, "vst": vst, "xld": xld, "csld": csld,
            "wld": [wld], "bld": [bld], "qkv_done": qkv_done}


def ph_attn(kb, qfm_dram, kfm_dram, v_dram, wo_dram, borep_dram, s_dram,
            resid_dram=None, deps=None, gate=None, tag="a0"):
    block = kb.block
    pe, dv, ac = kb.pe, kb.dv, kb.ac

    wo_sb = kb.sb(f"{tag}_wo", (128, 8, 1024), BF16)
    borep = kb.sb(f"{tag}_borep", (128, 1024), F32)
    ones64 = kb.sb(f"{tag}_ones64", (1, 64), F32)
    q_sb = kb.sb(f"{tag}_q", (128, 8, SEQ), BF16)
    k_sb = kb.sb(f"{tag}_k", (128, 8, SEQ), BF16)
    v_sb = kb.sb(f"{tag}_v", (128, 16, 1040), BF16)
    nm4 = kb.sb(f"{tag}_nm4", (128, 8), F32)
    nmx = kb.sb(f"{tag}_nmx", (128, 2), F32)
    phat = kb.sb(f"{tag}_phat", (128, 2048), BF16)
    ptblk = kb.sb(f"{tag}_ptblk", (128, 16, 512), BF16)
    rs = kb.sb(f"{tag}_rs", (1, 512), F32)
    rep = kb.sb(f"{tag}_rep", (64, 512), F32)
    afm = kb.sb(f"{tag}_afm", (128, 8, 512), BF16)
    stile = [kb.sb(f"{tag}_st{i}", (128, 1024), F32) for i in range(2)]
    rtile = [kb.sb(f"{tag}_rt{i}", (128, 1024), F32) for i in range(2)]
    pS = kb.ps(f"{tag}_pS", (128, 2048), F32)
    pT = kb.ps(f"{tag}_pT", (128, 2048), BF16)
    pO = kb.ps(f"{tag}_pO", (65, 512), F32)
    pR = kb.ps(f"{tag}_pR", (64, 512), F32)

    wld = kb.dsem(f"{tag}_wld")
    qkld = kb.dsem(f"{tag}_qkld")
    rld = kb.dsem(f"{tag}_rld", 2)
    sst = kb.dsem(f"{tag}_sst", 2)

    def loadw(g):
        kb.gate_wait(g, gate)
        wld.inc(g.dma_start(out=wo_sb[:, :, :],
                            in_=wo_dram.rearrange("(k p) j -> p k j", p=128)))
    block.gpsimd(loadw)
    WLD = wld.n

    bsem = kb.dsem(f"{tag}_bld")

    def loadb(sy):
        kb.gate_wait(sy, gate)
        bsem.inc(sy.dma_start(out=borep[:, :], in_=borep_dram))
    block.sync(loadb)
    BSEM = bsem.n

    def ones_init(v):
        kb.gate_wait(v, gate)
        dv.inc(v.memset(ones64[:, :], 1.0))
    block.vector(ones_init)
    ONES = dv.n

    s_done = []
    scb_at = [0, 0]
    prev_sxp = [0]
    prev_tev = [0]
    prev_str = [0]
    for seq in range(2):
        def loadqkv(sy, seq=seq):
            if seq == 0:
                kb.gate_wait(sy, gate)
            if seq >= 1:
                sy.wait_ge(pe.h, pe.n)
            if deps is not None:
                for (p, n) in deps[seq]:
                    sy.wait_ge(p.h, n)
            qkld.inc(sy.dma_start(out=q_sb[:, :, :],
                                  in_=qfm_dram[seq, :, :, :].rearrange("c p n -> p c n")))
            qkld.inc(sy.dma_start(out=k_sb[:, :, :],
                                  in_=kfm_dram[seq, :, :, :].rearrange("c p n -> p c n")))
            qkld.inc(sy.dma_start(out=v_sb[:, :, :],
                                  in_=v_dram[seq, :, :].rearrange("(c p) j -> p c j", p=128)))
        block.sync(loadqkv)
        QKL = qkld.n

        for qb in range(4):
            for h in range(16):
                hp = 64 * (h % 2)
                hc = h // 2
                SXP_at = [0] * 4
                STR_at = [0] * 4
                TEV_at = [0] * 4

                def emit_smm(qt):
                    qcol = 512 * qb + 128 * qt

                    def s_mm(te, hp=hp, hc=hc, qcol=qcol, qt=qt,
                             first=(qb == 0 and h == 0 and qt == 0), w=QKL,
                             wexp=(SXP_at[qt - 1] if qt >= 1 else prev_sxp[0]),
                             wscb=(scb_at[0], scb_at[1])):
                        if first:
                            te.wait_ge(qkld.h, w)
                        if wexp:
                            te.wait_ge(ac.h, wexp)
                        if h == 0 and qt == 0:
                            te.wait_ge(dv.h, max(wscb))
                        for j in range(4):
                            mm = te.matmul(pS[:, 512 * j:512 * (j + 1)],
                                           q_sb[hp:hp + 64, hc, qcol:qcol + 128],
                                           k_sb[hp:hp + 64, hc, 512 * j:512 * (j + 1)],
                                           start=True, stop=True)
                        pe.inc(mm)
                    block.tensor(s_mm)
                    return pe.n

                def emit_max_exp(qt, wsmm):
                    par = qt % 2

                    def s_max(v, w=wsmm, par=par):
                        v.wait_ge(pe.h, w)
                        for j in range(4):
                            r = v.tensor_reduce(out=nm4[:, 4 * par + j:4 * par + j + 1],
                                                in_=pS[:, 512 * j:512 * (j + 1)],
                                                axis=AX.X, op=ALU.max)
                        n = dv.inc(r)
                        v.wait_ge(dv.h, n)
                        dv.inc(v.tensor_reduce(out=nmx[:, par:par + 1],
                                               in_=nm4[:, 4 * par:4 * par + 4], axis=AX.X,
                                               op=ALU.max, negate=True))
                    block.vector(s_max)
                    smx = dv.n

                    def s_exp(sc, w=smx, par=par,
                              wstr=(STR_at[qt - 1] if qt >= 1 else prev_str[0])):
                        sc.wait_ge(dv.h, w)
                        if wstr:
                            sc.wait_ge(pe.h, wstr)
                        for j in range(4):
                            a = sc.activation(out=phat[:, 512 * j:512 * (j + 1)],
                                              in_=pS[:, 512 * j:512 * (j + 1)],
                                              func=AF.Exp, bias=nmx[:, par:par + 1], scale=1.0)
                        ac.inc(a)
                    block.scalar(s_exp)
                    SXP_at[qt] = ac.n

                def emit_tr(qt):
                    def s_tr(te, qt=qt, w=SXP_at[qt],
                             wtev=(TEV_at[qt - 1] if qt >= 1 else prev_tev[0])):
                        te.wait_ge(ac.h, w)
                        if wtev:
                            te.wait_ge(dv.h, wtev)
                        for c in range(16):
                            tr = te.transpose(pT[:, 128 * c:128 * (c + 1)],
                                              phat[:, 128 * c:128 * (c + 1)], kb.ident[:, :])
                        pe.inc(tr)
                    block.tensor(s_tr)
                    if not STR_at[qt]:
                        STR_at[qt] = pe.n
                    assert STR_at[qt] == pe.n

                    def s_tev(v, w=STR_at[qt], qt=qt):
                        v.wait_ge(pe.h, w)
                        v.tensor_copy(ptblk[:, 0:8, 128 * qt:128 * (qt + 1)],
                                      pT.ap().rearrange("p (c n) -> p c n", n=128)[:, 0:8, :])
                        dv.inc(v.tensor_copy(ptblk[:, 8:16, 128 * qt:128 * (qt + 1)],
                                             pT.ap().rearrange("p (c n) -> p c n", n=128)[:, 8:16, :]))
                    block.vector(s_tev)
                    TEV_at[qt] = dv.n

                w0 = emit_smm(0)
                emit_max_exp(0, w0)
                for qt in range(1, 4):
                    w = emit_smm(qt)
                    STR_at[qt - 1] = pe.n + 1
                    emit_max_exp(qt, w)
                    emit_tr(qt - 1)
                    assert pe.n == STR_at[qt - 1]
                emit_tr(3)
                prev_sxp[0] = SXP_at[3]
                prev_tev[0] = TEV_at[3]
                prev_str[0] = STR_at[3]
                TEV = TEV_at[3]

                def pv_mm(te, hp=hp, hc=hc, w=TEV, first=(qb == 0 and h == 0), wo=ONES):
                    te.wait_ge(dv.h, w)
                    if first:
                        te.wait_ge(dv.h, wo)
                    for kc in range(16):
                        mm = te.matmul(pO[:, :], v_sb[:, kc, 65 * h:65 * h + 65],
                                       ptblk[:, kc, :], start=(kc == 0), stop=(kc == 15))
                    pe.inc(mm)
                block.tensor(pv_mm)
                PVM = pe.n

                def pv_recip(v, w=PVM):
                    v.wait_ge(pe.h, w)
                    dv.inc(v.reciprocal(rs[:, :], pO[64:65, :]))
                block.vector(pv_recip)
                RCP = dv.n

                def pv_rep(te, w=RCP):
                    te.wait_ge(dv.h, w)
                    pe.inc(te.matmul(pR[:, :], ones64[:, :], rs[:, :], start=True, stop=True))
                block.tensor(pv_rep)
                REP = pe.n

                def pv_comb(v, w=REP, hp=hp, hc=hc):
                    v.wait_ge(pe.h, w)
                    n = dv.inc(v.tensor_copy(rep[:, :], pR[:, :]))
                    v.wait_ge(dv.h, n)
                    dv.inc(v.tensor_tensor(out=afm[hp:hp + 64, hc, :], in0=pO[0:64, :],
                                           in1=rep[:, :], op=ALU.mult))
                block.vector(pv_comb)
            HEADS_DONE = dv.n

            for qt in range(4):
                t = 16 * seq + 4 * qb + qt
                bb = qt % 2

                def wo_mm(te, qt=qt, w=HEADS_DONE, wld_n=WLD,
                          first=(seq == 0 and qb == 0 and qt == 0),
                          wscb=(scb_at[0], scb_at[1])):
                    te.wait_ge(dv.h, max(w, max(wscb)))
                    if first:
                        te.wait_ge(wld.h, wld_n)
                    for nb in range(2):
                        for kc in range(8):
                            mm = te.matmul(pS[:, 512 * nb:512 * (nb + 1)],
                                           afm[:, kc, 128 * qt:128 * (qt + 1)],
                                           wo_sb[:, kc, 512 * nb:512 * (nb + 1)],
                                           start=(kc == 0), stop=(kc == 7))
                    pe.inc(mm)
                block.tensor(wo_mm)
                WOM = pe.n

                if resid_dram is not None:
                    def loadr(sy, t=t, bb=bb, first=(seq == 0 and qb == 0 and qt < 2)):
                        if first:
                            kb.gate_wait(sy, gate)
                        sy.wait_ge(dv.h, scb_at[bb])
                        rld[bb].inc(sy.dma_start(out=rtile[bb][:, :],
                                                 in_=resid_dram[128 * t:128 * (t + 1), :]))
                    block.sync(loadr)
                RLD = rld[bb].n

                def s_comb(v, bb=bb, w=WOM, wr=RLD, useres=(resid_dram is not None),
                           first=(seq == 0 and qb == 0 and qt < 2)):
                    v.wait_ge(pe.h, w)
                    if first:
                        v.wait_ge(bsem.h, BSEM)
                    v.wait_ge(sst[bb].h, sst[bb].n)
                    v.tensor_tensor(out=stile[bb][:, 0:512], in0=pS[:, 0:512],
                                    in1=borep[:, 0:512], op=ALU.add)
                    n = dv.inc(v.tensor_tensor(out=stile[bb][:, 512:1024], in0=pS[:, 512:1024],
                                               in1=borep[:, 512:1024], op=ALU.add))
                    if useres:
                        v.wait_ge(rld[bb].h, wr)
                        v.wait_ge(dv.h, n)
                        n = dv.inc(v.tensor_tensor(out=stile[bb][:, :], in0=stile[bb][:, :],
                                                   in1=rtile[bb][:, :], op=ALU.add))
                block.vector(s_comb)
                SCB = dv.n
                scb_at[bb] = SCB

                def s_st(sy, t=t, bb=bb, w=SCB):
                    sy.wait_ge(dv.h, w)
                    sst[bb].inc(sy.dma_start(out=s_dram[128 * t:128 * (t + 1), :],
                                             in_=stile[bb][:, :]))
                block.sync(s_st)
                s_done.append((sst[bb], sst[bb].n))

    return {"wld": [wld], "qkld": [qkld], "rld": rld, "sst": sst, "s_done": s_done}


FFN_SC = float(np.float32(1.0) / np.float32(127.0))
FFN_SC2 = float((np.float32(1.0) / np.float32(127.0)) * (np.float32(1.0) / np.float32(127.0)))


def ph_ffn0(kb, xnfm_dram, wg_dram, wu_dram, wd_dram, s_dram, x0_dram, x1_dram,
            fm_done=None, s_done=None, gate=None, tag="f0"):
    block = kb.block
    pe, dv, ac = kb.pe, kb.dv, kb.ac
    SL = 256
    NSL = TOK // SL

    wg = kb.sb(f"{tag}_wg", (128, 8, 3072), BF16)
    wu = kb.sb(f"{tag}_wu", (128, 8, 3072), BF16)
    wd = kb.sb(f"{tag}_wd", (128, 24, 1024), BF16)
    xn_sl = [kb.sb(f"{tag}_xn{i}", (128, 8, SL), BF16) for i in range(2)]
    sg_sb = kb.sb(f"{tag}_sg", (128, SL), BF16)
    h_fm = kb.sb(f"{tag}_h", (128, 24, SL), BF16)
    gt = kb.sb(f"{tag}_gt", (128, 1024), F32)
    st_ = [kb.sb(f"{tag}_s{i}", (128, 1024), F32) for i in range(2)]
    xt_ = [kb.sb(f"{tag}_x{i}", (128, 1024), F32) for i in range(2)]
    x1t = [kb.sb(f"{tag}_x1{i}", (128, 1024), F32) for i in range(2)]
    pG = kb.ps(f"{tag}_pG", (128, 512), F32)
    pU = kb.ps(f"{tag}_pU", (128, 512), F32)
    pD = kb.ps(f"{tag}_pD", (128, 1024), F32)

    wld = kb.dsem(f"{tag}_wld")
    xld = kb.dsem(f"{tag}_xld", 2)
    sld = kb.dsem(f"{tag}_sld", 2)
    x0ld = kb.dsem(f"{tag}_x0ld", 2)
    x1st = kb.dsem(f"{tag}_x1st", 2)

    def loadw(g):
        kb.gate_wait(g, gate)
        for half in range(2):
            wld.inc(g.dma_start(out=wg[:, :, 1536 * half:1536 * (half + 1)],
                                in_=wg_dram.rearrange("(k p) j -> p k j", p=128)[:, :, 1536 * half:1536 * (half + 1)]))
            wld.inc(g.dma_start(out=wu[:, :, 1536 * half:1536 * (half + 1)],
                                in_=wu_dram.rearrange("(k p) j -> p k j", p=128)[:, :, 1536 * half:1536 * (half + 1)]))
        wld.inc(g.dma_start(out=wd[:, :, :],
                            in_=wd_dram.rearrange("(k p) j -> p k j", p=128)))
    block.gpsimd(loadw)
    WLD = wld.n

    x1_done = []
    slice_pe_end = [0] * NSL
    scb = [0, 0]
    for s in range(NSL):
        b = s % 2

        def loadx(sy, s=s, b=b):
            if s == 0:
                kb.gate_wait(sy, gate)
            if s >= 2:
                sy.wait_ge(pe.h, slice_pe_end[s - 2])
            if fm_done is not None:
                sy.wait_ge(fm_done[s // 2][0].h, fm_done[s // 2][1])
            xld[b].inc(sy.dma_start(
                out=xn_sl[b][:, :, :],
                in_=xnfm_dram[:, :, SL * s:SL * (s + 1)].rearrange("c p n -> p c n")))
        block.sync(loadx)
        XL = xld[b].n

        for m in range(24):
            def gu_mm(te, m=m, b=b, s=s, w=XL, first=(s == 0 and m == 0)):
                te.wait_ge(xld[b].h, w)
                if first:
                    te.wait_ge(wld.h, WLD)
                te.wait_ge(ac.h, ac.n)
                te.wait_ge(dv.h, dv.n)
                for kc in range(8):
                    mm = te.matmul(pG[:, 0:SL], wg[:, kc, 128 * m:128 * (m + 1)],
                                   xn_sl[b][:, kc, :], start=(kc == 0), stop=(kc == 7))
                for kc in range(8):
                    mm = te.matmul(pU[:, 0:SL], wu[:, kc, 128 * m:128 * (m + 1)],
                                   xn_sl[b][:, kc, :], start=(kc == 0), stop=(kc == 7))
                pe.inc(mm)
            block.tensor(gu_mm)
            GU = pe.n

            def silu_g(sc, w=GU):
                sc.wait_ge(pe.h, w)
                ac.inc(sc.activation(out=sg_sb[:, :], in_=pG[:, 0:SL], func=AF.Silu,
                                     scale=FFN_SC))
            block.scalar(silu_g)
            SG = ac.n

            def h_mul(v, m=m, w=SG):
                v.wait_ge(ac.h, w)
                dv.inc(v.tensor_tensor(out=h_fm[:, m, :], in0=pU[:, 0:SL],
                                       in1=sg_sb[:, :], op=ALU.mult))
            block.vector(h_mul)
        H_DONE = dv.n

        for t2 in range(SL // 128):
            t = (SL // 128) * s + t2
            bb = t % 2

            def d_mm(te, t2=t2, w=H_DONE):
                te.wait_ge(dv.h, w)
                te.wait_ge(ac.h, ac.n)
                for nb in range(2):
                    for kc in range(24):
                        mm = te.matmul(pD[:, 512 * nb:512 * (nb + 1)],
                                       h_fm[:, kc, 128 * t2:128 * (t2 + 1)],
                                       wd[:, kc, 512 * nb:512 * (nb + 1)],
                                       start=(kc == 0), stop=(kc == 23))
                pe.inc(mm)
            block.tensor(d_mm)
            DM_ = pe.n

            def loadsx(sy, t=t, bb=bb):
                sy.wait_ge(dv.h, scb[bb])
                if s_done is not None and t < 2:
                    for (p, n) in s_done:
                        sy.wait_ge(p.h, n)
                sld[bb].inc(sy.dma_start(out=st_[bb][:, :],
                                         in_=s_dram[128 * t:128 * (t + 1), :]))
                x0ld[bb].inc(sy.dma_start(out=xt_[bb][:, :],
                                          in_=x0_dram[128 * t:128 * (t + 1), :]))
            block.sync(loadsx)
            SXL = (sld[bb].n, x0ld[bb].n)

            def g_silu(sc, w=DM_):
                sc.wait_ge(pe.h, w)
                sc.activation(out=gt[:, 0:512], in_=pD[:, 0:512], func=AF.Silu,
                              scale=FFN_SC2)
                ac.inc(sc.activation(out=gt[:, 512:1024], in_=pD[:, 512:1024],
                                     func=AF.Silu, scale=FFN_SC2))
            block.scalar(g_silu)
            GS = ac.n

            def comb(v, bb=bb, w=GS, wsx=SXL):
                v.wait_ge(ac.h, w)
                v.wait_ge(sld[bb].h, wsx[0])
                v.wait_ge(x0ld[bb].h, wsx[1])
                v.wait_ge(x1st[bb].h, x1st[bb].n)
                n = dv.inc(v.tensor_tensor(out=x1t[bb][:, :], in0=gt[:, :],
                                           in1=st_[bb][:, :], op=ALU.mult))
                v.wait_ge(dv.h, n)
                dv.inc(v.tensor_tensor(out=x1t[bb][:, :], in0=x1t[bb][:, :],
                                       in1=xt_[bb][:, :], op=ALU.add))
            block.vector(comb)
            CB = dv.n
            scb[bb] = CB

            def x1_store(sy, t=t, bb=bb, w=CB):
                sy.wait_ge(dv.h, w)
                x1st[bb].inc(sy.dma_start(out=x1_dram[128 * t:128 * (t + 1), :],
                                          in_=x1t[bb][:, :]))
            block.scalar(x1_store)
            x1_done.append((x1st[bb], x1st[bb].n))
        slice_pe_end[s] = pe.n

    return {"wld": [wld], "xld": xld, "sld": sld, "x0ld": x0ld, "x1st": x1st,
            "x1_done": x1_done}


def ph_gn(kb, xnfm_dram, gind_dram, e2_dram, gcols_dram, bcols_dram,
          A_gn, B_gn, fm_done=None, gate=None, tag="gn"):
    block = kb.block
    pe, dv, ac = kb.pe, kb.dv, kb.ac

    xc = [kb.sb(f"{tag}_xc{i}", (128, 2048), BF16) for i in range(2)]
    sums = kb.sb(f"{tag}_sums", (128, 16), F32)
    sqs = kb.sb(f"{tag}_sqs", (128, 16), F32)
    dump = kb.sb(f"{tag}_dump", (128, 2048), F32)
    gind = kb.sb(f"{tag}_gind", (128, 4), F32)
    e2 = kb.sb(f"{tag}_e2", (4, 128), F32)
    gcols = kb.sb(f"{tag}_gcols", (128, 8), F32)
    bcols = kb.sb(f"{tag}_bcols", (128, 8), F32)
    grp = kb.sb(f"{tag}_grp", (4, 16), F32)
    grpq = kb.sb(f"{tag}_grpq", (4, 16), F32)
    mean_g = kb.sb(f"{tag}_meang", (4, 16), F32)
    var_g = kb.sb(f"{tag}_varg", (4, 16), F32)
    rstd_g = kb.sb(f"{tag}_rstdg", (4, 16), F32)
    mexp = kb.sb(f"{tag}_mexp", (128, 16), F32)
    rexp = kb.sb(f"{tag}_rexp", (128, 16), F32)
    pG = kb.ps(f"{tag}_pG", (4, 512), F32)
    pE = kb.ps(f"{tag}_pE", (128, 512), F32)

    cld = kb.dsem(f"{tag}_cld")
    xldg = kb.dsem(f"{tag}_xldg", 2)

    def loadc(sy):
        kb.gate_wait(sy, gate)
        cld.inc(sy.dma_start(out=gind[:, :], in_=gind_dram))
        cld.inc(sy.dma_start(out=e2[:, :], in_=e2_dram))
        cld.inc(sy.dma_start(out=gcols[:, :], in_=gcols_dram))
        cld.inc(sy.dma_start(out=bcols[:, :], in_=bcols_dram))
    block.sync(loadc)
    CLD = cld.n

    red_at = [0] * 16
    for idx in range(16):
        seq, c = idx // 8, idx % 8
        b = idx % 2

        def loadx(sy, seq=seq, c=c, b=b, idx=idx):
            if idx < 2:
                kb.gate_wait(sy, gate)
            if idx >= 2:
                sy.wait_ge(dv.h, red_at[idx - 2])
            if fm_done is not None:
                sy.wait_ge(fm_done[-1][0].h, fm_done[-1][1])
                sy.wait_ge(fm_done[-2][0].h, fm_done[-2][1])
            xldg[b].inc(sy.dma_start(out=xc[b][:, :],
                                     in_=xnfm_dram[c, :, SEQ * seq:SEQ * (seq + 1)]))
        block.sync(loadx)
        XL = xldg[b].n

        def red(v, idx=idx, b=b, w=XL):
            v.wait_ge(xldg[b].h, w)
            if idx >= 1:
                v.wait_ge(dv.h, red_at[idx - 1])
            v.tensor_reduce(out=sums[:, idx:idx + 1], in_=xc[b][:, :], axis=AX.X, op=ALU.add)
            dv.inc(v.scalar_tensor_tensor(out=dump[:, :], in0=xc[b][:, :], scalar=1.0,
                                          in1=xc[b][:, :], op0=ALU.mult, op1=ALU.mult,
                                          accum_out=sqs[:, idx:idx + 1]))
            red_at[idx] = dv.n
        block.vector(red)
    RED = dv.n

    def grp_mm(te, w=RED, wc=CLD):
        te.wait_ge(dv.h, w)
        te.wait_ge(cld.h, wc)
        te.matmul(pG[:, 0:16], gind[:, :], sums[:, :], start=True, stop=True)
        mm = te.matmul(pG[:, 16:32], gind[:, :], sqs[:, :], start=True, stop=True)
        pe.inc(mm)
    block.tensor(grp_mm)
    GMM = pe.n

    def grp_stats(v, w=GMM):
        v.wait_ge(pe.h, w)
        inv = 1.0 / (SEQ * 32.0)
        v.tensor_scalar(out=mean_g[:, :], in0=pG[:, 0:16], scalar1=inv, scalar2=None, op0=ALU.mult)
        n = dv.inc(v.tensor_scalar(out=var_g[:, :], in0=pG[:, 16:32], scalar1=inv, scalar2=None, op0=ALU.mult))
        v.wait_ge(dv.h, n)
        n = dv.inc(v.tensor_tensor(out=grpq[:, :], in0=mean_g[:, :], in1=mean_g[:, :], op=ALU.mult))
        v.wait_ge(dv.h, n)
        dv.inc(v.tensor_tensor(out=var_g[:, :], in0=var_g[:, :], in1=grpq[:, :], op=ALU.subtract))
    block.vector(grp_stats)
    VARN = dv.n

    def grp_rstd(sc, w=VARN):
        sc.wait_ge(dv.h, w)
        a = ac.inc(sc.activation(out=rstd_g[:, :], in_=var_g[:, :], func=AF.Ln,
                                 bias=kb.epsgn[0:4, :], scale=1.0))
        sc.wait_ge(ac.h, a)
        ac.inc(sc.activation(out=rstd_g[:, :], in_=rstd_g[:, :], func=AF.Exp, scale=-0.5))
    block.scalar(grp_rstd)
    RSTD = ac.n

    def exp_mm(te, w=RSTD):
        te.wait_ge(ac.h, w)
        te.matmul(pE[:, 0:16], e2[:, :], rstd_g[:, :], start=True, stop=True)
        mm = te.matmul(pE[:, 16:32], e2[:, :], mean_g[:, :], start=True, stop=True)
        pe.inc(mm)
    block.tensor(exp_mm)
    EMM = pe.n

    def ab_final(v, w=EMM):
        v.wait_ge(pe.h, w)
        v.tensor_copy(rexp[:, :], pE[:, 0:16])
        n = dv.inc(v.tensor_copy(mexp[:, :], pE[:, 16:32]))
        v.wait_ge(dv.h, n)
        for seq in range(2):
            n = dv.inc(v.tensor_tensor(out=A_gn[:, 8 * seq:8 * seq + 8],
                                       in0=rexp[:, 8 * seq:8 * seq + 8],
                                       in1=gcols[:, :], op=ALU.mult))
        v.wait_ge(dv.h, n)
        for seq in range(2):
            n = dv.inc(v.tensor_tensor(out=B_gn[:, 8 * seq:8 * seq + 8],
                                       in0=mexp[:, 8 * seq:8 * seq + 8],
                                       in1=A_gn[:, 8 * seq:8 * seq + 8], op=ALU.mult))
        v.wait_ge(dv.h, n)
        for seq in range(2):
            n = dv.inc(v.tensor_tensor(out=B_gn[:, 8 * seq:8 * seq + 8],
                                       in0=bcols[:, :],
                                       in1=B_gn[:, 8 * seq:8 * seq + 8], op=ALU.subtract))
    block.vector(ab_final)
    AB = dv.n

    return {"cld": [cld], "xldg": xldg, "ab_done": AB}


def ph_proj1(kb, xnfm_dram, w_dram, bq_dram, bk_dram, bvrep_dram, A_gn, B_gn,
             qfm_dram, kfm_dram, v_dram, fm_done=None, ab_done=0, gate=None, tag="p1"):
    block = kb.block
    pe, dv, ac = kb.pe, kb.dv, kb.ac

    w1 = kb.sb(f"{tag}_w1", (128, 8, 3072), BF16)
    bq = kb.sb(f"{tag}_bq", (128, 8), F32)
    bk = kb.sb(f"{tag}_bk", (128, 8), F32)
    bvrep = kb.sb(f"{tag}_bvrep", (128, 1024), F32)
    xn_sl = [kb.sb(f"{tag}_xn{i}", (128, 8, 512), BF16) for i in range(2)]
    gn_sl = kb.sb(f"{tag}_gn", (128, 8, 512), BF16)
    qfm_buf = [kb.sb(f"{tag}_qfm{i}", (128, 8, 512), BF16) for i in range(2)]
    kfm_buf = [kb.sb(f"{tag}_kfm{i}", (128, 8, 512), BF16) for i in range(2)]
    vt_buf = [kb.sb(f"{tag}_vt{i}", (128, 1040), BF16) for i in range(2)]
    vtmp = kb.sb(f"{tag}_vtmp", (128, 1024), F32)
    pQ = kb.ps(f"{tag}_pQ", (128, 1024), F32)
    pV = kb.ps(f"{tag}_pV", (128, 1024), F32)

    wld = kb.dsem(f"{tag}_wld")
    bld = kb.dsem(f"{tag}_bld")
    xld = kb.dsem(f"{tag}_xld", 2)
    qst = kb.dsem(f"{tag}_qst", 2)
    kst = kb.dsem(f"{tag}_kst", 2)
    vst = kb.dsem(f"{tag}_vst", 2)

    def loadw(g):
        kb.gate_wait(g, gate)
        for half in range(2):
            wld.inc(g.dma_start(out=w1[:, :, 1536 * half:1536 * (half + 1)],
                                in_=w_dram.rearrange("(k p) j -> p k j", p=128)[:, :, 1536 * half:1536 * (half + 1)]))
    block.gpsimd(loadw)
    WLD = wld.n

    def loadb(sy):
        kb.gate_wait(sy, gate)
        bld.inc(sy.dma_start(out=bq[:, :], in_=bq_dram))
        bld.inc(sy.dma_start(out=bk[:, :], in_=bk_dram))
        bld.inc(sy.dma_start(out=bvrep[:, :], in_=bvrep_dram))
    block.sync(loadb)
    BLD = bld.n

    def bscale(v, w=BLD):
        v.wait_ge(bld.h, w)
        dv.inc(v.tensor_scalar(out=bq[:, :], in0=bq[:, :], scalar1=0.125, scalar2=None,
                               op0=ALU.mult))
    block.vector(bscale)
    BSC = dv.n

    def ones_init(v):
        kb.gate_wait(v, gate)
        for i in range(2):
            dv.inc(v.memset(vt_buf[i].ap().rearrange("p (c n) -> p c n", n=65)[:, :, 64:65], 1.0))
    block.vector(ones_init)

    qkv_done = [None, None]
    slice_pe_end = [0] * 8
    for s in range(8):
        b = s % 2
        seq = s // 4
        scol = 512 * (s % 4)

        def loadx(sy, s=s, b=b):
            if s == 0:
                kb.gate_wait(sy, gate)
            if s >= 2:
                sy.wait_ge(pe.h, slice_pe_end[s - 2])
            if fm_done is not None:
                sy.wait_ge(fm_done[s][0].h, fm_done[s][1])
            xld[b].inc(sy.dma_start(
                out=xn_sl[b][:, :, :],
                in_=xnfm_dram[:, :, 512 * s:512 * (s + 1)].rearrange("c p n -> p c n")))
        block.sync(loadx)
        XL = xld[b].n

        def gn_apply(sc, s=s, b=b, seq=seq, w=XL, wab=ab_done):
            sc.wait_ge(xld[b].h, w)
            if s == 0:
                sc.wait_ge(dv.h, wab)
            if s >= 1:
                sc.wait_ge(pe.h, pe.n)
            for c in range(8):
                a = sc.activation(out=gn_sl[:, c, :], in_=xn_sl[b][:, c, :],
                                  func=AF.Identity, scale=A_gn[:, 8 * seq + c:8 * seq + c + 1],
                                  bias=B_gn[:, 8 * seq + c:8 * seq + c + 1])
            ac.inc(a)
        block.scalar(gn_apply)
        GN = ac.n

        for which, (nm_, buf, coff, sca, stq) in enumerate(
                [("q", qfm_buf, 0, 0.125, qst), ("k", kfm_buf, 1024, 1.0, kst)]):
            for m in range(8):
                def qk_mm(te, b=b, m=m, coff=coff, w=GN, first=(s == 0 and which == 0 and m == 0), which=which):
                    te.wait_ge(ac.h, w)
                    if first:
                        te.wait_ge(wld.h, WLD)
                    te.wait_ge(ac.h, ac.n)
                    for kc in range(8):
                        mm = te.matmul(pQ[:, 512 * (m % 2):512 * (m % 2 + 1)],
                                       w1[:, kc, coff + 128 * m:coff + 128 * (m + 1)],
                                       gn_sl[:, kc, :], start=(kc == 0), stop=(kc == 7))
                    pe.inc(mm)
                block.tensor(qk_mm)
                QKM = pe.n

                def qk_ev(sc, m=m, which=which, buf=buf, sca=sca, sl2=s % 2, s=s, w=QKM,
                          wb=BSC, stq=stq):
                    sc.wait_ge(pe.h, w)
                    if s == 0 and m == 0 and which == 0:
                        sc.wait_ge(dv.h, wb)
                    if s >= 2 and m == 0:
                        sc.wait_ge(stq[sl2].h, stq[sl2].n)
                    bb2 = bq if which == 0 else bk
                    ac.inc(sc.activation(out=buf[sl2][:, m, :],
                                         in_=pQ[:, 512 * (m % 2):512 * (m % 2 + 1)],
                                         func=AF.Identity, scale=sca,
                                         bias=bb2[:, m:m + 1]))
                block.scalar(qk_ev)
        QKE = ac.n

        for t4 in range(4):
            t = 4 * s + t4
            bb = t % 2

            def v_mm(te, t4=t4, w=QKE):
                te.wait_ge(ac.h, w)
                te.wait_ge(dv.h, dv.n)
                for nb in range(2):
                    for kc in range(8):
                        mm = te.matmul(pV[:, 512 * nb:512 * (nb + 1)],
                                       gn_sl[:, kc, 128 * t4:128 * (t4 + 1)],
                                       w1[:, kc, 2048 + 512 * nb:2048 + 512 * (nb + 1)],
                                       start=(kc == 0), stop=(kc == 7))
                pe.inc(mm)
            block.tensor(v_mm)
            VMM = pe.n

            def v_ev(v, t=t, bb=bb, w=VMM):
                v.wait_ge(pe.h, w)
                if t >= 2:
                    v.wait_ge(vst[bb].h, vst[bb].n)
                v.tensor_tensor(out=vtmp[:, 0:512], in0=pV[:, 0:512],
                                in1=bvrep[:, 0:512], op=ALU.add)
                n = dv.inc(v.tensor_tensor(out=vtmp[:, 512:1024], in0=pV[:, 512:1024],
                                           in1=bvrep[:, 512:1024], op=ALU.add))
                v.wait_ge(dv.h, n)
                dv.inc(v.tensor_copy(
                    vt_buf[bb].ap().rearrange("p (c n) -> p c n", n=65)[:, :, 0:64],
                    vtmp.ap().rearrange("p (c n) -> p c n", n=64)))
            block.vector(v_ev)
            VEV = dv.n

            def v_st(sy, t=t, bb=bb, w=VEV):
                sy.wait_ge(dv.h, w)
                vst[bb].inc(sy.dma_start(
                    out=v_dram[t // 16, 128 * (t % 16):128 * (t % 16) + 128, :],
                    in_=vt_buf[bb][:, :]))
            block.sync(v_st)

        slice_pe_end[s] = pe.n

        def qk_store(sy, b2=s % 2, seq=seq, scol=scol, w=QKE):
            sy.wait_ge(ac.h, w)
            qst[b2].inc(sy.dma_start(
                out=qfm_dram[seq, :, :, scol:scol + 512].rearrange("c p n -> p c n"),
                in_=qfm_buf[b2][:, :, :]))
            kst[b2].inc(sy.dma_start(
                out=kfm_dram[seq, :, :, scol:scol + 512].rearrange("c p n -> p c n"),
                in_=kfm_buf[b2][:, :, :]))
        block.sync(qk_store)
        if s % 4 == 3:
            qkv_done[seq] = [(qst[0], qst[0].n), (qst[1], qst[1].n),
                             (kst[0], kst[0].n), (kst[1], kst[1].n),
                             (vst[0], vst[0].n), (vst[1], vst[1].n)]

    return {"wld": [wld], "bld": [bld], "xld": xld, "qst": qst, "kst": kst,
            "vst": vst, "qkv_done": qkv_done}


def ph_proj2(kb, xnfm_dram, w_dram, qgrep_dram, qfm_dram, kfm_dram, v_dram,
             simall_dram, minmax_dram, fm_done=None, gate=None, tag="p2"):
    block = kb.block
    pe, dv, ac = kb.pe, kb.dv, kb.ac

    w2 = kb.sb(f"{tag}_w2", (128, 8, 3072), BF16)
    qgrep = kb.sb(f"{tag}_qgrep", (128, 1024), F32)
    inv_rep = kb.sb(f"{tag}_invrep", (128, 16), F32)
    xn_sl = [kb.sb(f"{tag}_xn{i}", (128, 8, 512), BF16) for i in range(2)]
    qf32 = kb.sb(f"{tag}_qf32", (128, 1024), F32)
    qbf = kb.sb(f"{tag}_qbf", (128, 1024), BF16)
    dump = kb.sb(f"{tag}_dump", (128, 1024), F32)
    dh16 = kb.sb(f"{tag}_dh16", (128, 16), F32)
    sq16 = kb.sb(f"{tag}_sq16", (128, 16), F32)
    rstdq = kb.sb(f"{tag}_rstdq", (128, 16), F32)
    sim_all = kb.sb(f"{tag}_simall", (128, 32), F32)
    qfm_buf = [kb.sb(f"{tag}_qfm{i}", (128, 8, 512), BF16) for i in range(2)]
    kfm_buf = [kb.sb(f"{tag}_kfm{i}", (128, 8, 512), BF16) for i in range(2)]
    vt_buf = [kb.sb(f"{tag}_vt{i}", (128, 1040), BF16) for i in range(2)]
    sbuf2 = kb.sb(f"{tag}_sbuf2", (128, 32), F32)
    sbuf3 = kb.sb(f"{tag}_sbuf3", (128, 32), F32)
    mm2 = kb.sb(f"{tag}_mm2", (128, 2), F32)
    mmt = kb.sb(f"{tag}_mmt", (2, 128), F32)
    mmv = kb.sb(f"{tag}_mmv", (2, 1), F32)
    mml = kb.sb(f"{tag}_mml", (1, 2), F32)
    tiny = kb.sb(f"{tag}_tiny", (1, 16), F32)
    pQ2 = kb.ps(f"{tag}_pQ2", (128, 1024), F32)
    pK2 = kb.ps(f"{tag}_pK2", (128, 1024), F32)
    pTR = kb.ps(f"{tag}_pTR", (128, 1024), BF16)
    pMM = kb.ps(f"{tag}_pMM", (128, 128), F32)

    wld = kb.dsem(f"{tag}_wld")
    cld = kb.dsem(f"{tag}_cld")
    xld = kb.dsem(f"{tag}_xld", 2)
    qst = kb.dsem(f"{tag}_qst", 2)
    kst = kb.dsem(f"{tag}_kst", 2)
    vst = kb.dsem(f"{tag}_vst", 2)
    mst = kb.dsem(f"{tag}_mst")

    def loadw(g):
        kb.gate_wait(g, gate)
        for half in range(2):
            wld.inc(g.dma_start(out=w2[:, :, 1536 * half:1536 * (half + 1)],
                                in_=w_dram.rearrange("(k p) j -> p k j", p=128)[:, :, 1536 * half:1536 * (half + 1)]))
    block.gpsimd(loadw)
    WLD = wld.n

    def loadc(sy):
        kb.gate_wait(sy, gate)
        cld.inc(sy.dma_start(out=qgrep[:, :], in_=qgrep_dram))
    block.sync(loadc)
    CLD = cld.n

    def ones_init2(v):
        kb.gate_wait(v, gate)
        for i in range(2):
            dv.inc(v.memset(vt_buf[i].ap().rearrange("p (c n) -> p c n", n=65)[:, :, 64:65], 1.0))
    block.vector(ones_init2)

    def qg_prep_v(v, w=CLD):
        v.wait_ge(cld.h, w)
        dv.inc(v.tensor_tensor(out=dump[0:1, :], in0=qgrep[0:1, :], in1=qgrep[0:1, :],
                               op=ALU.mult))
    block.vector(qg_prep_v)
    QG1 = dv.n

    def qg_red(v, w=QG1):
        v.wait_ge(dv.h, w)
        dv.inc(v.tensor_reduce(out=tiny[:, :],
                               in_=dump.ap().rearrange("p (h d) -> p h d", d=64)[0:1, :, :],
                               axis=AX.X, op=ALU.add))
    block.vector(qg_red)
    QG2 = dv.n

    def qg_rstd(sc, w=QG2):
        sc.wait_ge(dv.h, w)
        a = ac.inc(sc.activation(out=tiny[:, :], in_=tiny[:, :], func=AF.Ln,
                                 bias=kb.epsz[0:1, :], scale=1.0))
        sc.wait_ge(ac.h, a)
        ac.inc(sc.activation(out=tiny[:, :], in_=tiny[:, :], func=AF.Exp, scale=-0.5))
    block.scalar(qg_rstd)
    QG3 = ac.n

    def qg_scale(v, w=QG3):
        v.wait_ge(ac.h, w)
        dv.inc(v.tensor_scalar(out=tiny[:, :], in0=tiny[:, :], scalar1=1.0 / 16.0,
                               scalar2=None, op0=ALU.mult))
    block.vector(qg_scale)
    QG4 = dv.n

    def qg_repmm(te, w=QG4):
        te.wait_ge(dv.h, w)
        pe.inc(te.matmul(pMM[:, 0:16], kb.ones128[:, :], tiny[:, :], start=True, stop=True))
    block.tensor(qg_repmm)
    QG5 = pe.n

    def qg_repev(v, w=QG5):
        v.wait_ge(pe.h, w)
        dv.inc(v.tensor_copy(inv_rep[:, :], pMM[:, 0:16]))
    block.vector(qg_repev)
    INVR = dv.n

    qkv_done = [None, None]
    sim_done_at = 0
    slice_pe_end = [0] * 8
    for s in range(8):
        b = s % 2
        seq = s // 4
        scol = 512 * (s % 4)

        def loadx(sy, s=s, b=b):
            if s == 0:
                kb.gate_wait(sy, gate)
            if s >= 2:
                sy.wait_ge(pe.h, slice_pe_end[s - 2])
            if fm_done is not None:
                sy.wait_ge(fm_done[s][0].h, fm_done[s][1])
            xld[b].inc(sy.dma_start(
                out=xn_sl[b][:, :, :],
                in_=xnfm_dram[:, :, 512 * s:512 * (s + 1)].rearrange("c p n -> p c n")))
        block.sync(loadx)
        XL = xld[b].n

        for m in range(8):
            def k_mm(te, b=b, m=m, w=XL, first=(s == 0 and m == 0)):
                te.wait_ge(xld[b].h, w)
                if first:
                    te.wait_ge(wld.h, WLD)
                te.wait_ge(ac.h, ac.n)
                for kc in range(8):
                    mm = te.matmul(pK2[:, 512 * (m % 2):512 * (m % 2 + 1)],
                                   w2[:, kc, 1024 + 128 * m:1024 + 128 * (m + 1)],
                                   xn_sl[b][:, kc, :], start=(kc == 0), stop=(kc == 7))
                pe.inc(mm)
            block.tensor(k_mm)
            KM = pe.n

            def k_ev(sc, m=m, sl2=s % 2, s=s, w=KM):
                sc.wait_ge(pe.h, w)
                if s >= 2 and m == 0:
                    sc.wait_ge(kst[sl2].h, kst[sl2].n)
                ac.inc(sc.copy(kfm_buf[sl2][:, m, :],
                               pK2[:, 512 * (m % 2):512 * (m % 2 + 1)]))
            block.scalar(k_ev)
        KQE = ac.n

        for t4 in range(4):
            t = 4 * s + t4
            bb = t % 2

            def q_mm(te, b=b, t4=t4, w=KQE):
                te.wait_ge(ac.h, w)
                te.wait_ge(ac.h, ac.n)
                te.wait_ge(dv.h, dv.n)
                for nb in range(2):
                    for kc in range(8):
                        mm = te.matmul(pQ2[:, 512 * nb:512 * (nb + 1)],
                                       xn_sl[b][:, kc, 128 * t4:128 * (t4 + 1)],
                                       w2[:, kc, 512 * nb:512 * (nb + 1)],
                                       start=(kc == 0), stop=(kc == 7))
                pe.inc(mm)
            block.tensor(q_mm)
            QM = pe.n

            def q_cast(sc, w=QM):
                sc.wait_ge(pe.h, w)
                sc.copy(qf32[:, 0:512], pQ2[:, 0:512])
                sc.copy(qf32[:, 512:1024], pQ2[:, 512:1024])
                sc.mul(qbf[:, 0:512], pQ2[:, 0:512], 0.125)
                ac.inc(sc.mul(qbf[:, 512:1024], pQ2[:, 512:1024], 0.125))
            block.scalar(q_cast)
            QC = ac.n

            def q_tr(te, w=QC):
                te.wait_ge(ac.h, w)
                te.wait_ge(dv.h, dv.n)
                for c in range(8):
                    tr = te.transpose(pTR[:, 128 * c:128 * (c + 1)],
                                      qbf[:, 128 * c:128 * (c + 1)], kb.ident[:, :])
                pe.inc(tr)
            block.tensor(q_tr)
            QTR = pe.n

            def q_ev(v, w=QTR, t4=t4, sl2=s % 2, s=s):
                v.wait_ge(pe.h, w)
                if s >= 2 and t4 == 0:
                    v.wait_ge(qst[sl2].h, qst[sl2].n)
                dv.inc(v.tensor_copy(qfm_buf[sl2][:, :, 128 * t4:128 * (t4 + 1)],
                                     pTR.ap().rearrange("p (c n) -> p c n", n=128)))
            block.vector(q_ev)

            def sim_v(v, t=t, w=QC, winv=INVR):
                v.wait_ge(ac.h, w)
                if t == 0:
                    v.wait_ge(dv.h, winv)
                n = dv.inc(v.tensor_tensor(out=dump[:, :], in0=qf32[:, :], in1=qgrep[:, :],
                                           op=ALU.mult))
                v.wait_ge(dv.h, n)
                n = dv.inc(v.tensor_reduce(out=dh16[:, :],
                                           in_=dump.ap().rearrange("p (h d) -> p h d", d=64),
                                           axis=AX.X, op=ALU.add))
                v.wait_ge(dv.h, n)
                n = dv.inc(v.tensor_tensor(out=dump[:, :], in0=qf32[:, :], in1=qf32[:, :],
                                           op=ALU.mult))
                v.wait_ge(dv.h, n)
                dv.inc(v.tensor_reduce(out=sq16[:, :],
                                        in_=dump.ap().rearrange("p (h d) -> p h d", d=64),
                                        axis=AX.X, op=ALU.add))
            block.vector(sim_v)
            SV = dv.n

            def sim_rstd(sc, w=SV):
                sc.wait_ge(dv.h, w)
                a = ac.inc(sc.activation(out=rstdq[:, :], in_=sq16[:, :], func=AF.Ln,
                                         bias=kb.epsz[:, :], scale=1.0))
                sc.wait_ge(ac.h, a)
                ac.inc(sc.activation(out=rstdq[:, :], in_=rstdq[:, :], func=AF.Exp,
                                     scale=-0.5))
            block.scalar(sim_rstd)
            SR = ac.n

            def sim_fin(v, t=t, w=SR):
                v.wait_ge(ac.h, w)
                n = dv.inc(v.tensor_tensor(out=dh16[:, :], in0=dh16[:, :], in1=inv_rep[:, :],
                                           op=ALU.mult))
                v.wait_ge(dv.h, n)
                n = dv.inc(v.tensor_tensor(out=dh16[:, :], in0=dh16[:, :], in1=rstdq[:, :],
                                           op=ALU.mult))
                v.wait_ge(dv.h, n)
                dv.inc(v.tensor_reduce(out=sim_all[:, t:t + 1], in_=dh16[:, :],
                                       axis=AX.X, op=ALU.add))
            block.vector(sim_fin)

            def v_mm(te, b=b, t4=t4):
                te.wait_ge(dv.h, dv.n)
                for nb in range(2):
                    for kc in range(8):
                        mm = te.matmul(pQ2[:, 512 * nb:512 * (nb + 1)],
                                       xn_sl[b][:, kc, 128 * t4:128 * (t4 + 1)],
                                       w2[:, kc, 2048 + 512 * nb:2048 + 512 * (nb + 1)],
                                       start=(kc == 0), stop=(kc == 7))
                pe.inc(mm)
            block.tensor(v_mm)
            VM = pe.n

            def v_ev(v, t=t, bb=bb, w=VM):
                v.wait_ge(pe.h, w)
                if t >= 2:
                    v.wait_ge(vst[bb].h, vst[bb].n)
                v.tensor_copy(
                    vt_buf[bb].ap().rearrange("p (c n) -> p c n", n=65)[:, 0:8, 0:64],
                    pQ2.ap().rearrange("p (c n) -> p c n", n=64)[:, 0:8, :])
                dv.inc(v.tensor_copy(
                    vt_buf[bb].ap().rearrange("p (c n) -> p c n", n=65)[:, 8:16, 0:64],
                    pQ2.ap().rearrange("p (c n) -> p c n", n=64)[:, 8:16, :]))
            block.vector(v_ev)
            VEV = dv.n

            def v_st(sy, t=t, bb=bb, w=VEV):
                sy.wait_ge(dv.h, w)
                vst[bb].inc(sy.dma_start(
                    out=v_dram[t // 16, 128 * (t % 16):128 * (t % 16) + 128, :],
                    in_=vt_buf[bb][:, :]))
            block.sync(v_st)

        slice_pe_end[s] = pe.n

        def qk_store(sy, b2=s % 2, seq=seq, scol=scol, w=dv.n):
            sy.wait_ge(dv.h, w)
            qst[b2].inc(sy.dma_start(
                out=qfm_dram[seq, :, :, scol:scol + 512].rearrange("c p n -> p c n"),
                in_=qfm_buf[b2][:, :, :]))
            kst[b2].inc(sy.dma_start(
                out=kfm_dram[seq, :, :, scol:scol + 512].rearrange("c p n -> p c n"),
                in_=kfm_buf[b2][:, :, :]))
        block.sync(qk_store)
        if s % 4 == 3:
            qkv_done[seq] = [(qst[0], qst[0].n), (qst[1], qst[1].n),
                             (kst[0], kst[0].n), (kst[1], kst[1].n),
                             (vst[0], vst[0].n), (vst[1], vst[1].n)]

    SIM_ALL = dv.n

    def mm_bufs(v, w=SIM_ALL):
        v.wait_ge(dv.h, w)
        v.tensor_copy(sbuf2[:, :], sim_all[:, :])
        n = dv.inc(v.tensor_scalar(out=sbuf3[:, :], in0=sim_all[:, :], scalar1=-1.0,
                                   scalar2=None, op0=ALU.mult))
        v.wait_ge(dv.h, n)
        v.memset(sbuf2[0:5, 0:1], -1e30)
        v.memset(sbuf2[0:5, 16:17], -1e30)
        v.memset(sbuf3[0:5, 0:1], -1e30)
        n = dv.inc(v.memset(sbuf3[0:5, 16:17], -1e30))
        v.wait_ge(dv.h, n)
        v.tensor_reduce(out=mm2[:, 0:1], in_=sbuf2[:, :], axis=AX.X, op=ALU.max)
        dv.inc(v.tensor_reduce(out=mm2[:, 1:2], in_=sbuf3[:, :], axis=AX.X, op=ALU.max))
    block.vector(mm_bufs)
    MMB = dv.n

    def mm_tr(te, w=MMB):
        te.wait_ge(dv.h, w)
        te.wait_ge(dv.h, dv.n)
        pe.inc(te.transpose(pMM[0:2, 0:128], mm2[:, :], kb.identf[:, :]))
    block.tensor(mm_tr)
    MMT = pe.n

    def mm_red(v, w=MMT):
        v.wait_ge(pe.h, w)
        n = dv.inc(v.tensor_copy(mmt[:, :], pMM[0:2, 0:128]))
        v.wait_ge(dv.h, n)
        dv.inc(v.tensor_reduce(out=mmv[:, :], in_=mmt[:, :], axis=AX.X, op=ALU.max))
    block.vector(mm_red)
    MMR = dv.n

    def mm_tr2(te, w=MMR):
        te.wait_ge(dv.h, w)
        pe.inc(te.transpose(pMM[0:1, 0:2], mmv[:, :], kb.identf[0:2, 0:2]))
    block.tensor(mm_tr2)
    MMT2 = pe.n

    def mm_fin(v, w=MMT2):
        v.wait_ge(pe.h, w)
        dv.inc(v.tensor_copy(mml[:, :], pMM[0:1, 0:2]))
    block.vector(mm_fin)
    MMF = dv.n

    def mm_st(sy, w=MMF):
        sy.wait_ge(dv.h, w)
        mst.inc(sy.dma_start(out=minmax_dram, in_=mml[:, :]))
        mst.inc(sy.dma_start(out=simall_dram, in_=sim_all[:, :]))
    block.sync(mm_st)

    return {"wld": [wld], "cld": [cld], "xld": xld, "qst": qst, "kst": kst,
            "vst": vst, "mst": [mst], "qkv_done": qkv_done}


def ph_maskattn(kb, allmm_dram, simall_dram, qfm_dram, kfm_dram, v_dram,
                c09_dram, wflat_dram, wproj_dram, bprep_dram, x1_dram, s1_dram,
                out_dram, deps=None, x1s1_done=None, gate=None, tag="mf"):
    block = kb.block
    pe, dv, ac, po = kb.pe, kb.dv, kb.ac, kb.po

    mask_pos = kb.sb(f"{tag}_mpos", (128, 32), F32)
    mask_neg = kb.sb(f"{tag}_mneg", (128, 32), F32)
    w1m = kb.sb(f"{tag}_w1m", (128, 32), F32)
    w2m = kb.sb(f"{tag}_w2m", (128, 32), F32)
    wtb = kb.sb(f"{tag}_wtb", (32, 128), BF16)
    wf1 = kb.sb(f"{tag}_wf1", (1, 512), BF16)
    wf2 = kb.sb(f"{tag}_wf2", (1, 512), BF16)
    thr_rep = kb.sb(f"{tag}_thrrep", (128, 1), F32)
    simall = kb.sb(f"{tag}_simall", (128, 32), F32)
    allmm = kb.sb(f"{tag}_allmm", (8, 2), F32)
    c09 = kb.sb(f"{tag}_c09", (1, 2), F32)
    tiny2 = kb.sb(f"{tag}_tiny2", (2, 8), F32)
    tiny3 = kb.sb(f"{tag}_tiny3", (2, 1), F32)
    thr1 = kb.sb(f"{tag}_thr1", (1, 2), F32)
    thr0 = kb.sb(f"{tag}_thr0", (1, 1), F32)

    wproj_sb = kb.sb(f"{tag}_wproj", (128, 8, 1024), BF16)
    bprep = kb.sb(f"{tag}_bprep", (128, 1024), F32)
    q_sb = kb.sb(f"{tag}_q", (128, 8, SEQ), BF16)
    k_sb = kb.sb(f"{tag}_k", (128, 8, SEQ), BF16)
    v_sb = kb.sb(f"{tag}_v", (128, 16, 1040), BF16)
    vpos = kb.sb(f"{tag}_vpos", (128, 16, 1040), BF16)
    nm4 = kb.sb(f"{tag}_nm4", (128, 8), F32)
    nmx = kb.sb(f"{tag}_nmx", (128, 2), F32)
    phat = kb.sb(f"{tag}_phat", (128, 2048), BF16)
    ptblk = kb.sb(f"{tag}_ptblk", (128, 16, 512), BF16)
    c1 = kb.sb(f"{tag}_c1", (1, 512), F32)[0:1, :]
    c2 = kb.sb(f"{tag}_c2", (1, 512), F32)[0:1, :]
    c1rep = kb.sb(f"{tag}_c1rep", (64, 512), F32)
    c2rep = kb.sb(f"{tag}_c2rep", (64, 512), F32)
    ot1 = kb.sb(f"{tag}_ot1", (128, 512), F32)
    afm = kb.sb(f"{tag}_afm", (128, 8, 512), BF16)
    gsil = kb.sb(f"{tag}_gsil", (128, 1024), F32)
    x1t = [kb.sb(f"{tag}_x1t0", (128, 1024), F32)] * 2
    s1t = [kb.sb(f"{tag}_s1t0", (128, 1024), F32)] * 2
    outt = [kb.sb(f"{tag}_outt0", (128, 1024), F32)] * 2

    pS = kb.ps(f"{tag}_pS", (128, 2048), F32)
    pT = kb.ps(f"{tag}_pT", (128, 1024), BF16)
    pOp = kb.ps(f"{tag}_pOp", (65, 512), F32)
    pOn = kb.ps(f"{tag}_pOn", (65, 512), F32)
    pR = kb.ps(f"{tag}_pR", (64, 512), F32)

    mld = kb.dsem(f"{tag}_mld")
    wld = kb.dsem(f"{tag}_wld")
    wfb = kb.dsem(f"{tag}_wfb")
    wfl = kb.dsem(f"{tag}_wfl")
    qkld = kb.dsem(f"{tag}_qkld")
    xsld = kb.dsem(f"{tag}_xsld", 2)
    ost = kb.dsem(f"{tag}_ost", 2)

    def loadm(sy):
        kb.gate_wait(sy, gate)
        if deps is not None:
            for (p, n) in deps:
                sy.wait_ge(p.h, n)
        mld.inc(sy.dma_start(out=allmm[:, :], in_=allmm_dram))
        mld.inc(sy.dma_start(out=simall[:, :], in_=simall_dram))
        mld.inc(sy.dma_start(out=c09[:, :], in_=c09_dram))
    block.sync(loadm)
    MLD = mld.n

    def loadw(g):
        kb.gate_wait(g, gate)
        wld.inc(g.dma_start(out=wproj_sb[:, :, :],
                            in_=wproj_dram.rearrange("(k p) j -> p k j", p=128)))
    block.gpsimd(loadw)
    WLD = wld.n

    bpl = kb.dsem(f"{tag}_bpl")

    def loadbp(sy):
        kb.gate_wait(sy, gate)
        bpl.inc(sy.dma_start(out=bprep[:, :], in_=bprep_dram))
    block.sync(loadbp)
    MLD2 = bpl.n

    def thr_tr(te, w=MLD):
        te.wait_ge(mld.h, w)
        kb.gate_wait(te, gate)
        pe.inc(te.transpose(pS[0:2, 0:8], allmm[:, :], kb.identf[0:8, 0:8]))
    block.tensor(thr_tr)
    T1 = pe.n

    def thr_red(v, w=T1):
        v.wait_ge(pe.h, w)
        kb.gate_wait(v, gate)
        n = dv.inc(v.tensor_copy(tiny2[:, :], pS[0:2, 0:8]))
        v.wait_ge(dv.h, n)
        dv.inc(v.tensor_reduce(out=tiny3[:, :], in_=tiny2[:, :], axis=AX.X, op=ALU.max))
    block.vector(thr_red)
    T2 = dv.n

    def thr_tr2(te, w=T2):
        te.wait_ge(dv.h, w)
        pe.inc(te.transpose(pS[0:1, 0:2], tiny3[:, :], kb.identf[0:2, 0:2]))
    block.tensor(thr_tr2)
    T3 = pe.n

    def thr_fin(v, w=T3):
        v.wait_ge(pe.h, w)
        n = dv.inc(v.tensor_copy(thr1[:, :], pS[0:1, 0:2]))
        v.wait_ge(dv.h, n)
        n = dv.inc(v.tensor_tensor(out=thr1[:, :], in0=thr1[:, :], in1=c09[:, :],
                                   op=ALU.mult))
        v.wait_ge(dv.h, n)
        dv.inc(v.tensor_reduce(out=thr0[:, :], in_=thr1[:, :], axis=AX.X, op=ALU.add))
    block.vector(thr_fin)
    T4 = dv.n

    def thr_repmm(te, w=T4):
        te.wait_ge(dv.h, w)
        pe.inc(te.matmul(pS[:, 0:1], kb.ones128[:, :], thr0[:, :], start=True, stop=True))
    block.tensor(thr_repmm)
    T5 = pe.n

    def masks(v, w=T5):
        v.wait_ge(pe.h, w)
        n = dv.inc(v.tensor_copy(thr_rep[:, :], pS[:, 0:1]))
        v.wait_ge(dv.h, n)
        n = dv.inc(v.tensor_scalar(out=mask_pos[:, :], in0=simall[:, :],
                                   scalar1=thr_rep[:, :], scalar2=None, op0=ALU.is_gt))
        v.wait_ge(dv.h, n)
        n = dv.inc(v.tensor_scalar(out=mask_neg[:, :], in0=mask_pos[:, :],
                                   scalar1=-1.0, scalar2=1.0, op0=ALU.mult, op1=ALU.add))
        v.wait_ge(dv.h, n)
        v.tensor_copy(w1m[:, :], mask_pos[:, :])
        n = dv.inc(v.tensor_copy(w2m[:, :], mask_neg[:, :]))
        v.wait_ge(dv.h, n)
        for col in (0, 16):
            v.memset(mask_pos[0:5, col:col + 1], 1.0)
            v.memset(mask_neg[0:5, col:col + 1], 1.0)
            v.memset(w1m[0:5, col:col + 1], 0.5)
            n = dv.inc(v.memset(w2m[0:5, col:col + 1], 0.5))
    block.vector(masks)
    MSK = dv.n

    wf_re_at = 0
    for wi, wm in enumerate([w1m, w2m]):
        def w_tr(te, wm=wm, wi=wi, w=MSK):
            te.wait_ge(dv.h, w if wi == 0 else dv.n)
            pe.inc(te.transpose(pS[0:32, 0:128], wm[:, :], kb.identf[:, :]))
        block.tensor(w_tr)
        WT = pe.n

        def w_ev(v, w=WT, wi=wi):
            v.wait_ge(pe.h, w)
            if wi == 1:
                v.wait_ge(wfb.h, wfb.n)
            dv.inc(v.tensor_copy(wtb[:, :], pS[0:32, 0:128]))
        block.vector(w_ev)
        WE = dv.n

        def w_st(sy, wi=wi, w=WE):
            sy.wait_ge(dv.h, w)
            wfb.inc(sy.dma_start(out=wflat_dram[wi, :, :], in_=wtb[:, :]))
        block.sync(w_st)
        WS = wfb.n

    WFB = wfb.n

    out_done = []
    oscb = [0, 0]
    prev_sxp = [0]
    prev_str = [0]
    tev_prev = [0]
    fad_last = [0]
    for seq in range(2):
        def loadqkv(sy, seq=seq):
            if seq == 0:
                kb.gate_wait(sy, gate)
            if seq >= 1:
                sy.wait_ge(pe.h, pe.n)
                sy.wait_ge(dv.h, dv.n)
            if deps is not None:
                for (p, n) in deps:
                    sy.wait_ge(p.h, n)
            qkld.inc(sy.dma_start(out=q_sb[:, :, :],
                                  in_=qfm_dram[seq, :, :, :].rearrange("c p n -> p c n")))
            qkld.inc(sy.dma_start(out=k_sb[:, :, :],
                                  in_=kfm_dram[seq, :, :, :].rearrange("c p n -> p c n")))
            qkld.inc(sy.dma_start(out=v_sb[:, :, :],
                                  in_=v_dram[seq, :, :].rearrange("(c p) j -> p c j", p=128)))
        block.sync(loadqkv)
        QKL = qkld.n

        def v_masks(v, seq=seq, w=QKL, wm=MSK):
            v.wait_ge(qkld.h, w)
            v.wait_ge(dv.h, wm)
            for c in range(16):
                n = dv.inc(v.tensor_scalar(out=vpos[:, c, :], in0=v_sb[:, c, :],
                                           scalar1=mask_pos[:, 16 * seq + c:16 * seq + c + 1],
                                           scalar2=None, op0=ALU.mult))
            v.wait_ge(dv.h, n)
            for c in range(16):
                n = dv.inc(v.tensor_tensor(out=v_sb[:, c, :], in0=v_sb[:, c, :],
                                           in1=vpos[:, c, :], op=ALU.subtract))
            v.wait_ge(dv.h, n)
            dv.inc(v.tensor_copy(v_sb[0:5, 0, :], vpos[0:5, 0, :]))
        block.vector(v_masks)
        VMK = dv.n

        for qb in range(4):
            def wf_load(sy, seq=seq, qb=qb, w=WFB):
                sy.wait_ge(wfb.h, w)
                sy.wait_ge(dv.h, dv.n)
                base = 128 * (16 * seq + 4 * qb)
                wfl.inc(sy.dma_start(
                    out=wf1[0:1, :],
                    in_=wflat_dram[0, base // 128:base // 128 + 4, :].rearrange("a b -> (a b)")))
                wfl.inc(sy.dma_start(
                    out=wf2[0:1, :],
                    in_=wflat_dram[1, base // 128:base // 128 + 4, :].rearrange("a b -> (a b)")))
            block.sync(wf_load)
            WFL = wfl.n

            for h in range(16):
                hp = 64 * (h % 2)
                hc = h // 2
                SXP_at = [0] * 4
                STR_at = [0] * 4
                TEV_at = [0] * 4

                def emit_smm(qt):
                    qcol = 512 * qb + 128 * qt

                    def s_mm(te, hp=hp, hc=hc, qcol=qcol, w=VMK,
                             first=(qb == 0 and h == 0 and qt == 0),
                             wexp=(SXP_at[qt - 1] if qt >= 1 else prev_sxp[0]),
                             wfad=fad_last[0]):
                        if first:
                            te.wait_ge(dv.h, w)
                        if wexp:
                            te.wait_ge(ac.h, wexp)
                        if h == 0 and qt == 0 and wfad:
                            te.wait_ge(dv.h, wfad)
                        for j in range(4):
                            mm = te.matmul(pS[:, 512 * j:512 * (j + 1)],
                                           q_sb[hp:hp + 64, hc, qcol:qcol + 128],
                                           k_sb[hp:hp + 64, hc, 512 * j:512 * (j + 1)],
                                           start=True, stop=True)
                        pe.inc(mm)
                    block.tensor(s_mm)
                    return pe.n

                def emit_max_exp(qt, wsmm):
                    par = qt % 2

                    def s_max(v, w=wsmm, par=par):
                        v.wait_ge(pe.h, w)
                        for j in range(4):
                            r = v.tensor_reduce(out=nm4[:, 4 * par + j:4 * par + j + 1],
                                                in_=pS[:, 512 * j:512 * (j + 1)],
                                                axis=AX.X, op=ALU.max)
                        n = dv.inc(r)
                        v.wait_ge(dv.h, n)
                        dv.inc(v.tensor_reduce(out=nmx[:, par:par + 1],
                                               in_=nm4[:, 4 * par:4 * par + 4], axis=AX.X,
                                               op=ALU.max, negate=True))
                    block.vector(s_max)
                    smx = dv.n

                    def s_exp(sc, w=smx, par=par,
                              wstr=(STR_at[qt - 1] if qt >= 1 else prev_str[0])):
                        sc.wait_ge(dv.h, w)
                        if wstr:
                            sc.wait_ge(pe.h, wstr)
                        for j in range(4):
                            a = sc.activation(out=phat[:, 512 * j:512 * (j + 1)],
                                              in_=pS[:, 512 * j:512 * (j + 1)],
                                              func=AF.Exp, bias=nmx[:, par:par + 1], scale=1.0)
                        ac.inc(a)
                    block.scalar(s_exp)
                    SXP_at[qt] = ac.n

                def emit_tr(qt):
                    for rr in range(2):
                        def s_tr(te, w=SXP_at[qt], rr=rr, wtev=tev_prev[0]):
                            te.wait_ge(ac.h, w)
                            if wtev:
                                te.wait_ge(dv.h, wtev)
                            for c in range(8):
                                tr = te.transpose(pT[:, 128 * c:128 * (c + 1)],
                                                  phat[:, 128 * (8 * rr + c):128 * (8 * rr + c + 1)],
                                                  kb.ident[:, :])
                            pe.inc(tr)
                        block.tensor(s_tr)
                        STR = pe.n

                        def s_tev(v, w=STR, qt=qt, rr=rr):
                            v.wait_ge(pe.h, w)
                            dv.inc(v.tensor_copy(
                                ptblk[:, 8 * rr:8 * rr + 8, 128 * qt:128 * (qt + 1)],
                                pT.ap().rearrange("p (c n) -> p c n", n=128)[:, 0:8, :]))
                        block.vector(s_tev)
                        tev_prev[0] = dv.n
                    if not STR_at[qt]:
                        STR_at[qt] = pe.n
                    TEV_at[qt] = dv.n

                w0 = emit_smm(0)
                emit_max_exp(0, w0)
                for qt in range(1, 4):
                    w = emit_smm(qt)
                    STR_at[qt - 1] = pe.n + 2
                    emit_max_exp(qt, w)
                    emit_tr(qt - 1)
                    assert pe.n == STR_at[qt - 1]
                emit_tr(3)
                prev_sxp[0] = SXP_at[3]
                prev_str[0] = STR_at[3]
                TEV = TEV_at[3]

                def pv_mm(te, h=h, w=TEV):
                    te.wait_ge(dv.h, w)
                    for kc in range(16):
                        te.matmul(pOp[:, :], vpos[:, kc, 65 * h:65 * h + 65],
                                  ptblk[:, kc, :], start=(kc == 0), stop=(kc == 15))
                    for kc in range(16):
                        mm = te.matmul(pOn[:, :], v_sb[:, kc, 65 * h:65 * h + 65],
                                       ptblk[:, kc, :], start=(kc == 0), stop=(kc == 15))
                    pe.inc(mm)
                block.tensor(pv_mm)
                PVM = pe.n

                def pv_recip(v, w=PVM, wfl_n=WFL, first=(h == 0)):
                    v.wait_ge(pe.h, w)
                    if first:
                        v.wait_ge(wfl.h, wfl_n)
                    v.reciprocal(c1, pOp[64:65, :])
                    n = dv.inc(v.reciprocal(c2, pOn[64:65, :]))
                    v.wait_ge(dv.h, n)
                    v.tensor_tensor(out=c1, in0=c1, in1=wf1[0:1, :], op=ALU.mult)
                    dv.inc(v.tensor_tensor(out=c2, in0=c2, in1=wf2[0:1, :], op=ALU.mult))
                block.vector(pv_recip)
                RCP = dv.n

                def pv_rep1(te, w=RCP):
                    te.wait_ge(dv.h, w)
                    pe.inc(te.matmul(pR[:, :], kb.ones128[0:1, 0:64], c1,
                                     start=True, stop=True))
                block.tensor(pv_rep1)
                REP1 = pe.n

                def pv_rev1(v, w=REP1):
                    v.wait_ge(pe.h, w)
                    dv.inc(v.tensor_copy(c1rep[:, :], pR[:, :]))
                block.vector(pv_rev1)
                REV1 = dv.n

                def pv_rep2(te, w=REV1):
                    te.wait_ge(dv.h, w)
                    pe.inc(te.matmul(pR[:, :], kb.ones128[0:1, 0:64], c2,
                                     start=True, stop=True))
                block.tensor(pv_rep2)
                REP2 = pe.n

                def pv_comb(v, w=REP2, hp=hp, hc=hc):
                    v.wait_ge(pe.h, w)
                    n0 = dv.inc(v.tensor_copy(c2rep[:, :], pR[:, :]))
                    v.wait_ge(dv.h, n0)
                    v.tensor_tensor(out=ot1[hp:hp + 64, :], in0=pOp[0:64, :],
                                    in1=c1rep[:, :], op=ALU.mult)
                    n = dv.inc(v.tensor_tensor(out=afm[hp:hp + 64, hc, :], in0=pOn[0:64, :],
                                               in1=c2rep[:, :], op=ALU.mult))
                    v.wait_ge(dv.h, n)
                    dv.inc(v.tensor_tensor(out=afm[hp:hp + 64, hc, :],
                                           in0=afm[hp:hp + 64, hc, :],
                                           in1=ot1[hp:hp + 64, :], op=ALU.add))
                block.vector(pv_comb)
            HEADS_DONE = dv.n

            for qt in range(4):
                t = 16 * seq + 4 * qb + qt
                bb = qt % 2

                def wp_mm(te, qt=qt, w=HEADS_DONE, wld_n=WLD,
                          first=(seq == 0 and qb == 0 and qt == 0),
                          wfad=fad_last[0]):
                    te.wait_ge(dv.h, max(w, wfad))
                    if first:
                        te.wait_ge(wld.h, wld_n)
                    for nb in range(2):
                        for kc in range(8):
                            mm = te.matmul(pS[:, 512 * nb:512 * (nb + 1)],
                                           afm[:, kc, 128 * qt:128 * (qt + 1)],
                                           wproj_sb[:, kc, 512 * nb:512 * (nb + 1)],
                                           start=(kc == 0), stop=(kc == 7))
                    pe.inc(mm)
                block.tensor(wp_mm)
                WPM = pe.n

                def loadxs(sy, t=t, bb=bb, first=(seq == 0 and qb == 0 and qt == 0)):
                    if first:
                        kb.gate_wait(sy, gate)
                        if x1s1_done is not None:
                            for (p, n) in x1s1_done:
                                sy.wait_ge(p.h, n)
                    sy.wait_ge(dv.h, max(oscb))
                    xsld[bb].inc(sy.dma_start(out=x1t[bb][:, :],
                                              in_=x1_dram[128 * t:128 * (t + 1), :]))
                    xsld[bb].inc(sy.dma_start(out=s1t[bb][:, :],
                                              in_=s1_dram[128 * t:128 * (t + 1), :]))
                block.sync(loadxs)
                XSL = xsld[bb].n

                def f_add(v, w=WPM, wb=MLD2):
                    v.wait_ge(pe.h, w)
                    v.wait_ge(bpl.h, wb)
                    v.tensor_tensor(out=gsil[:, 0:512], in0=pS[:, 0:512],
                                    in1=bprep[:, 0:512], op=ALU.add)
                    dv.inc(v.tensor_tensor(out=gsil[:, 512:1024], in0=pS[:, 512:1024],
                                           in1=bprep[:, 512:1024], op=ALU.add))
                block.vector(f_add)
                FAD = dv.n
                fad_last[0] = FAD

                def g_silu(sc, w=FAD):
                    sc.wait_ge(dv.h, w)
                    ac.inc(sc.activation(out=gsil[:, :], in_=gsil[:, :], func=AF.Silu))
                block.scalar(g_silu)
                GS = ac.n

                def comb(v, bb=bb, w=GS, wxs=XSL):
                    v.wait_ge(ac.h, w)
                    v.wait_ge(xsld[bb].h, wxs)
                    v.wait_ge(ost[0].h, ost[0].n)
                    v.wait_ge(ost[1].h, ost[1].n)
                    n = dv.inc(v.tensor_tensor(out=outt[bb][:, :], in0=gsil[:, :],
                                               in1=s1t[bb][:, :], op=ALU.mult))
                    v.wait_ge(dv.h, n)
                    dv.inc(v.tensor_tensor(out=outt[bb][:, :], in0=outt[bb][:, :],
                                           in1=x1t[bb][:, :], op=ALU.add))
                block.vector(comb)
                CB = dv.n
                oscb[bb] = CB

                def o_st(sy, t=t, bb=bb, w=CB):
                    sy.wait_ge(dv.h, w)
                    ost[bb].inc(sy.dma_start(out=out_dram[128 * t:128 * (t + 1), :],
                                             in_=outt[bb][:, :]))
                block.sync(o_st)
                out_done.append((ost[bb], ost[bb].n))

    return {"mld": [mld], "wld": [wld], "bpl": [bpl], "wfb": [wfb], "wfl": [wfl],
            "qkld": [qkld], "xsld": xsld, "ost": ost, "out_done": out_done}


def build_l1(dev=()):
    nc = bass.Bass("TRN2", target_bir_lowering=False, debug=False)

    def dram(name, shape, dtype):
        kind = "ExternalOutput" if name in dev else "Internal"
        return nc.dram_tensor(name, list(shape), dtype, kind=kind).ap()

    x0 = nc.dram_tensor("x0", [TOK, DM], F32, kind="ExternalInput").ap()
    ident_d = nc.dram_tensor("c_ident", [128, 128], BF16, kind="ExternalInput").ap()
    identf_d = nc.dram_tensor("c_identf", [128, 128], F32, kind="ExternalInput").ap()
    ones_d = nc.dram_tensor("c_ones", [1, 128], F32, kind="ExternalInput").ap()
    cos_d = nc.dram_tensor("c_cos", [SEQ, DM // 2], F32, kind="ExternalInput").ap()
    sin_d = nc.dram_tensor("c_sin", [SEQ, DM // 2], F32, kind="ExternalInput").ap()

    W0 = {}
    for nm, shp in [("wdq", (1024, 512)), ("wdkv", (1024, 512)), ("wkr", (1024, 1024)),
                    ("wuq", (512, 1024)), ("wqr", (512, 1024)), ("wuk", (512, 1024)),
                    ("wuv", (512, 1024)), ("bdq", (128, 4)), ("bdkv", (128, 4)),
                    ("brep_uq", (128, 1024)), ("brep_qr", (128, 1024)),
                    ("brep_uk", (128, 1024)), ("brep_kr", (128, 1024)),
                    ("brep_uv", (128, 1024))]:
        W0[nm] = nc.dram_tensor("w0_" + nm, list(shp), F32, kind="ExternalInput").ap()

    wo0 = nc.dram_tensor("w0_wo", [1024, 1024], F32, kind="ExternalInput").ap()
    borep0 = nc.dram_tensor("w0_brep_o", [128, 1024], F32, kind="ExternalInput").ap()

    xnfm0 = dram("xnfm0", (8, 128, TOK), BF16)
    xntm0 = dram("xntm0", (TOK, DM), F32) if "xntm0" in dev else None
    qfm0 = dram("qfm0", (2, 8, 128, SEQ), BF16)
    kfm0 = dram("kfm0", (2, 8, 128, SEQ), BF16)
    v0 = dram("v0", (2, SEQ, 1040), BF16)
    s0 = dram("s0", (TOK, DM), F32)
    wgT = nc.dram_tensor("w0_wgT", [1024, 3072], I32, kind="ExternalInput").ap()
    wuT = nc.dram_tensor("w0_wuT", [1024, 3072], I32, kind="ExternalInput").ap()
    wdT = nc.dram_tensor("w0_wdT", [3072, 1024], I32, kind="ExternalInput").ap()
    x1 = dram("x1", (TOK, DM), F32)
    gind_d = nc.dram_tensor("c_gind", [128, 4], F32, kind="ExternalInput").ap()
    e2_d = nc.dram_tensor("c_e2", [4, 128], F32, kind="ExternalInput").ap()
    gcols_d = nc.dram_tensor("w1_gcols", [128, 8], F32, kind="ExternalInput").ap()
    bcols_d = nc.dram_tensor("w1_bcols", [128, 8], F32, kind="ExternalInput").ap()
    xnfm1 = dram("xnfm1", (8, 128, TOK), BF16)
    xn1tm = dram("xn1tm", (TOK, DM), F32)
    abdev = dram("abdev", (128, 32), F32) if "abdev" in dev else None
    w1qkv = nc.dram_tensor("w1_qkv", [1024, 3072], F32, kind="ExternalInput").ap()
    w1bq = nc.dram_tensor("w1_bq", [128, 8], F32, kind="ExternalInput").ap()
    w1bk = nc.dram_tensor("w1_bk", [128, 8], F32, kind="ExternalInput").ap()
    w1bvrep = nc.dram_tensor("w1_bvrep", [128, 1024], F32, kind="ExternalInput").ap()
    w1o = nc.dram_tensor("w1_wo", [1024, 1024], F32, kind="ExternalInput").ap()
    w1borep = nc.dram_tensor("w1_brep_o", [128, 1024], F32, kind="ExternalInput").ap()
    qfm1 = dram("qfm1", (2, 8, 128, SEQ), BF16)
    kfm1 = dram("kfm1", (2, 8, 128, SEQ), BF16)
    v1 = dram("v1", (2, SEQ, 1040), BF16)
    s1 = dram("s1", (TOK, DM), F32)
    w2qkv = nc.dram_tensor("w2_qkv", [1024, 3072], F32, kind="ExternalInput").ap()
    qgrep_d = nc.dram_tensor("w2_qgrep", [128, 1024], F32, kind="ExternalInput").ap()
    qfm2 = dram("qfm2", (2, 8, 128, SEQ), BF16)
    kfm2 = dram("kfm2", (2, 8, 128, SEQ), BF16)
    v2 = dram("v2", (2, SEQ, 1040), BF16)
    simall = dram("simall", (128, 32), F32)
    minmax = dram("minmax", (1, 2), F32)

    with ExitStack() as ctx:
        block = ctx.enter_context(nc.Block())
        kb = KB(nc, ctx, block)
        kb.ident = kb.sb("identsb", (128, 128), BF16)
        kb.identf = kb.sb("identfsb", (128, 128), F32)
        kb.ones128 = kb.sb("ones128sb", (1, 128), F32)
        kb.epsln = kb.sb("epsln", (128, 1), F32)
        kb.epsgn = kb.sb("epsgn", (128, 1), F32)
        kb.epsz = kb.sb("epsz", (128, 1), F32)
        identld = kb.dsem("identld")

        def load_const(s):
            identld.inc(s.dma_start(out=kb.ident[:, :], in_=ident_d))
            identld.inc(s.dma_start(out=kb.identf[:, :], in_=identf_d))
            identld.inc(s.dma_start(out=kb.ones128[:, :], in_=ones_d))
        block.sync(load_const)
        iw = identld.n

        def init_consts(v):
            v.memset(kb.epsln[:, :], LN_EPS)
            v.memset(kb.epsz[:, :], 0.0)
            kb.dv.inc(v.memset(kb.epsgn[:, :], GN_EPS))
        block.vector(init_consts)
        cw = kb.dv.n

        def wident(t):
            t.wait_ge(identld.h, iw)
        block.tensor(wident)

        def wconst(sc):
            sc.wait_ge(kb.dv.h, cw)
        block.scalar(wconst)

        allsems = []
        with kb.phase():
            lnres = ph_ln(kb, x0, xnfm0, xntm_dram=xntm0, tag="0")
        allsems.append(lnres)
        with kb.phase():
            p0res = ph_proj0(kb, xnfm0, cos_d, sin_d, W0, qfm0, kfm0, v0,
                             fm_done=lnres["fm_done"], gate=kb.token(), tag="p0")
        allsems.append(p0res)
        with kb.phase():
            a0res = ph_attn(kb, qfm0, kfm0, v0, wo0, borep0, s0,
                            resid_dram=None, deps=p0res["qkv_done"],
                            gate=kb.token(), tag="a0")
        allsems.append(a0res)
        with kb.phase():
            f0res = ph_ffn0(kb, xnfm0, wgT, wuT, wdT, s0, x0, x1,
                            fm_done=lnres["fm_done"], s_done=a0res["s_done"],
                            gate=kb.token(), tag="f0")
        allsems.append(f0res)
        A_gn = kb.sb("A_gn", (128, 16), F32)
        B_gn = kb.sb("B_gn", (128, 16), F32)
        with kb.phase():
            ln1res = ph_ln(kb, x1, xnfm1, xntm_dram=xn1tm, tag="1",
                           gate=kb.token(), x_done=f0res["x1_done"])
        allsems.append(ln1res)
        with kb.phase():
            gnres = ph_gn(kb, xnfm1, gind_d, e2_d, gcols_d, bcols_d, A_gn, B_gn,
                          fm_done=ln1res["fm_done"], gate=kb.token(), tag="gn")
        allsems.append(gnres)
        with kb.phase():
            p1res = ph_proj1(kb, xnfm1, w1qkv, w1bq, w1bk, w1bvrep, A_gn, B_gn,
                             qfm1, kfm1, v1, fm_done=ln1res["fm_done"],
                             ab_done=gnres["ab_done"], gate=kb.token(), tag="p1")
        allsems.append(p1res)
        with kb.phase():
            a1res = ph_attn(kb, qfm1, kfm1, v1, w1o, w1borep, s1,
                            resid_dram=xn1tm, deps=p1res["qkv_done"],
                            gate=kb.token(), tag="a1")
        allsems.append(a1res)
        with kb.phase():
            p2res = ph_proj2(kb, xnfm1, w2qkv, qgrep_d, qfm2, kfm2, v2,
                             simall, minmax, fm_done=ln1res["fm_done"],
                             gate=kb.token(), tag="p2")
        allsems.append(p2res)
        if abdev is not None:
            abst = kb.dsem("abst")

            def ab_store(sy):
                sy.wait_ge(kb.dv.h, gnres["ab_done"])
                abst.inc(sy.dma_start(out=abdev[:, 0:16], in_=A_gn[:, :]))
                abst.inc(sy.dma_start(out=abdev[:, 16:32], in_=B_gn[:, :]))
            block.sync(ab_store)
            allsems.append({"abst": [abst]})

        def fin(s):
            for sems in allsems:
                for key, grp in sems.items():
                    if key.endswith("_done"):
                        continue
                    for p in (grp if isinstance(grp, list) else [grp]):
                        s.wait_ge(p.h, p.n)
        block.sync(fin)
    return nc


def shard_inputs(inputs):
    c = host_constants()
    I = inputs
    f32 = lambda k: np.ascontiguousarray(np.asarray(I[k], dtype=np.float32))
    rep = lambda k: np.ascontiguousarray(np.tile(np.asarray(I[k], dtype=np.float32)[None, :], (128, 1)))
    shared = {
        "c_ident": c["c_ident"], "c_cos": c["c_cos"], "c_sin": c["c_sin"],
        "w0_wdq": f32("L0_attn_Wd_q_w"), "w0_wdkv": f32("L0_attn_Wd_kv_w"),
        "w0_wkr": f32("L0_attn_Wk_r_w"), "w0_wuq": f32("L0_attn_Wu_q_w"),
        "w0_wqr": f32("L0_attn_Wq_r_w"), "w0_wuk": f32("L0_attn_Wu_k_w"),
        "w0_wuv": f32("L0_attn_Wu_v_w"),
        "w0_bdq": np.ascontiguousarray(f32("L0_attn_Wd_q_b").reshape(4, 128).T),
        "w0_bdkv": np.ascontiguousarray(f32("L0_attn_Wd_kv_b").reshape(4, 128).T),
        "w0_brep_uq": rep("L0_attn_Wu_q_b"), "w0_brep_qr": rep("L0_attn_Wq_r_b"),
        "w0_brep_uk": rep("L0_attn_Wu_k_b"), "w0_brep_kr": rep("L0_attn_Wk_r_b"),
        "w0_brep_uv": rep("L0_attn_Wu_v_b"),
        "w0_wo": f32("L0_attn_Wo_w"), "w0_brep_o": rep("L0_attn_Wo_b"),
        "w0_wgT": np.ascontiguousarray(np.asarray(I["L0_ffn_w_gate"]).T),
        "w0_wuT": np.ascontiguousarray(np.asarray(I["L0_ffn_w_up"]).T),
        "w0_wdT": np.ascontiguousarray(np.asarray(I["L0_ffn_w_down"]).T),
        "c_gind": c["c_gind"], "c_e2": c["c_e2"],
        "c_identf": c["c_identf"], "c_ones": c["c_ones"],
        "w1_gcols": np.ascontiguousarray(f32("L1_attn_gn_gamma").reshape(8, 128).T),
        "w1_bcols": np.ascontiguousarray(f32("L1_attn_gn_beta").reshape(8, 128).T),
        "w1_qkv": f32("L1_attn_W_qkv"),
        "w1_bq": np.ascontiguousarray(f32("L1_attn_b_qkv")[0:1024].reshape(8, 128).T),
        "w1_bk": np.ascontiguousarray(f32("L1_attn_b_qkv")[1024:2048].reshape(8, 128).T),
        "w1_bvrep": np.ascontiguousarray(np.tile(f32("L1_attn_b_qkv")[None, 2048:3072], (128, 1))),
        "w1_wo": f32("L1_attn_W_o"), "w1_brep_o": rep("L1_attn_b_o"),
        "w2_qkv": f32("L1_ffn_W_qkv"),
        "w2_qgrep": np.ascontiguousarray(np.tile(
            np.asarray(I["L1_ffn_g_info"], np.float32)[0, 0][None, :], (128, 1))),
    }
    x = np.ascontiguousarray(np.asarray(inputs["L0_attn_x"], dtype=np.float32))
    maps = []
    for core in range(NCORES):
        m = dict(shared)
        m["x0"] = np.ascontiguousarray(x[2 * core:2 * core + 2].reshape(TOK, DM))
        maps.append(m)
    return maps


CROSS = ("qfm2", "kfm2", "v2", "simall", "minmax", "x1", "s1")


def build_l2(dev=()):
    nc = bass.Bass("TRN2", target_bir_lowering=False, debug=False)

    def dram(name, shape, dtype):
        kind = "ExternalOutput" if name in dev else "Internal"
        return nc.dram_tensor(name, list(shape), dtype, kind=kind).ap()

    ident_d = nc.dram_tensor("c_ident", [128, 128], BF16, kind="ExternalInput").ap()
    identf_d = nc.dram_tensor("c_identf", [128, 128], F32, kind="ExternalInput").ap()
    ones_d = nc.dram_tensor("c_ones", [1, 128], F32, kind="ExternalInput").ap()
    c09_d = nc.dram_tensor("c_c09", [1, 2], F32, kind="ExternalInput").ap()
    allmm_d = nc.dram_tensor("allmm", [8, 2], F32, kind="ExternalInput").ap()
    simall_d = nc.dram_tensor("simall", [128, 32], F32, kind="ExternalInput").ap()
    qfm2 = nc.dram_tensor("qfm2", [2, 8, 128, SEQ], BF16, kind="ExternalInput").ap()
    kfm2 = nc.dram_tensor("kfm2", [2, 8, 128, SEQ], BF16, kind="ExternalInput").ap()
    v2 = nc.dram_tensor("v2", [2, SEQ, 1040], BF16, kind="ExternalInput").ap()
    x1 = nc.dram_tensor("x1", [TOK, DM], F32, kind="ExternalInput").ap()
    s1 = nc.dram_tensor("s1", [TOK, DM], F32, kind="ExternalInput").ap()
    wproj_d = nc.dram_tensor("w2_proj", [1024, 1024], F32, kind="ExternalInput").ap()
    bprep_d = nc.dram_tensor("w2_bprep", [128, 1024], F32, kind="ExternalInput").ap()
    out_d = nc.dram_tensor("outp", [TOK, DM], F32, kind="ExternalOutput").ap()
    wflat = dram("wflat", (2, 32, 128), BF16)

    with ExitStack() as ctx:
        block = ctx.enter_context(nc.Block())
        kb = KB(nc, ctx, block)
        kb.ident = kb.sb("identsb", (128, 128), BF16)
        kb.identf = kb.sb("identfsb", (128, 128), F32)
        kb.ones128 = kb.sb("ones128sb", (1, 128), F32)
        identld = kb.dsem("identld")

        def load_const(s):
            identld.inc(s.dma_start(out=kb.ident[:, :], in_=ident_d))
            identld.inc(s.dma_start(out=kb.identf[:, :], in_=identf_d))
            identld.inc(s.dma_start(out=kb.ones128[:, :], in_=ones_d))
        block.sync(load_const)
        iw = identld.n

        def wident(t):
            t.wait_ge(identld.h, iw)
        block.tensor(wident)

        with kb.phase():
            mres = ph_maskattn(kb, allmm_d, simall_d, qfm2, kfm2, v2,
                               c09_d, wflat, wproj_d, bprep_d, x1, s1, out_d,
                               tag="mf")

        def fin(s):
            for key, grp in mres.items():
                if key.endswith("_done"):
                    continue
                for p in (grp if isinstance(grp, list) else [grp]):
                    s.wait_ge(p.h, p.n)
        block.sync(fin)
    return nc


def l2_shared(inputs):
    c = host_constants()
    I = inputs
    f32 = lambda k: np.ascontiguousarray(np.asarray(I[k], dtype=np.float32))
    rep = lambda k: np.ascontiguousarray(np.tile(np.asarray(I[k], dtype=np.float32)[None, :], (128, 1)))
    return {
        "c_ident": c["c_ident"], "c_identf": c["c_identf"], "c_ones": c["c_ones"],
        "c_c09": np.array([[0.9, -0.1]], dtype=np.float32),
        "w2_proj": f32("L1_ffn_W_proj"), "w2_bprep": rep("L1_ffn_b_proj"),
    }


def kernel(**inputs):
    maps = shard_inputs(inputs)
    nc1 = build_l1(dev=CROSS)
    res1 = run_bass_kernel_spmd(nc1, maps, core_ids=list(range(NCORES)))
    r1 = res1.results

    allmm = np.ascontiguousarray(
        np.concatenate([np.asarray(r1[c]["minmax"]) for c in range(NCORES)], axis=0))
    sh2 = l2_shared(inputs)
    maps2 = []
    for c in range(NCORES):
        m = dict(sh2)
        m["allmm"] = allmm
        for k in ("simall", "qfm2", "kfm2", "v2", "x1", "s1"):
            m[k] = np.asarray(r1[c][k])
        maps2.append(m)
    nc2 = build_l2()
    res2 = run_bass_kernel_spmd(nc2, maps2, core_ids=list(range(NCORES)))
    out = np.concatenate(
        [np.asarray(res2.results[c]["outp"], dtype=np.float32).reshape(2, SEQ, DM)
         for c in range(NCORES)], axis=0)
    return out


# version 06 (final)
# speedup vs baseline: 1.1821x; 1.1821x over previous
import os
import numpy as np
import ml_dtypes
import concourse.bass as bass
import concourse.mybir as mybir
from concourse.bass_utils import run_bass_kernel_spmd
from contextlib import ExitStack

F32 = mybir.dt.float32
BF16 = mybir.dt.bfloat16
I32 = mybir.dt.int32
AF = mybir.ActivationFunctionType
ALU = mybir.AluOpType
AX = mybir.AxisListType
bfnp = ml_dtypes.bfloat16

NCORES = 8
SEQ = 2048
DM = 1024
TOK = 2 * SEQ
NT = TOK // 128
LN_EPS = 1e-5
GN_EPS = 1e-6


class Prod:
    def __init__(self, ctx, nc, name, dma=False):
        self.h = ctx.enter_context(nc.semaphore(name))
        self.n = 0
        self.step = 16 if dma else 1

    def inc(self, instr, k=1):
        instr.then_inc(self.h, self.step * k)
        self.n += self.step * k
        return self.n


class KB:
    def __init__(self, nc, ctx, block):
        self.nc = nc
        self.ctx = ctx
        self.mem = ctx
        self.block = block
        self.prods = []
        self.pe = self._prod("pe")
        self.dv = self._prod("dv")
        self.ac = self._prod("ac")
        self.po = self._prod("po")

    def _prod(self, name, dma=False):
        p = Prod(self.ctx, self.nc, name, dma=dma)
        self.prods.append(p)
        return p

    def dsem(self, name, n=1):
        if n == 1:
            return self._prod(name, dma=True)
        return [self._prod(f"{name}{i}", dma=True) for i in range(n)]

    def token(self):
        return [(p, p.n) for p in self.prods if p.n > 0]

    def gate_wait(self, eng, gate):
        if gate:
            for p, n in gate:
                eng.wait_ge(p.h, n)

    def phase(self):
        return _PhaseScope(self)

    def sb(self, name, shape, dtype):
        return self.mem.enter_context(self.nc.sbuf_tensor(name, list(shape), dtype))

    def ps(self, name, shape, dtype):
        return self.mem.enter_context(self.nc.psum_tensor(name, list(shape), dtype))


class _PhaseScope:
    def __init__(self, kb):
        self.kb = kb
        self.stack = ExitStack()

    def __enter__(self):
        self.prev = self.kb.mem
        self.kb.mem = self.stack
        return self

    def __exit__(self, *a):
        self.stack.close()
        self.kb.mem = self.prev
        return False


def host_constants():
    c = {}
    c["c_ident"] = np.eye(128, dtype=bfnp)
    c["c_identf"] = np.eye(128, dtype=np.float32)
    c["c_ones"] = np.ones((1, 128), dtype=np.float32)
    pos = np.arange(SEQ, dtype=np.float32)[:, None]
    ids = np.arange(DM // 2, dtype=np.float32)
    theta = (np.float32(10000.0) ** (np.float32(-2.0) * ids)) / np.float32(DM // 2)
    r = pos * theta[None, :]
    c["c_cos"] = np.cos(r).astype(np.float32)
    c["c_sin"] = np.sin(r).astype(np.float32)
    gind = np.zeros((128, 4), dtype=np.float32)
    for p in range(128):
        gind[p, p // 32] = 1.0
    c["c_gind"] = gind
    c["c_e2"] = np.ascontiguousarray(gind.T)
    return c


def ph_ln(kb, x_dram, xnfm_dram, xntm_dram=None, gate=None, x_done=None, tag=""):
    block = kb.block
    pe, dv, ac = kb.pe, kb.dv, kb.ac

    xt = [kb.sb(f"ln{tag}_x{i}", (128, 1024), F32) for i in range(2)]
    xn32 = [kb.sb(f"ln{tag}_xn32{i}", (128, 1024), F32) for i in range(2)]
    xnbf = [kb.sb(f"ln{tag}_xnbf{i}", (128, 1024), BF16) for i in range(2)]
    stat = kb.sb(f"ln{tag}_stat", (128, 12), F32)
    mv = kb.sb(f"ln{tag}_mv", (128, 2), F32)
    rstd = kb.sb(f"ln{tag}_rstd", (128, 1), F32)
    lnb = kb.sb(f"ln{tag}_lnb", (128, 1), F32)
    fmbuf = [kb.sb(f"ln{tag}_fm{i}", (128, 8, 512), BF16) for i in range(2)]
    ptr = kb.ps(f"ln{tag}_ptr", (128, 1024), BF16)

    xld = kb.dsem(f"ln{tag}_xld", 2)
    tmst = kb.dsem(f"ln{tag}_tmst", 2)
    fmst = kb.dsem(f"ln{tag}_fmst", 2)

    end = [0] * NT
    fm_done = [None] * (NT // 4)
    tm_done = [None] * NT
    for t in range(NT):
        b = t % 2
        sl = t // 4

        def load(s, t=t, b=b):
            if t < 2:
                kb.gate_wait(s, gate)
            if t >= 2:
                s.wait_ge(dv.h, end[t - 2])
            if x_done is not None:
                s.wait_ge(x_done[t][0].h, x_done[t][1])
            xld[b].inc(s.dma_start(out=xt[b][:, :], in_=x_dram[128 * t:128 * (t + 1), :]))
        block.sync(load)
        ld_n = xld[b].n

        def stats(v, t=t, b=b, w=ld_n):
            v.wait_ge(xld[b].h, w)
            if t >= 1:
                v.wait_ge(ac.h, ac.n)
            v.bn_stats(stat[:, 0:6], xt[b][:, 0:512])
            n = dv.inc(v.bn_stats(stat[:, 6:12], xt[b][:, 512:1024]))
            v.wait_ge(dv.h, n)
            dv.inc(v.bn_aggr(mv[:, :], stat[:, :]))
        block.vector(stats)
        aggr_n = dv.n

        def rstd_f(sc, w=aggr_n):
            sc.wait_ge(dv.h, w)
            a = ac.inc(sc.activation(out=rstd[:, :], in_=mv[:, 1:2], func=AF.Ln,
                                     bias=kb.epsln[:, :], scale=1.0))
            sc.wait_ge(ac.h, a)
            ac.inc(sc.activation(out=rstd[:, :], in_=rstd[:, :], func=AF.Exp, scale=-0.5))
        block.scalar(rstd_f)
        rstd_n = ac.n

        def bias_f(v, w=rstd_n):
            v.wait_ge(ac.h, w)
            dv.inc(v.tensor_scalar(out=lnb[:, :], in0=mv[:, 0:1], scalar1=rstd[:, :],
                                   scalar2=-1.0, op0=ALU.mult, op1=ALU.mult))
        block.vector(bias_f)
        bias_n = dv.n

        def apply(sc, t=t, b=b, w=bias_n):
            sc.wait_ge(dv.h, w)
            if xntm_dram is not None:
                if t >= 2:
                    sc.wait_ge(tmst[b].h, tmst[b].n)
                sc.activation(out=xn32[b][:, :], in_=xt[b][:, :], func=AF.Identity,
                              scale=rstd[:, :], bias=lnb[:, :])
            ac.inc(sc.activation(out=xnbf[b][:, :], in_=xt[b][:, :], func=AF.Identity,
                                 scale=rstd[:, :], bias=lnb[:, :]))
        block.scalar(apply)
        xn_n = ac.n

        if xntm_dram is not None:
            def store_tm(s, t=t, b=b, w=xn_n):
                s.wait_ge(ac.h, w)
                tmst[b].inc(s.dma_start(out=xntm_dram[128 * t:128 * (t + 1), :],
                                        in_=xn32[b][:, :]))
            block.sync(store_tm)
            tm_done[t] = (tmst[b], tmst[b].n)

        def transp(te, t=t, b=b, w=xn_n):
            te.wait_ge(ac.h, w)
            if t >= 1:
                te.wait_ge(dv.h, end[t - 1])
            for c in range(8):
                tr = te.transpose(ptr[:, 128 * c:128 * (c + 1)],
                                  xnbf[b][:, 128 * c:128 * (c + 1)], kb.ident[:, :])
            pe.inc(tr)
        block.tensor(transp)
        tr_n = pe.n

        def evac(v, t=t, sl=sl, w=tr_n):
            v.wait_ge(pe.h, w)
            if t % 4 == 0 and sl >= 2:
                v.wait_ge(fmst[sl % 2].h, fmst[sl % 2].n)
            j = t % 4
            fb = fmbuf[sl % 2]
            r = v.tensor_copy(
                fb[:, :, 128 * j:128 * (j + 1)],
                ptr.ap().rearrange("p (c n) -> p c n", n=128))
            dv.inc(r)
            end[t] = dv.n
        block.vector(evac)

        if t % 4 == 3:
            def store_fm(s, sl=sl, t=t, w=end[t]):
                s.wait_ge(dv.h, w)
                fmst[sl % 2].inc(s.dma_start(
                    out=xnfm_dram[:, :, 512 * sl:512 * (sl + 1)].rearrange("c p n -> p c n"),
                    in_=fmbuf[sl % 2][:, :, :]))
            block.scalar(store_fm)
            fm_done[sl] = (fmst[sl % 2], fmst[sl % 2].n)

    return {"xld": xld, "tmst": tmst, "fmst": fmst,
            "fm_done": fm_done, "tm_done": tm_done}


RSC0 = 1.0 / float(np.sqrt(np.float32(16 + 512 + 512)))


def rope_assemble(kb, v, pA, pB, brep_c, brep_r, cs, aq, qtmp, tmp1, tmp2):
    dv = kb.dv
    v.tensor_tensor(out=aq[:, 0:512], in0=pB[:, 0:512], in1=brep_r[:, 0:512], op=ALU.add)
    v.tensor_tensor(out=aq[:, 512:1024], in0=pB[:, 512:1024], in1=brep_r[:, 512:1024], op=ALU.add)
    v.tensor_tensor(out=qtmp[:, 0:512], in0=pA[:, 0:512], in1=brep_c[:, 0:512], op=ALU.add)
    n = dv.inc(v.tensor_tensor(out=qtmp[:, 512:1024], in0=pA[:, 512:1024], in1=brep_c[:, 512:1024], op=ALU.add))
    v.wait_ge(dv.h, n)
    ae = aq[:, 0:1024:2]
    ao = aq[:, 1:1024:2]
    v.tensor_tensor(out=tmp1[:, :], in0=ae, in1=cs[:, 0:512], op=ALU.mult)
    n = dv.inc(v.tensor_tensor(out=tmp2[:, :], in0=ao, in1=cs[:, 512:1024], op=ALU.mult))
    v.wait_ge(dv.h, n)
    n = dv.inc(v.tensor_tensor(out=tmp1[:, :], in0=tmp1[:, :], in1=tmp2[:, :], op=ALU.subtract))
    v.wait_ge(dv.h, n)
    n = dv.inc(v.tensor_tensor(out=qtmp[:, 0:1024:2], in0=qtmp[:, 0:1024:2], in1=tmp1[:, :], op=ALU.add))
    v.wait_ge(dv.h, n)
    v.tensor_tensor(out=tmp1[:, :], in0=ao, in1=cs[:, 0:512], op=ALU.mult)
    n = dv.inc(v.tensor_tensor(out=tmp2[:, :], in0=ae, in1=cs[:, 512:1024], op=ALU.mult))
    v.wait_ge(dv.h, n)
    n = dv.inc(v.tensor_tensor(out=tmp1[:, :], in0=tmp1[:, :], in1=tmp2[:, :], op=ALU.add))
    v.wait_ge(dv.h, n)
    dv.inc(v.tensor_tensor(out=qtmp[:, 1:1024:2], in0=qtmp[:, 1:1024:2], in1=tmp1[:, :], op=ALU.add))


def ph_proj0(kb, xnfm_dram, cos_dram, sin_dram, W, qfm_dram, kfm_dram, v_dram,
             fm_done=None, gate=None, tag="p0"):
    block = kb.block
    pe, dv, ac = kb.pe, kb.dv, kb.ac

    w_sb = {
        "wdq": kb.sb(f"{tag}_wdq", (128, 8, 512), BF16),
        "wdkv": kb.sb(f"{tag}_wdkv", (128, 8, 512), BF16),
        "wkr": kb.sb(f"{tag}_wkr", (128, 8, 1024), BF16),
        "wuq": kb.sb(f"{tag}_wuq", (128, 4, 1024), BF16),
        "wqr": kb.sb(f"{tag}_wqr", (128, 4, 1024), BF16),
        "wuk": kb.sb(f"{tag}_wuk", (128, 4, 1024), BF16),
        "wuv": kb.sb(f"{tag}_wuv", (128, 4, 1024), BF16),
    }
    b_fm = kb.sb(f"{tag}_bfm", (128, 8), F32)
    brep = {nm: kb.sb(f"{tag}_{nm}", (128, 1024), F32)
            for nm in ["brep_uq", "brep_qr", "brep_uk", "brep_kr", "brep_uv"]}

    wld = kb.dsem(f"{tag}_wld")
    bld = kb.dsem(f"{tag}_bld")

    def loadw(g):
        kb.gate_wait(g, gate)
        for nm in ["wdq", "wdkv", "wkr", "wuq", "wqr", "wuk", "wuv"]:
            wld.inc(g.dma_start(out=w_sb[nm][:, :, :],
                                in_=W[nm].rearrange("(k p) j -> p k j", p=128)))
    block.gpsimd(loadw)
    WLD = wld.n

    def loadb(s):
        kb.gate_wait(s, gate)
        bld.inc(s.dma_start(out=b_fm[:, 0:4], in_=W["bdq"]))
        bld.inc(s.dma_start(out=b_fm[:, 4:8], in_=W["bdkv"]))
        for nm in brep:
            bld.inc(s.dma_start(out=brep[nm][:, :], in_=W[nm]))
    block.sync(loadb)
    BLD = bld.n

    xn_sl = [kb.sb(f"{tag}_xn{i}", (128, 8, 512), BF16) for i in range(2)]
    ctq = kb.sb(f"{tag}_ctq", (128, 4, 512), BF16)
    ctkv = kb.sb(f"{tag}_ctkv", (128, 4, 512), BF16)
    cs_t = [kb.sb(f"{tag}_cs{i}", (128, 1024), F32) for i in range(2)]
    aq = kb.sb(f"{tag}_aq", (128, 1024), F32)
    qtmp = kb.sb(f"{tag}_qtmp", (128, 1024), F32)
    tmp1 = kb.sb(f"{tag}_tmp1", (128, 512), F32)
    tmp2 = kb.sb(f"{tag}_tmp2", (128, 512), F32)
    qbf = kb.sb(f"{tag}_qbf", (128, 1024), BF16)
    qfm_buf = [kb.sb(f"{tag}_qfm{i}", (128, 8, 512), BF16) for i in range(2)]
    kfm_buf = [kb.sb(f"{tag}_kfm{i}", (128, 8, 512), BF16) for i in range(2)]
    vt_buf = [kb.sb(f"{tag}_vt{i}", (128, 1040), BF16) for i in range(2)]
    pCT = kb.ps(f"{tag}_pct", (128, 1024), F32)
    pA = kb.ps(f"{tag}_pa", (128, 1024), F32)
    pB = kb.ps(f"{tag}_pb", (128, 1024), F32)
    pTR = kb.ps(f"{tag}_ptrp", (128, 1024), BF16)

    xld = kb.dsem(f"{tag}_xld", 2)
    csld = kb.dsem(f"{tag}_csld", 2)
    qst = kb.dsem(f"{tag}_qst", 2)
    kst = kb.dsem(f"{tag}_kst", 2)
    vst = kb.dsem(f"{tag}_vst", 2)

    def ones_init(v):
        kb.gate_wait(v, gate)
        for i in range(2):
            dv.inc(v.memset(vt_buf[i].ap().rearrange("p (c n) -> p c n", n=65)[:, :, 64:65], 1.0))
    block.vector(ones_init)

    slice_pe_end = [0] * 8
    qkv_done = [None, None]

    for s in range(8):
        b = s % 2
        seq = s // 4
        scol = 512 * (s % 4)

        def loadx(sy, s=s, b=b):
            if s == 0:
                kb.gate_wait(sy, gate)
            if s >= 2:
                sy.wait_ge(pe.h, slice_pe_end[s - 2])
            if fm_done is not None:
                sy.wait_ge(fm_done[s][0].h, fm_done[s][1])
            xld[b].inc(sy.dma_start(
                out=xn_sl[b][:, :, :],
                in_=xnfm_dram[:, :, 512 * s:512 * (s + 1)].rearrange("c p n -> p c n")))
        block.sync(loadx)
        XL = xld[b].n

        ev_at = []
        for half, (wname, csb, bcol) in enumerate([("wdq", ctq, 0), ("wdkv", ctkv, 4)]):
            for m in range(4):
                idx = half * 4 + m

                def ct_mm(t, wname=wname, m=m, b=b, s=s, idx=idx, w=XL):
                    t.wait_ge(xld[b].h, w)
                    if s == 0 and idx == 0:
                        t.wait_ge(wld.h, WLD)
                    if idx >= 2:
                        t.wait_ge(ac.h, ev_at[idx - 2])
                    elif s >= 1:
                        t.wait_ge(ac.h, ct_prev_ev[idx - 2])
                    for kc in range(8):
                        mm = t.matmul(pCT[:, 512 * (idx % 2):512 * (idx % 2 + 1)],
                                      w_sb[wname][:, kc, 128 * m:128 * (m + 1)],
                                      xn_sl[b][:, kc, :],
                                      start=(kc == 0), stop=(kc == 7))
                    pe.inc(mm)
                block.tensor(ct_mm)
                MYW = pe.n

                def ct_ev(sc, half=half, m=m, csb=csb, bcol=bcol, idx=idx, s=s, w=MYW):
                    sc.wait_ge(pe.h, w)
                    if s == 0 and idx == 0:
                        sc.wait_ge(bld.h, BLD)
                    ac.inc(sc.activation(out=csb[:, m, :],
                                         in_=pCT[:, 512 * (idx % 2):512 * (idx % 2 + 1)],
                                         func=AF.Identity,
                                         bias=b_fm[:, bcol + m:bcol + m + 1], scale=1.0))
                block.scalar(ct_ev)
                ev_at.append(ac.n)
        ct_prev_ev = ev_at
        CT_EV = ac.n

        for t4 in range(4):
            t = 4 * s + t4
            bb = t % 2
            prow = 128 * (t % 16)

            def loadcs(sy, bb=bb, prow=prow, t=t):
                if t < 2:
                    kb.gate_wait(sy, gate)
                if t >= 2:
                    sy.wait_ge(dv.h, cs_done[bb])
                csld[bb].inc(sy.dma_start(out=cs_t[bb][:, 0:512],
                                          in_=cos_dram[prow:prow + 128, :]))
                csld[bb].inc(sy.dma_start(out=cs_t[bb][:, 512:1024],
                                          in_=sin_dram[prow:prow + 128, :]))
            block.sync(loadcs)
            CSL = csld[bb].n

            def q_mm(te, b=b, t4=t4, w=CT_EV, s=s):
                te.wait_ge(ac.h, w)
                te.wait_ge(dv.h, dv.n)
                for nb in range(2):
                    for kc in range(4):
                        mm = te.matmul(pA[:, 512 * nb:512 * (nb + 1)],
                                       ctq[:, kc, 128 * t4:128 * (t4 + 1)],
                                       w_sb["wuq"][:, kc, 512 * nb:512 * (nb + 1)],
                                       start=(kc == 0), stop=(kc == 3))
                    for kc in range(4):
                        mm = te.matmul(pB[:, 512 * nb:512 * (nb + 1)],
                                       ctq[:, kc, 128 * t4:128 * (t4 + 1)],
                                       w_sb["wqr"][:, kc, 512 * nb:512 * (nb + 1)],
                                       start=(kc == 0), stop=(kc == 3))
                pe.inc(mm)
            block.tensor(q_mm)
            QMM = pe.n

            def q_rope(v, bb=bb, w=QMM, wcs=CSL):
                v.wait_ge(pe.h, w)
                v.wait_ge(csld[bb].h, wcs)
                if w == kb._first_qmm:
                    v.wait_ge(bld.h, BLD)
                rope_assemble(kb, v, pA, pB, brep["brep_uq"], brep["brep_qr"],
                              cs_t[bb], aq, qtmp, tmp1, tmp2)
            if s == 0 and t4 == 0:
                kb._first_qmm = QMM
            block.vector(q_rope)
            QAS = dv.n

            def q_cast(sc, w=QAS):
                sc.wait_ge(dv.h, w)
                ac.inc(sc.activation(out=qbf[:, :], in_=qtmp[:, :], func=AF.Identity,
                                     scale=RSC0))
            block.scalar(q_cast)
            QC = ac.n

            def q_tr(te, w=QC):
                te.wait_ge(ac.h, w)
                for c in range(8):
                    tr = te.transpose(pTR[:, 128 * c:128 * (c + 1)],
                                      qbf[:, 128 * c:128 * (c + 1)], kb.ident[:, :])
                pe.inc(tr)
            block.tensor(q_tr)
            QTR = pe.n

            def q_ev(v, w=QTR, t4=t4, sl2=s % 2, s=s):
                v.wait_ge(pe.h, w)
                if s >= 2 and t4 == 0:
                    v.wait_ge(qst[sl2].h, qst[sl2].n)
                dv.inc(v.tensor_copy(qfm_buf[sl2][:, :, 128 * t4:128 * (t4 + 1)],
                                     pTR.ap().rearrange("p (c n) -> p c n", n=128)))
            block.vector(q_ev)
            QEV = dv.n

            def k_mm(te, b=b, t4=t4, w=QEV):
                te.wait_ge(dv.h, w)
                for nb in range(2):
                    for kc in range(4):
                        mm = te.matmul(pA[:, 512 * nb:512 * (nb + 1)],
                                       ctkv[:, kc, 128 * t4:128 * (t4 + 1)],
                                       w_sb["wuk"][:, kc, 512 * nb:512 * (nb + 1)],
                                       start=(kc == 0), stop=(kc == 3))
                    for kc in range(8):
                        mm = te.matmul(pB[:, 512 * nb:512 * (nb + 1)],
                                       xn_sl[b][:, kc, 128 * t4:128 * (t4 + 1)],
                                       w_sb["wkr"][:, kc, 512 * nb:512 * (nb + 1)],
                                       start=(kc == 0), stop=(kc == 7))
                pe.inc(mm)
            block.tensor(k_mm)
            KMM = pe.n

            def k_rope(v, bb=bb, w=KMM):
                v.wait_ge(pe.h, w)
                rope_assemble(kb, v, pA, pB, brep["brep_uk"], brep["brep_kr"],
                              cs_t[bb], aq, qtmp, tmp1, tmp2)
            block.vector(k_rope)
            KAS = dv.n

            def k_cast(sc, w=KAS):
                sc.wait_ge(dv.h, w)
                ac.inc(sc.activation(out=qbf[:, :], in_=qtmp[:, :], func=AF.Identity))
            block.scalar(k_cast)
            KC = ac.n

            def k_tr(te, w=KC):
                te.wait_ge(ac.h, w)
                for c in range(8):
                    tr = te.transpose(pTR[:, 128 * c:128 * (c + 1)],
                                      qbf[:, 128 * c:128 * (c + 1)], kb.ident[:, :])
                pe.inc(tr)
            block.tensor(k_tr)
            KTR = pe.n

            def k_ev(v, w=KTR, t4=t4, sl2=s % 2, s=s):
                v.wait_ge(pe.h, w)
                if s >= 2 and t4 == 0:
                    v.wait_ge(kst[sl2].h, kst[sl2].n)
                dv.inc(v.tensor_copy(kfm_buf[sl2][:, :, 128 * t4:128 * (t4 + 1)],
                                     pTR.ap().rearrange("p (c n) -> p c n", n=128)))
            block.vector(k_ev)
            KEV = dv.n

            def v_mm(te, t4=t4, w=KEV):
                te.wait_ge(dv.h, w)
                for nb in range(2):
                    for kc in range(4):
                        mm = te.matmul(pA[:, 512 * nb:512 * (nb + 1)],
                                       ctkv[:, kc, 128 * t4:128 * (t4 + 1)],
                                       w_sb["wuv"][:, kc, 512 * nb:512 * (nb + 1)],
                                       start=(kc == 0), stop=(kc == 3))
                pe.inc(mm)
            block.tensor(v_mm)
            VMM = pe.n

            def v_ev(v, t=t, bb=bb, w=VMM):
                v.wait_ge(pe.h, w)
                if t >= 2:
                    v.wait_ge(vst[bb].h, vst[bb].n)
                v.tensor_tensor(out=qtmp[:, 0:512], in0=pA[:, 0:512],
                                in1=brep["brep_uv"][:, 0:512], op=ALU.add)
                n = dv.inc(v.tensor_tensor(out=qtmp[:, 512:1024], in0=pA[:, 512:1024],
                                           in1=brep["brep_uv"][:, 512:1024], op=ALU.add))
                v.wait_ge(dv.h, n)
                dv.inc(v.tensor_copy(
                    vt_buf[bb].ap().rearrange("p (c n) -> p c n", n=65)[:, :, 0:64],
                    qtmp.ap().rearrange("p (c n) -> p c n", n=64)))
            block.vector(v_ev)
            VEV = dv.n
            if t < 2:
                cs_done = [0, 0]
            cs_done[bb] = VEV

            def v_st(sy, t=t, bb=bb, w=VEV):
                sy.wait_ge(dv.h, w)
                vst[bb].inc(sy.dma_start(
                    out=v_dram[t // 16, 128 * (t % 16):128 * (t % 16) + 128, :],
                    in_=vt_buf[bb][:, :]))
            block.sync(v_st)

        slice_pe_end[s] = pe.n

        def qk_store(sy, b2=s % 2, seq=seq, scol=scol, w=dv.n):
            sy.wait_ge(dv.h, w)
            qst[b2].inc(sy.dma_start(
                out=qfm_dram[seq, :, :, scol:scol + 512].rearrange("c p n -> p c n"),
                in_=qfm_buf[b2][:, :, :]))
            kst[b2].inc(sy.dma_start(
                out=kfm_dram[seq, :, :, scol:scol + 512].rearrange("c p n -> p c n"),
                in_=kfm_buf[b2][:, :, :]))
        block.sync(qk_store)
        if s % 4 == 3:
            qkv_done[seq] = [(qst[0], qst[0].n), (qst[1], qst[1].n),
                             (kst[0], kst[0].n), (kst[1], kst[1].n),
                             (vst[0], vst[0].n), (vst[1], vst[1].n)]

    return {"qst": qst, "kst": kst, "vst": vst, "xld": xld, "csld": csld,
            "wld": [wld], "bld": [bld], "qkv_done": qkv_done}


def ph_attn(kb, qfm_dram, kfm_dram, v_dram, wo_dram, borep_dram, s_dram,
            resid_dram=None, deps=None, gate=None, tag="a0"):
    block = kb.block
    pe, dv, ac = kb.pe, kb.dv, kb.ac

    wo_sb = kb.sb(f"{tag}_wo", (128, 8, 1024), BF16)
    borep = kb.sb(f"{tag}_borep", (128, 1024), F32)
    ones64 = kb.sb(f"{tag}_ones64", (1, 64), F32)
    q_sb = kb.sb(f"{tag}_q", (128, 8, SEQ), BF16)
    k_sb = kb.sb(f"{tag}_k", (128, 8, SEQ), BF16)
    v_sb = kb.sb(f"{tag}_v", (128, 16, 1040), BF16)
    nm4 = kb.sb(f"{tag}_nm4", (128, 8), F32)
    nmx = kb.sb(f"{tag}_nmx", (128, 2), F32)
    phat = kb.sb(f"{tag}_phat", (128, 2048), BF16)
    ptblk = kb.sb(f"{tag}_ptblk", (128, 16, 512), BF16)
    rs = kb.sb(f"{tag}_rs", (1, 512), F32)
    rep = kb.sb(f"{tag}_rep", (64, 512), F32)
    afm = kb.sb(f"{tag}_afm", (128, 8, 512), BF16)
    stile = [kb.sb(f"{tag}_st{i}", (128, 1024), F32) for i in range(2)]
    rtile = [kb.sb(f"{tag}_rt{i}", (128, 1024), F32) for i in range(2)]
    pS = kb.ps(f"{tag}_pS", (128, 2048), F32)
    pT = kb.ps(f"{tag}_pT", (128, 2048), BF16)
    pO = kb.ps(f"{tag}_pO", (65, 512), F32)
    pR = kb.ps(f"{tag}_pR", (64, 512), F32)

    wld = kb.dsem(f"{tag}_wld")
    qkld = kb.dsem(f"{tag}_qkld")
    rld = kb.dsem(f"{tag}_rld", 2)
    sst = kb.dsem(f"{tag}_sst", 2)

    def loadw(g):
        kb.gate_wait(g, gate)
        wld.inc(g.dma_start(out=wo_sb[:, :, :],
                            in_=wo_dram.rearrange("(k p) j -> p k j", p=128)))
    block.gpsimd(loadw)
    WLD = wld.n

    bsem = kb.dsem(f"{tag}_bld")

    def loadb(sy):
        kb.gate_wait(sy, gate)
        bsem.inc(sy.dma_start(out=borep[:, :], in_=borep_dram))
    block.sync(loadb)
    BSEM = bsem.n

    def ones_init(v):
        kb.gate_wait(v, gate)
        dv.inc(v.memset(ones64[:, :], 1.0))
    block.vector(ones_init)
    ONES = dv.n

    s_done = []
    scb_at = [0, 0]
    prev_sxp = [0]
    prev_tev = [0]
    prev_str = [0]
    for seq in range(2):
        def loadqkv(sy, seq=seq):
            if seq == 0:
                kb.gate_wait(sy, gate)
            if seq >= 1:
                sy.wait_ge(pe.h, pe.n)
            if deps is not None:
                for (p, n) in deps[seq]:
                    sy.wait_ge(p.h, n)
            qkld.inc(sy.dma_start(out=q_sb[:, :, :],
                                  in_=qfm_dram[seq, :, :, :].rearrange("c p n -> p c n")))
            qkld.inc(sy.dma_start(out=k_sb[:, :, :],
                                  in_=kfm_dram[seq, :, :, :].rearrange("c p n -> p c n")))
            qkld.inc(sy.dma_start(out=v_sb[:, :, :],
                                  in_=v_dram[seq, :, :].rearrange("(c p) j -> p c j", p=128)))
        block.sync(loadqkv)
        QKL = qkld.n

        for qb in range(4):
            for h in range(16):
                hp = 64 * (h % 2)
                hc = h // 2
                SXP_at = [0] * 4
                STR_at = [0] * 4
                TEV_at = [0] * 4

                def emit_smm(qt):
                    qcol = 512 * qb + 128 * qt

                    def s_mm(te, hp=hp, hc=hc, qcol=qcol, qt=qt,
                             first=(qb == 0 and h == 0 and qt == 0), w=QKL,
                             wexp=(SXP_at[qt - 1] if qt >= 1 else prev_sxp[0]),
                             wscb=(scb_at[0], scb_at[1])):
                        if first:
                            te.wait_ge(qkld.h, w)
                        if wexp:
                            te.wait_ge(ac.h, wexp)
                        if h == 0 and qt == 0:
                            te.wait_ge(dv.h, max(wscb))
                        for j in range(4):
                            mm = te.matmul(pS[:, 512 * j:512 * (j + 1)],
                                           q_sb[hp:hp + 64, hc, qcol:qcol + 128],
                                           k_sb[hp:hp + 64, hc, 512 * j:512 * (j + 1)],
                                           start=True, stop=True)
                        pe.inc(mm)
                    block.tensor(s_mm)
                    return pe.n

                def emit_max_exp(qt, wsmm):
                    par = qt % 2

                    def s_max(v, w=wsmm, par=par):
                        v.wait_ge(pe.h, w)
                        for j in range(4):
                            r = v.tensor_reduce(out=nm4[:, 4 * par + j:4 * par + j + 1],
                                                in_=pS[:, 512 * j:512 * (j + 1)],
                                                axis=AX.X, op=ALU.max)
                        n = dv.inc(r)
                        v.wait_ge(dv.h, n)
                        dv.inc(v.tensor_reduce(out=nmx[:, par:par + 1],
                                               in_=nm4[:, 4 * par:4 * par + 4], axis=AX.X,
                                               op=ALU.max, negate=True))
                    block.vector(s_max)
                    smx = dv.n

                    def s_exp(sc, w=smx, par=par,
                              wstr=(STR_at[qt - 1] if qt >= 1 else prev_str[0])):
                        sc.wait_ge(dv.h, w)
                        if wstr:
                            sc.wait_ge(pe.h, wstr)
                        for j in range(4):
                            a = sc.activation(out=phat[:, 512 * j:512 * (j + 1)],
                                              in_=pS[:, 512 * j:512 * (j + 1)],
                                              func=AF.Exp, bias=nmx[:, par:par + 1], scale=1.0)
                        ac.inc(a)
                    block.scalar(s_exp)
                    SXP_at[qt] = ac.n

                def emit_tr(qt):
                    def s_tr(te, qt=qt, w=SXP_at[qt],
                             wtev=(TEV_at[qt - 1] if qt >= 1 else prev_tev[0])):
                        te.wait_ge(ac.h, w)
                        if wtev:
                            te.wait_ge(dv.h, wtev)
                        for c in range(16):
                            tr = te.transpose(pT[:, 128 * c:128 * (c + 1)],
                                              phat[:, 128 * c:128 * (c + 1)], kb.ident[:, :])
                        pe.inc(tr)
                    block.tensor(s_tr)
                    if not STR_at[qt]:
                        STR_at[qt] = pe.n
                    assert STR_at[qt] == pe.n

                    def s_tev(v, w=STR_at[qt], qt=qt):
                        v.wait_ge(pe.h, w)
                        v.tensor_copy(ptblk[:, 0:8, 128 * qt:128 * (qt + 1)],
                                      pT.ap().rearrange("p (c n) -> p c n", n=128)[:, 0:8, :])
                        dv.inc(v.tensor_copy(ptblk[:, 8:16, 128 * qt:128 * (qt + 1)],
                                             pT.ap().rearrange("p (c n) -> p c n", n=128)[:, 8:16, :]))
                    block.vector(s_tev)
                    TEV_at[qt] = dv.n

                w0 = emit_smm(0)
                emit_max_exp(0, w0)
                for qt in range(1, 4):
                    w = emit_smm(qt)
                    STR_at[qt - 1] = pe.n + 1
                    emit_max_exp(qt, w)
                    emit_tr(qt - 1)
                    assert pe.n == STR_at[qt - 1]
                emit_tr(3)
                prev_sxp[0] = SXP_at[3]
                prev_tev[0] = TEV_at[3]
                prev_str[0] = STR_at[3]
                TEV = TEV_at[3]

                def pv_mm(te, hp=hp, hc=hc, w=TEV, first=(qb == 0 and h == 0), wo=ONES):
                    te.wait_ge(dv.h, w)
                    if first:
                        te.wait_ge(dv.h, wo)
                    for kc in range(16):
                        mm = te.matmul(pO[:, :], v_sb[:, kc, 65 * h:65 * h + 65],
                                       ptblk[:, kc, :], start=(kc == 0), stop=(kc == 15))
                    pe.inc(mm)
                block.tensor(pv_mm)
                PVM = pe.n

                def pv_recip(v, w=PVM):
                    v.wait_ge(pe.h, w)
                    dv.inc(v.reciprocal(rs[:, :], pO[64:65, :]))
                block.vector(pv_recip)
                RCP = dv.n

                def pv_rep(te, w=RCP):
                    te.wait_ge(dv.h, w)
                    pe.inc(te.matmul(pR[:, :], ones64[:, :], rs[:, :], start=True, stop=True))
                block.tensor(pv_rep)
                REP = pe.n

                def pv_comb(v, w=REP, hp=hp, hc=hc):
                    v.wait_ge(pe.h, w)
                    n = dv.inc(v.tensor_copy(rep[:, :], pR[:, :]))
                    v.wait_ge(dv.h, n)
                    dv.inc(v.tensor_tensor(out=afm[hp:hp + 64, hc, :], in0=pO[0:64, :],
                                           in1=rep[:, :], op=ALU.mult))
                block.vector(pv_comb)
            HEADS_DONE = dv.n

            for qt in range(4):
                t = 16 * seq + 4 * qb + qt
                bb = qt % 2

                def wo_mm(te, qt=qt, w=HEADS_DONE, wld_n=WLD,
                          first=(seq == 0 and qb == 0 and qt == 0),
                          wscb=(scb_at[0], scb_at[1])):
                    te.wait_ge(dv.h, max(w, max(wscb)))
                    if first:
                        te.wait_ge(wld.h, wld_n)
                    for nb in range(2):
                        for kc in range(8):
                            mm = te.matmul(pS[:, 512 * nb:512 * (nb + 1)],
                                           afm[:, kc, 128 * qt:128 * (qt + 1)],
                                           wo_sb[:, kc, 512 * nb:512 * (nb + 1)],
                                           start=(kc == 0), stop=(kc == 7))
                    pe.inc(mm)
                block.tensor(wo_mm)
                WOM = pe.n

                if resid_dram is not None:
                    def loadr(sy, t=t, bb=bb, first=(seq == 0 and qb == 0 and qt < 2)):
                        if first:
                            kb.gate_wait(sy, gate)
                        sy.wait_ge(dv.h, scb_at[bb])
                        rld[bb].inc(sy.dma_start(out=rtile[bb][:, :],
                                                 in_=resid_dram[128 * t:128 * (t + 1), :]))
                    block.sync(loadr)
                RLD = rld[bb].n

                def s_comb(v, bb=bb, w=WOM, wr=RLD, useres=(resid_dram is not None),
                           first=(seq == 0 and qb == 0 and qt < 2)):
                    v.wait_ge(pe.h, w)
                    if first:
                        v.wait_ge(bsem.h, BSEM)
                    v.wait_ge(sst[bb].h, sst[bb].n)
                    v.tensor_tensor(out=stile[bb][:, 0:512], in0=pS[:, 0:512],
                                    in1=borep[:, 0:512], op=ALU.add)
                    n = dv.inc(v.tensor_tensor(out=stile[bb][:, 512:1024], in0=pS[:, 512:1024],
                                               in1=borep[:, 512:1024], op=ALU.add))
                    if useres:
                        v.wait_ge(rld[bb].h, wr)
                        v.wait_ge(dv.h, n)
                        n = dv.inc(v.tensor_tensor(out=stile[bb][:, :], in0=stile[bb][:, :],
                                                   in1=rtile[bb][:, :], op=ALU.add))
                block.vector(s_comb)
                SCB = dv.n
                scb_at[bb] = SCB

                def s_st(sy, t=t, bb=bb, w=SCB):
                    sy.wait_ge(dv.h, w)
                    sst[bb].inc(sy.dma_start(out=s_dram[128 * t:128 * (t + 1), :],
                                             in_=stile[bb][:, :]))
                block.sync(s_st)
                s_done.append((sst[bb], sst[bb].n))

    return {"wld": [wld], "qkld": [qkld], "rld": rld, "sst": sst, "s_done": s_done}


FFN_SC = float(np.float32(1.0) / np.float32(127.0))
FFN_SC2 = float((np.float32(1.0) / np.float32(127.0)) * (np.float32(1.0) / np.float32(127.0)))


def ph_ffn0(kb, xnfm_dram, wg_dram, wu_dram, wd_dram, s_dram, x0_dram, x1_dram,
            fm_done=None, s_done=None, gate=None, tag="f0"):
    block = kb.block
    pe, dv, ac = kb.pe, kb.dv, kb.ac
    SL = 256
    NSL = TOK // SL

    wg = kb.sb(f"{tag}_wg", (128, 8, 3072), BF16)
    wu = kb.sb(f"{tag}_wu", (128, 8, 3072), BF16)
    wd = kb.sb(f"{tag}_wd", (128, 24, 1024), BF16)
    xn_sl = [kb.sb(f"{tag}_xn{i}", (128, 8, SL), BF16) for i in range(2)]
    sg_sb = kb.sb(f"{tag}_sg", (128, SL), BF16)
    h_fm = kb.sb(f"{tag}_h", (128, 24, SL), BF16)
    gt = kb.sb(f"{tag}_gt", (128, 1024), F32)
    st_ = [kb.sb(f"{tag}_s{i}", (128, 1024), F32) for i in range(2)]
    xt_ = [kb.sb(f"{tag}_x{i}", (128, 1024), F32) for i in range(2)]
    x1t = [kb.sb(f"{tag}_x1{i}", (128, 1024), F32) for i in range(2)]
    pG = kb.ps(f"{tag}_pG", (128, 512), F32)
    pU = kb.ps(f"{tag}_pU", (128, 512), F32)
    pD = kb.ps(f"{tag}_pD", (128, 1024), F32)

    wld = kb.dsem(f"{tag}_wld")
    xld = kb.dsem(f"{tag}_xld", 2)
    sld = kb.dsem(f"{tag}_sld", 2)
    x0ld = kb.dsem(f"{tag}_x0ld", 2)
    x1st = kb.dsem(f"{tag}_x1st", 2)

    def loadw(g):
        kb.gate_wait(g, gate)
        for half in range(2):
            wld.inc(g.dma_start(out=wg[:, :, 1536 * half:1536 * (half + 1)],
                                in_=wg_dram.rearrange("(k p) j -> p k j", p=128)[:, :, 1536 * half:1536 * (half + 1)]))
            wld.inc(g.dma_start(out=wu[:, :, 1536 * half:1536 * (half + 1)],
                                in_=wu_dram.rearrange("(k p) j -> p k j", p=128)[:, :, 1536 * half:1536 * (half + 1)]))
        wld.inc(g.dma_start(out=wd[:, :, :],
                            in_=wd_dram.rearrange("(k p) j -> p k j", p=128)))
    block.gpsimd(loadw)
    WLD = wld.n

    x1_done = []
    slice_pe_end = [0] * NSL
    scb = [0, 0]
    for s in range(NSL):
        b = s % 2

        def loadx(sy, s=s, b=b):
            if s == 0:
                kb.gate_wait(sy, gate)
            if s >= 2:
                sy.wait_ge(pe.h, slice_pe_end[s - 2])
            if fm_done is not None:
                sy.wait_ge(fm_done[s // 2][0].h, fm_done[s // 2][1])
            xld[b].inc(sy.dma_start(
                out=xn_sl[b][:, :, :],
                in_=xnfm_dram[:, :, SL * s:SL * (s + 1)].rearrange("c p n -> p c n")))
        block.sync(loadx)
        XL = xld[b].n

        for m in range(24):
            def gu_mm(te, m=m, b=b, s=s, w=XL, first=(s == 0 and m == 0)):
                te.wait_ge(xld[b].h, w)
                if first:
                    te.wait_ge(wld.h, WLD)
                te.wait_ge(ac.h, ac.n)
                te.wait_ge(dv.h, dv.n)
                for kc in range(8):
                    mm = te.matmul(pG[:, 0:SL], wg[:, kc, 128 * m:128 * (m + 1)],
                                   xn_sl[b][:, kc, :], start=(kc == 0), stop=(kc == 7))
                for kc in range(8):
                    mm = te.matmul(pU[:, 0:SL], wu[:, kc, 128 * m:128 * (m + 1)],
                                   xn_sl[b][:, kc, :], start=(kc == 0), stop=(kc == 7))
                pe.inc(mm)
            block.tensor(gu_mm)
            GU = pe.n

            def silu_g(sc, w=GU):
                sc.wait_ge(pe.h, w)
                ac.inc(sc.activation(out=sg_sb[:, :], in_=pG[:, 0:SL], func=AF.Silu,
                                     scale=FFN_SC))
            block.scalar(silu_g)
            SG = ac.n

            def h_mul(v, m=m, w=SG):
                v.wait_ge(ac.h, w)
                dv.inc(v.tensor_tensor(out=h_fm[:, m, :], in0=pU[:, 0:SL],
                                       in1=sg_sb[:, :], op=ALU.mult))
            block.vector(h_mul)
        H_DONE = dv.n

        for t2 in range(SL // 128):
            t = (SL // 128) * s + t2
            bb = t % 2

            def d_mm(te, t2=t2, w=H_DONE):
                te.wait_ge(dv.h, w)
                te.wait_ge(ac.h, ac.n)
                for nb in range(2):
                    for kc in range(24):
                        mm = te.matmul(pD[:, 512 * nb:512 * (nb + 1)],
                                       h_fm[:, kc, 128 * t2:128 * (t2 + 1)],
                                       wd[:, kc, 512 * nb:512 * (nb + 1)],
                                       start=(kc == 0), stop=(kc == 23))
                pe.inc(mm)
            block.tensor(d_mm)
            DM_ = pe.n

            def loadsx(sy, t=t, bb=bb):
                sy.wait_ge(dv.h, scb[bb])
                if s_done is not None and t < 2:
                    for (p, n) in s_done:
                        sy.wait_ge(p.h, n)
                sld[bb].inc(sy.dma_start(out=st_[bb][:, :],
                                         in_=s_dram[128 * t:128 * (t + 1), :]))
                x0ld[bb].inc(sy.dma_start(out=xt_[bb][:, :],
                                          in_=x0_dram[128 * t:128 * (t + 1), :]))
            block.sync(loadsx)
            SXL = (sld[bb].n, x0ld[bb].n)

            def g_silu(sc, w=DM_):
                sc.wait_ge(pe.h, w)
                sc.activation(out=gt[:, 0:512], in_=pD[:, 0:512], func=AF.Silu,
                              scale=FFN_SC2)
                ac.inc(sc.activation(out=gt[:, 512:1024], in_=pD[:, 512:1024],
                                     func=AF.Silu, scale=FFN_SC2))
            block.scalar(g_silu)
            GS = ac.n

            def comb(v, bb=bb, w=GS, wsx=SXL):
                v.wait_ge(ac.h, w)
                v.wait_ge(sld[bb].h, wsx[0])
                v.wait_ge(x0ld[bb].h, wsx[1])
                v.wait_ge(x1st[bb].h, x1st[bb].n)
                n = dv.inc(v.tensor_tensor(out=x1t[bb][:, :], in0=gt[:, :],
                                           in1=st_[bb][:, :], op=ALU.mult))
                v.wait_ge(dv.h, n)
                dv.inc(v.tensor_tensor(out=x1t[bb][:, :], in0=x1t[bb][:, :],
                                       in1=xt_[bb][:, :], op=ALU.add))
            block.vector(comb)
            CB = dv.n
            scb[bb] = CB

            def x1_store(sy, t=t, bb=bb, w=CB):
                sy.wait_ge(dv.h, w)
                x1st[bb].inc(sy.dma_start(out=x1_dram[128 * t:128 * (t + 1), :],
                                          in_=x1t[bb][:, :]))
            block.scalar(x1_store)
            x1_done.append((x1st[bb], x1st[bb].n))
        slice_pe_end[s] = pe.n

    return {"wld": [wld], "xld": xld, "sld": sld, "x0ld": x0ld, "x1st": x1st,
            "x1_done": x1_done}


def ph_gn(kb, xnfm_dram, gind_dram, e2_dram, gcols_dram, bcols_dram,
          A_gn, B_gn, fm_done=None, gate=None, tag="gn"):
    block = kb.block
    pe, dv, ac = kb.pe, kb.dv, kb.ac

    xc = [kb.sb(f"{tag}_xc{i}", (128, 2048), BF16) for i in range(2)]
    sums = kb.sb(f"{tag}_sums", (128, 16), F32)
    sqs = kb.sb(f"{tag}_sqs", (128, 16), F32)
    dump = kb.sb(f"{tag}_dump", (128, 2048), F32)
    gind = kb.sb(f"{tag}_gind", (128, 4), F32)
    e2 = kb.sb(f"{tag}_e2", (4, 128), F32)
    gcols = kb.sb(f"{tag}_gcols", (128, 8), F32)
    bcols = kb.sb(f"{tag}_bcols", (128, 8), F32)
    grp = kb.sb(f"{tag}_grp", (4, 16), F32)
    grpq = kb.sb(f"{tag}_grpq", (4, 16), F32)
    mean_g = kb.sb(f"{tag}_meang", (4, 16), F32)
    var_g = kb.sb(f"{tag}_varg", (4, 16), F32)
    rstd_g = kb.sb(f"{tag}_rstdg", (4, 16), F32)
    mexp = kb.sb(f"{tag}_mexp", (128, 16), F32)
    rexp = kb.sb(f"{tag}_rexp", (128, 16), F32)
    pG = kb.ps(f"{tag}_pG", (4, 512), F32)
    pE = kb.ps(f"{tag}_pE", (128, 512), F32)

    cld = kb.dsem(f"{tag}_cld")
    xldg = kb.dsem(f"{tag}_xldg", 2)

    def loadc(sy):
        kb.gate_wait(sy, gate)
        cld.inc(sy.dma_start(out=gind[:, :], in_=gind_dram))
        cld.inc(sy.dma_start(out=e2[:, :], in_=e2_dram))
        cld.inc(sy.dma_start(out=gcols[:, :], in_=gcols_dram))
        cld.inc(sy.dma_start(out=bcols[:, :], in_=bcols_dram))
    block.sync(loadc)
    CLD = cld.n

    red_at = [0] * 16
    for idx in range(16):
        seq, c = idx // 8, idx % 8
        b = idx % 2

        def loadx(sy, seq=seq, c=c, b=b, idx=idx):
            if idx < 2:
                kb.gate_wait(sy, gate)
            if idx >= 2:
                sy.wait_ge(dv.h, red_at[idx - 2])
            if fm_done is not None:
                sy.wait_ge(fm_done[-1][0].h, fm_done[-1][1])
                sy.wait_ge(fm_done[-2][0].h, fm_done[-2][1])
            xldg[b].inc(sy.dma_start(out=xc[b][:, :],
                                     in_=xnfm_dram[c, :, SEQ * seq:SEQ * (seq + 1)]))
        block.sync(loadx)
        XL = xldg[b].n

        def red(v, idx=idx, b=b, w=XL):
            v.wait_ge(xldg[b].h, w)
            if idx >= 1:
                v.wait_ge(dv.h, red_at[idx - 1])
            v.tensor_reduce(out=sums[:, idx:idx + 1], in_=xc[b][:, :], axis=AX.X, op=ALU.add)
            dv.inc(v.scalar_tensor_tensor(out=dump[:, :], in0=xc[b][:, :], scalar=1.0,
                                          in1=xc[b][:, :], op0=ALU.mult, op1=ALU.mult,
                                          accum_out=sqs[:, idx:idx + 1]))
            red_at[idx] = dv.n
        block.vector(red)
    RED = dv.n

    def grp_mm(te, w=RED, wc=CLD):
        te.wait_ge(dv.h, w)
        te.wait_ge(cld.h, wc)
        te.matmul(pG[:, 0:16], gind[:, :], sums[:, :], start=True, stop=True)
        mm = te.matmul(pG[:, 16:32], gind[:, :], sqs[:, :], start=True, stop=True)
        pe.inc(mm)
    block.tensor(grp_mm)
    GMM = pe.n

    def grp_stats(v, w=GMM):
        v.wait_ge(pe.h, w)
        inv = 1.0 / (SEQ * 32.0)
        v.tensor_scalar(out=mean_g[:, :], in0=pG[:, 0:16], scalar1=inv, scalar2=None, op0=ALU.mult)
        n = dv.inc(v.tensor_scalar(out=var_g[:, :], in0=pG[:, 16:32], scalar1=inv, scalar2=None, op0=ALU.mult))
        v.wait_ge(dv.h, n)
        n = dv.inc(v.tensor_tensor(out=grpq[:, :], in0=mean_g[:, :], in1=mean_g[:, :], op=ALU.mult))
        v.wait_ge(dv.h, n)
        dv.inc(v.tensor_tensor(out=var_g[:, :], in0=var_g[:, :], in1=grpq[:, :], op=ALU.subtract))
    block.vector(grp_stats)
    VARN = dv.n

    def grp_rstd(sc, w=VARN):
        sc.wait_ge(dv.h, w)
        a = ac.inc(sc.activation(out=rstd_g[:, :], in_=var_g[:, :], func=AF.Ln,
                                 bias=kb.epsgn[0:4, :], scale=1.0))
        sc.wait_ge(ac.h, a)
        ac.inc(sc.activation(out=rstd_g[:, :], in_=rstd_g[:, :], func=AF.Exp, scale=-0.5))
    block.scalar(grp_rstd)
    RSTD = ac.n

    def exp_mm(te, w=RSTD):
        te.wait_ge(ac.h, w)
        te.matmul(pE[:, 0:16], e2[:, :], rstd_g[:, :], start=True, stop=True)
        mm = te.matmul(pE[:, 16:32], e2[:, :], mean_g[:, :], start=True, stop=True)
        pe.inc(mm)
    block.tensor(exp_mm)
    EMM = pe.n

    def ab_final(v, w=EMM):
        v.wait_ge(pe.h, w)
        v.tensor_copy(rexp[:, :], pE[:, 0:16])
        n = dv.inc(v.tensor_copy(mexp[:, :], pE[:, 16:32]))
        v.wait_ge(dv.h, n)
        for seq in range(2):
            n = dv.inc(v.tensor_tensor(out=A_gn[:, 8 * seq:8 * seq + 8],
                                       in0=rexp[:, 8 * seq:8 * seq + 8],
                                       in1=gcols[:, :], op=ALU.mult))
        v.wait_ge(dv.h, n)
        for seq in range(2):
            n = dv.inc(v.tensor_tensor(out=B_gn[:, 8 * seq:8 * seq + 8],
                                       in0=mexp[:, 8 * seq:8 * seq + 8],
                                       in1=A_gn[:, 8 * seq:8 * seq + 8], op=ALU.mult))
        v.wait_ge(dv.h, n)
        for seq in range(2):
            n = dv.inc(v.tensor_tensor(out=B_gn[:, 8 * seq:8 * seq + 8],
                                       in0=bcols[:, :],
                                       in1=B_gn[:, 8 * seq:8 * seq + 8], op=ALU.subtract))
    block.vector(ab_final)
    AB = dv.n

    return {"cld": [cld], "xldg": xldg, "ab_done": AB}


def ph_proj1(kb, xnfm_dram, w_dram, bq_dram, bk_dram, bvrep_dram, A_gn, B_gn,
             qfm_dram, kfm_dram, v_dram, fm_done=None, ab_done=0, gate=None, tag="p1"):
    block = kb.block
    pe, dv, ac = kb.pe, kb.dv, kb.ac

    w1 = kb.sb(f"{tag}_w1", (128, 8, 3072), BF16)
    bq = kb.sb(f"{tag}_bq", (128, 8), F32)
    bk = kb.sb(f"{tag}_bk", (128, 8), F32)
    bvrep = kb.sb(f"{tag}_bvrep", (128, 1024), F32)
    xn_sl = [kb.sb(f"{tag}_xn{i}", (128, 8, 512), BF16) for i in range(2)]
    gn_sl = kb.sb(f"{tag}_gn", (128, 8, 512), BF16)
    qfm_buf = [kb.sb(f"{tag}_qfm{i}", (128, 8, 512), BF16) for i in range(2)]
    kfm_buf = [kb.sb(f"{tag}_kfm{i}", (128, 8, 512), BF16) for i in range(2)]
    vt_buf = [kb.sb(f"{tag}_vt{i}", (128, 1040), BF16) for i in range(2)]
    vtmp = kb.sb(f"{tag}_vtmp", (128, 1024), F32)
    pQ = kb.ps(f"{tag}_pQ", (128, 1024), F32)
    pV = kb.ps(f"{tag}_pV", (128, 1024), F32)

    wld = kb.dsem(f"{tag}_wld")
    bld = kb.dsem(f"{tag}_bld")
    xld = kb.dsem(f"{tag}_xld", 2)
    qst = kb.dsem(f"{tag}_qst", 2)
    kst = kb.dsem(f"{tag}_kst", 2)
    vst = kb.dsem(f"{tag}_vst", 2)

    def loadw(g):
        kb.gate_wait(g, gate)
        for half in range(2):
            wld.inc(g.dma_start(out=w1[:, :, 1536 * half:1536 * (half + 1)],
                                in_=w_dram.rearrange("(k p) j -> p k j", p=128)[:, :, 1536 * half:1536 * (half + 1)]))
    block.gpsimd(loadw)
    WLD = wld.n

    def loadb(sy):
        kb.gate_wait(sy, gate)
        bld.inc(sy.dma_start(out=bq[:, :], in_=bq_dram))
        bld.inc(sy.dma_start(out=bk[:, :], in_=bk_dram))
        bld.inc(sy.dma_start(out=bvrep[:, :], in_=bvrep_dram))
    block.sync(loadb)
    BLD = bld.n

    def bscale(v, w=BLD):
        v.wait_ge(bld.h, w)
        dv.inc(v.tensor_scalar(out=bq[:, :], in0=bq[:, :], scalar1=0.125, scalar2=None,
                               op0=ALU.mult))
    block.vector(bscale)
    BSC = dv.n

    def ones_init(v):
        kb.gate_wait(v, gate)
        for i in range(2):
            dv.inc(v.memset(vt_buf[i].ap().rearrange("p (c n) -> p c n", n=65)[:, :, 64:65], 1.0))
    block.vector(ones_init)

    qkv_done = [None, None]
    slice_pe_end = [0] * 8
    for s in range(8):
        b = s % 2
        seq = s // 4
        scol = 512 * (s % 4)

        def loadx(sy, s=s, b=b):
            if s == 0:
                kb.gate_wait(sy, gate)
            if s >= 2:
                sy.wait_ge(pe.h, slice_pe_end[s - 2])
            if fm_done is not None:
                sy.wait_ge(fm_done[s][0].h, fm_done[s][1])
            xld[b].inc(sy.dma_start(
                out=xn_sl[b][:, :, :],
                in_=xnfm_dram[:, :, 512 * s:512 * (s + 1)].rearrange("c p n -> p c n")))
        block.sync(loadx)
        XL = xld[b].n

        def gn_apply(sc, s=s, b=b, seq=seq, w=XL, wab=ab_done):
            sc.wait_ge(xld[b].h, w)
            if s == 0:
                sc.wait_ge(dv.h, wab)
            if s >= 1:
                sc.wait_ge(pe.h, pe.n)
            for c in range(8):
                a = sc.activation(out=gn_sl[:, c, :], in_=xn_sl[b][:, c, :],
                                  func=AF.Identity, scale=A_gn[:, 8 * seq + c:8 * seq + c + 1],
                                  bias=B_gn[:, 8 * seq + c:8 * seq + c + 1])
            ac.inc(a)
        block.scalar(gn_apply)
        GN = ac.n

        for which, (nm_, buf, coff, sca, stq) in enumerate(
                [("q", qfm_buf, 0, 0.125, qst), ("k", kfm_buf, 1024, 1.0, kst)]):
            for m in range(8):
                def qk_mm(te, b=b, m=m, coff=coff, w=GN, first=(s == 0 and which == 0 and m == 0), which=which):
                    te.wait_ge(ac.h, w)
                    if first:
                        te.wait_ge(wld.h, WLD)
                    te.wait_ge(ac.h, ac.n)
                    for kc in range(8):
                        mm = te.matmul(pQ[:, 512 * (m % 2):512 * (m % 2 + 1)],
                                       w1[:, kc, coff + 128 * m:coff + 128 * (m + 1)],
                                       gn_sl[:, kc, :], start=(kc == 0), stop=(kc == 7))
                    pe.inc(mm)
                block.tensor(qk_mm)
                QKM = pe.n

                def qk_ev(sc, m=m, which=which, buf=buf, sca=sca, sl2=s % 2, s=s, w=QKM,
                          wb=BSC, stq=stq):
                    sc.wait_ge(pe.h, w)
                    if s == 0 and m == 0 and which == 0:
                        sc.wait_ge(dv.h, wb)
                    if s >= 2 and m == 0:
                        sc.wait_ge(stq[sl2].h, stq[sl2].n)
                    bb2 = bq if which == 0 else bk
                    ac.inc(sc.activation(out=buf[sl2][:, m, :],
                                         in_=pQ[:, 512 * (m % 2):512 * (m % 2 + 1)],
                                         func=AF.Identity, scale=sca,
                                         bias=bb2[:, m:m + 1]))
                block.scalar(qk_ev)
        QKE = ac.n

        for t4 in range(4):
            t = 4 * s + t4
            bb = t % 2

            def v_mm(te, t4=t4, w=QKE):
                te.wait_ge(ac.h, w)
                te.wait_ge(dv.h, dv.n)
                for nb in range(2):
                    for kc in range(8):
                        mm = te.matmul(pV[:, 512 * nb:512 * (nb + 1)],
                                       gn_sl[:, kc, 128 * t4:128 * (t4 + 1)],
                                       w1[:, kc, 2048 + 512 * nb:2048 + 512 * (nb + 1)],
                                       start=(kc == 0), stop=(kc == 7))
                pe.inc(mm)
            block.tensor(v_mm)
            VMM = pe.n

            def v_ev(v, t=t, bb=bb, w=VMM):
                v.wait_ge(pe.h, w)
                if t >= 2:
                    v.wait_ge(vst[bb].h, vst[bb].n)
                v.tensor_tensor(out=vtmp[:, 0:512], in0=pV[:, 0:512],
                                in1=bvrep[:, 0:512], op=ALU.add)
                n = dv.inc(v.tensor_tensor(out=vtmp[:, 512:1024], in0=pV[:, 512:1024],
                                           in1=bvrep[:, 512:1024], op=ALU.add))
                v.wait_ge(dv.h, n)
                dv.inc(v.tensor_copy(
                    vt_buf[bb].ap().rearrange("p (c n) -> p c n", n=65)[:, :, 0:64],
                    vtmp.ap().rearrange("p (c n) -> p c n", n=64)))
            block.vector(v_ev)
            VEV = dv.n

            def v_st(sy, t=t, bb=bb, w=VEV):
                sy.wait_ge(dv.h, w)
                vst[bb].inc(sy.dma_start(
                    out=v_dram[t // 16, 128 * (t % 16):128 * (t % 16) + 128, :],
                    in_=vt_buf[bb][:, :]))
            block.sync(v_st)

        slice_pe_end[s] = pe.n

        def qk_store(sy, b2=s % 2, seq=seq, scol=scol, w=QKE):
            sy.wait_ge(ac.h, w)
            qst[b2].inc(sy.dma_start(
                out=qfm_dram[seq, :, :, scol:scol + 512].rearrange("c p n -> p c n"),
                in_=qfm_buf[b2][:, :, :]))
            kst[b2].inc(sy.dma_start(
                out=kfm_dram[seq, :, :, scol:scol + 512].rearrange("c p n -> p c n"),
                in_=kfm_buf[b2][:, :, :]))
        block.sync(qk_store)
        if s % 4 == 3:
            qkv_done[seq] = [(qst[0], qst[0].n), (qst[1], qst[1].n),
                             (kst[0], kst[0].n), (kst[1], kst[1].n),
                             (vst[0], vst[0].n), (vst[1], vst[1].n)]

    return {"wld": [wld], "bld": [bld], "xld": xld, "qst": qst, "kst": kst,
            "vst": vst, "qkv_done": qkv_done}


def ph_proj2(kb, xnfm_dram, w_dram, qgrep_dram, qfm_dram, kfm_dram, v_dram,
             simall_dram, minmax_dram, fm_done=None, gate=None, tag="p2"):
    block = kb.block
    pe, dv, ac = kb.pe, kb.dv, kb.ac

    w2 = kb.sb(f"{tag}_w2", (128, 8, 3072), BF16)
    qgrep = kb.sb(f"{tag}_qgrep", (128, 1024), F32)
    inv_rep = kb.sb(f"{tag}_invrep", (128, 16), F32)
    xn_sl = [kb.sb(f"{tag}_xn{i}", (128, 8, 512), BF16) for i in range(2)]
    qf32 = kb.sb(f"{tag}_qf32", (128, 1024), F32)
    qbf = kb.sb(f"{tag}_qbf", (128, 1024), BF16)
    dump = kb.sb(f"{tag}_dump", (128, 1024), F32)
    dh16 = kb.sb(f"{tag}_dh16", (128, 16), F32)
    sq16 = kb.sb(f"{tag}_sq16", (128, 16), F32)
    rstdq = kb.sb(f"{tag}_rstdq", (128, 16), F32)
    sim_all = kb.sb(f"{tag}_simall", (128, 32), F32)
    qfm_buf = [kb.sb(f"{tag}_qfm{i}", (128, 8, 512), BF16) for i in range(2)]
    kfm_buf = [kb.sb(f"{tag}_kfm{i}", (128, 8, 512), BF16) for i in range(2)]
    vt_buf = [kb.sb(f"{tag}_vt{i}", (128, 1040), BF16) for i in range(2)]
    sbuf2 = kb.sb(f"{tag}_sbuf2", (128, 32), F32)
    sbuf3 = kb.sb(f"{tag}_sbuf3", (128, 32), F32)
    mm2 = kb.sb(f"{tag}_mm2", (128, 2), F32)
    mmt = kb.sb(f"{tag}_mmt", (2, 128), F32)
    mmv = kb.sb(f"{tag}_mmv", (2, 1), F32)
    mml = kb.sb(f"{tag}_mml", (1, 2), F32)
    tiny = kb.sb(f"{tag}_tiny", (1, 16), F32)
    pQ2 = kb.ps(f"{tag}_pQ2", (128, 1024), F32)
    pK2 = kb.ps(f"{tag}_pK2", (128, 1024), F32)
    pTR = kb.ps(f"{tag}_pTR", (128, 1024), BF16)
    pMM = kb.ps(f"{tag}_pMM", (128, 128), F32)

    wld = kb.dsem(f"{tag}_wld")
    cld = kb.dsem(f"{tag}_cld")
    xld = kb.dsem(f"{tag}_xld", 2)
    qst = kb.dsem(f"{tag}_qst", 2)
    kst = kb.dsem(f"{tag}_kst", 2)
    vst = kb.dsem(f"{tag}_vst", 2)
    mst = kb.dsem(f"{tag}_mst")

    def loadw(g):
        kb.gate_wait(g, gate)
        for half in range(2):
            wld.inc(g.dma_start(out=w2[:, :, 1536 * half:1536 * (half + 1)],
                                in_=w_dram.rearrange("(k p) j -> p k j", p=128)[:, :, 1536 * half:1536 * (half + 1)]))
    block.gpsimd(loadw)
    WLD = wld.n

    def loadc(sy):
        kb.gate_wait(sy, gate)
        cld.inc(sy.dma_start(out=qgrep[:, :], in_=qgrep_dram))
    block.sync(loadc)
    CLD = cld.n

    def ones_init2(v):
        kb.gate_wait(v, gate)
        for i in range(2):
            dv.inc(v.memset(vt_buf[i].ap().rearrange("p (c n) -> p c n", n=65)[:, :, 64:65], 1.0))
    block.vector(ones_init2)

    def qg_prep_v(v, w=CLD):
        v.wait_ge(cld.h, w)
        dv.inc(v.tensor_tensor(out=dump[0:1, :], in0=qgrep[0:1, :], in1=qgrep[0:1, :],
                               op=ALU.mult))
    block.vector(qg_prep_v)
    QG1 = dv.n

    def qg_red(v, w=QG1):
        v.wait_ge(dv.h, w)
        dv.inc(v.tensor_reduce(out=tiny[:, :],
                               in_=dump.ap().rearrange("p (h d) -> p h d", d=64)[0:1, :, :],
                               axis=AX.X, op=ALU.add))
    block.vector(qg_red)
    QG2 = dv.n

    def qg_rstd(sc, w=QG2):
        sc.wait_ge(dv.h, w)
        a = ac.inc(sc.activation(out=tiny[:, :], in_=tiny[:, :], func=AF.Ln,
                                 bias=kb.epsz[0:1, :], scale=1.0))
        sc.wait_ge(ac.h, a)
        ac.inc(sc.activation(out=tiny[:, :], in_=tiny[:, :], func=AF.Exp, scale=-0.5))
    block.scalar(qg_rstd)
    QG3 = ac.n

    def qg_scale(v, w=QG3):
        v.wait_ge(ac.h, w)
        dv.inc(v.tensor_scalar(out=tiny[:, :], in0=tiny[:, :], scalar1=1.0 / 16.0,
                               scalar2=None, op0=ALU.mult))
    block.vector(qg_scale)
    QG4 = dv.n

    def qg_repmm(te, w=QG4):
        te.wait_ge(dv.h, w)
        pe.inc(te.matmul(pMM[:, 0:16], kb.ones128[:, :], tiny[:, :], start=True, stop=True))
    block.tensor(qg_repmm)
    QG5 = pe.n

    def qg_repev(v, w=QG5):
        v.wait_ge(pe.h, w)
        dv.inc(v.tensor_copy(inv_rep[:, :], pMM[:, 0:16]))
    block.vector(qg_repev)
    INVR = dv.n

    qkv_done = [None, None]
    sim_done_at = 0
    slice_pe_end = [0] * 8
    for s in range(8):
        b = s % 2
        seq = s // 4
        scol = 512 * (s % 4)

        def loadx(sy, s=s, b=b):
            if s == 0:
                kb.gate_wait(sy, gate)
            if s >= 2:
                sy.wait_ge(pe.h, slice_pe_end[s - 2])
            if fm_done is not None:
                sy.wait_ge(fm_done[s][0].h, fm_done[s][1])
            xld[b].inc(sy.dma_start(
                out=xn_sl[b][:, :, :],
                in_=xnfm_dram[:, :, 512 * s:512 * (s + 1)].rearrange("c p n -> p c n")))
        block.sync(loadx)
        XL = xld[b].n

        for m in range(8):
            def k_mm(te, b=b, m=m, w=XL, first=(s == 0 and m == 0)):
                te.wait_ge(xld[b].h, w)
                if first:
                    te.wait_ge(wld.h, WLD)
                te.wait_ge(ac.h, ac.n)
                for kc in range(8):
                    mm = te.matmul(pK2[:, 512 * (m % 2):512 * (m % 2 + 1)],
                                   w2[:, kc, 1024 + 128 * m:1024 + 128 * (m + 1)],
                                   xn_sl[b][:, kc, :], start=(kc == 0), stop=(kc == 7))
                pe.inc(mm)
            block.tensor(k_mm)
            KM = pe.n

            def k_ev(sc, m=m, sl2=s % 2, s=s, w=KM):
                sc.wait_ge(pe.h, w)
                if s >= 2 and m == 0:
                    sc.wait_ge(kst[sl2].h, kst[sl2].n)
                ac.inc(sc.copy(kfm_buf[sl2][:, m, :],
                               pK2[:, 512 * (m % 2):512 * (m % 2 + 1)]))
            block.scalar(k_ev)
        KQE = ac.n

        for t4 in range(4):
            t = 4 * s + t4
            bb = t % 2

            def q_mm(te, b=b, t4=t4, w=KQE):
                te.wait_ge(ac.h, w)
                te.wait_ge(ac.h, ac.n)
                te.wait_ge(dv.h, dv.n)
                for nb in range(2):
                    for kc in range(8):
                        mm = te.matmul(pQ2[:, 512 * nb:512 * (nb + 1)],
                                       xn_sl[b][:, kc, 128 * t4:128 * (t4 + 1)],
                                       w2[:, kc, 512 * nb:512 * (nb + 1)],
                                       start=(kc == 0), stop=(kc == 7))
                pe.inc(mm)
            block.tensor(q_mm)
            QM = pe.n

            def q_cast(sc, w=QM):
                sc.wait_ge(pe.h, w)
                sc.copy(qf32[:, 0:512], pQ2[:, 0:512])
                sc.copy(qf32[:, 512:1024], pQ2[:, 512:1024])
                sc.mul(qbf[:, 0:512], pQ2[:, 0:512], 0.125)
                ac.inc(sc.mul(qbf[:, 512:1024], pQ2[:, 512:1024], 0.125))
            block.scalar(q_cast)
            QC = ac.n

            def q_tr(te, w=QC):
                te.wait_ge(ac.h, w)
                te.wait_ge(dv.h, dv.n)
                for c in range(8):
                    tr = te.transpose(pTR[:, 128 * c:128 * (c + 1)],
                                      qbf[:, 128 * c:128 * (c + 1)], kb.ident[:, :])
                pe.inc(tr)
            block.tensor(q_tr)
            QTR = pe.n

            def q_ev(v, w=QTR, t4=t4, sl2=s % 2, s=s):
                v.wait_ge(pe.h, w)
                if s >= 2 and t4 == 0:
                    v.wait_ge(qst[sl2].h, qst[sl2].n)
                dv.inc(v.tensor_copy(qfm_buf[sl2][:, :, 128 * t4:128 * (t4 + 1)],
                                     pTR.ap().rearrange("p (c n) -> p c n", n=128)))
            block.vector(q_ev)

            def sim_v(v, t=t, w=QC, winv=INVR):
                v.wait_ge(ac.h, w)
                if t == 0:
                    v.wait_ge(dv.h, winv)
                n = dv.inc(v.tensor_tensor(out=dump[:, :], in0=qf32[:, :], in1=qgrep[:, :],
                                           op=ALU.mult))
                v.wait_ge(dv.h, n)
                n = dv.inc(v.tensor_reduce(out=dh16[:, :],
                                           in_=dump.ap().rearrange("p (h d) -> p h d", d=64),
                                           axis=AX.X, op=ALU.add))
                v.wait_ge(dv.h, n)
                n = dv.inc(v.tensor_tensor(out=dump[:, :], in0=qf32[:, :], in1=qf32[:, :],
                                           op=ALU.mult))
                v.wait_ge(dv.h, n)
                dv.inc(v.tensor_reduce(out=sq16[:, :],
                                        in_=dump.ap().rearrange("p (h d) -> p h d", d=64),
                                        axis=AX.X, op=ALU.add))
            block.vector(sim_v)
            SV = dv.n

            def sim_rstd(sc, w=SV):
                sc.wait_ge(dv.h, w)
                a = ac.inc(sc.activation(out=rstdq[:, :], in_=sq16[:, :], func=AF.Ln,
                                         bias=kb.epsz[:, :], scale=1.0))
                sc.wait_ge(ac.h, a)
                ac.inc(sc.activation(out=rstdq[:, :], in_=rstdq[:, :], func=AF.Exp,
                                     scale=-0.5))
            block.scalar(sim_rstd)
            SR = ac.n

            def sim_fin(v, t=t, w=SR):
                v.wait_ge(ac.h, w)
                n = dv.inc(v.tensor_tensor(out=dh16[:, :], in0=dh16[:, :], in1=inv_rep[:, :],
                                           op=ALU.mult))
                v.wait_ge(dv.h, n)
                n = dv.inc(v.tensor_tensor(out=dh16[:, :], in0=dh16[:, :], in1=rstdq[:, :],
                                           op=ALU.mult))
                v.wait_ge(dv.h, n)
                dv.inc(v.tensor_reduce(out=sim_all[:, t:t + 1], in_=dh16[:, :],
                                       axis=AX.X, op=ALU.add))
            block.vector(sim_fin)

            def v_mm(te, b=b, t4=t4):
                te.wait_ge(dv.h, dv.n)
                for nb in range(2):
                    for kc in range(8):
                        mm = te.matmul(pQ2[:, 512 * nb:512 * (nb + 1)],
                                       xn_sl[b][:, kc, 128 * t4:128 * (t4 + 1)],
                                       w2[:, kc, 2048 + 512 * nb:2048 + 512 * (nb + 1)],
                                       start=(kc == 0), stop=(kc == 7))
                pe.inc(mm)
            block.tensor(v_mm)
            VM = pe.n

            def v_ev(v, t=t, bb=bb, w=VM):
                v.wait_ge(pe.h, w)
                if t >= 2:
                    v.wait_ge(vst[bb].h, vst[bb].n)
                v.tensor_copy(
                    vt_buf[bb].ap().rearrange("p (c n) -> p c n", n=65)[:, 0:8, 0:64],
                    pQ2.ap().rearrange("p (c n) -> p c n", n=64)[:, 0:8, :])
                dv.inc(v.tensor_copy(
                    vt_buf[bb].ap().rearrange("p (c n) -> p c n", n=65)[:, 8:16, 0:64],
                    pQ2.ap().rearrange("p (c n) -> p c n", n=64)[:, 8:16, :]))
            block.vector(v_ev)
            VEV = dv.n

            def v_st(sy, t=t, bb=bb, w=VEV):
                sy.wait_ge(dv.h, w)
                vst[bb].inc(sy.dma_start(
                    out=v_dram[t // 16, 128 * (t % 16):128 * (t % 16) + 128, :],
                    in_=vt_buf[bb][:, :]))
            block.sync(v_st)

        slice_pe_end[s] = pe.n

        def qk_store(sy, b2=s % 2, seq=seq, scol=scol, w=dv.n):
            sy.wait_ge(dv.h, w)
            qst[b2].inc(sy.dma_start(
                out=qfm_dram[seq, :, :, scol:scol + 512].rearrange("c p n -> p c n"),
                in_=qfm_buf[b2][:, :, :]))
            kst[b2].inc(sy.dma_start(
                out=kfm_dram[seq, :, :, scol:scol + 512].rearrange("c p n -> p c n"),
                in_=kfm_buf[b2][:, :, :]))
        block.sync(qk_store)
        if s % 4 == 3:
            qkv_done[seq] = [(qst[0], qst[0].n), (qst[1], qst[1].n),
                             (kst[0], kst[0].n), (kst[1], kst[1].n),
                             (vst[0], vst[0].n), (vst[1], vst[1].n)]

    SIM_ALL = dv.n

    def mm_bufs(v, w=SIM_ALL):
        v.wait_ge(dv.h, w)
        v.tensor_copy(sbuf2[:, :], sim_all[:, :])
        n = dv.inc(v.tensor_scalar(out=sbuf3[:, :], in0=sim_all[:, :], scalar1=-1.0,
                                   scalar2=None, op0=ALU.mult))
        v.wait_ge(dv.h, n)
        v.memset(sbuf2[0:5, 0:1], -1e30)
        v.memset(sbuf2[0:5, 16:17], -1e30)
        v.memset(sbuf3[0:5, 0:1], -1e30)
        n = dv.inc(v.memset(sbuf3[0:5, 16:17], -1e30))
        v.wait_ge(dv.h, n)
        v.tensor_reduce(out=mm2[:, 0:1], in_=sbuf2[:, :], axis=AX.X, op=ALU.max)
        dv.inc(v.tensor_reduce(out=mm2[:, 1:2], in_=sbuf3[:, :], axis=AX.X, op=ALU.max))
    block.vector(mm_bufs)
    MMB = dv.n

    def mm_tr(te, w=MMB):
        te.wait_ge(dv.h, w)
        te.wait_ge(dv.h, dv.n)
        pe.inc(te.transpose(pMM[0:2, 0:128], mm2[:, :], kb.identf[:, :]))
    block.tensor(mm_tr)
    MMT = pe.n

    def mm_red(v, w=MMT):
        v.wait_ge(pe.h, w)
        n = dv.inc(v.tensor_copy(mmt[:, :], pMM[0:2, 0:128]))
        v.wait_ge(dv.h, n)
        dv.inc(v.tensor_reduce(out=mmv[:, :], in_=mmt[:, :], axis=AX.X, op=ALU.max))
    block.vector(mm_red)
    MMR = dv.n

    def mm_tr2(te, w=MMR):
        te.wait_ge(dv.h, w)
        pe.inc(te.transpose(pMM[0:1, 0:2], mmv[:, :], kb.identf[0:2, 0:2]))
    block.tensor(mm_tr2)
    MMT2 = pe.n

    def mm_fin(v, w=MMT2):
        v.wait_ge(pe.h, w)
        dv.inc(v.tensor_copy(mml[:, :], pMM[0:1, 0:2]))
    block.vector(mm_fin)
    MMF = dv.n

    def mm_st(sy, w=MMF):
        sy.wait_ge(dv.h, w)
        mst.inc(sy.dma_start(out=minmax_dram, in_=mml[:, :]))
        mst.inc(sy.dma_start(out=simall_dram, in_=sim_all[:, :]))
    block.sync(mm_st)

    return {"wld": [wld], "cld": [cld], "xld": xld, "qst": qst, "kst": kst,
            "vst": vst, "mst": [mst], "qkv_done": qkv_done}


def ph_maskattn(kb, allmm_dram, simall_dram, qfm_dram, kfm_dram, v_dram,
                c09_dram, wflat_dram, wproj_dram, bprep_dram, x1_dram, s1_dram,
                out_dram, deps=None, x1s1_done=None, gate=None, tag="mf"):
    block = kb.block
    pe, dv, ac, po = kb.pe, kb.dv, kb.ac, kb.po

    mask_pos = kb.sb(f"{tag}_mpos", (128, 32), F32)
    mask_neg = kb.sb(f"{tag}_mneg", (128, 32), F32)
    w1m = kb.sb(f"{tag}_w1m", (128, 32), F32)
    w2m = kb.sb(f"{tag}_w2m", (128, 32), F32)
    wtb = kb.sb(f"{tag}_wtb", (32, 128), BF16)
    wf1 = kb.sb(f"{tag}_wf1", (1, 512), BF16)
    wf2 = kb.sb(f"{tag}_wf2", (1, 512), BF16)
    thr_rep = kb.sb(f"{tag}_thrrep", (128, 1), F32)
    simall = kb.sb(f"{tag}_simall", (128, 32), F32)
    allmm = kb.sb(f"{tag}_allmm", (8, 2), F32)
    c09 = kb.sb(f"{tag}_c09", (1, 2), F32)
    tiny2 = kb.sb(f"{tag}_tiny2", (2, 8), F32)
    tiny3 = kb.sb(f"{tag}_tiny3", (2, 1), F32)
    thr1 = kb.sb(f"{tag}_thr1", (1, 2), F32)
    thr0 = kb.sb(f"{tag}_thr0", (1, 1), F32)

    wproj_sb = kb.sb(f"{tag}_wproj", (128, 8, 1024), BF16)
    bprep = kb.sb(f"{tag}_bprep", (128, 1024), F32)
    q_sb = kb.sb(f"{tag}_q", (128, 8, SEQ), BF16)
    k_sb = kb.sb(f"{tag}_k", (128, 8, SEQ), BF16)
    v_sb = kb.sb(f"{tag}_v", (128, 16, 1040), BF16)
    vpos = kb.sb(f"{tag}_vpos", (128, 16, 1040), BF16)
    nm4 = kb.sb(f"{tag}_nm4", (128, 8), F32)
    nmx = kb.sb(f"{tag}_nmx", (128, 2), F32)
    phat = kb.sb(f"{tag}_phat", (128, 2048), BF16)
    ptblk = kb.sb(f"{tag}_ptblk", (128, 16, 512), BF16)
    c1 = kb.sb(f"{tag}_c1", (1, 512), F32)[0:1, :]
    c2 = kb.sb(f"{tag}_c2", (1, 512), F32)[0:1, :]
    c1rep = kb.sb(f"{tag}_c1rep", (64, 512), F32)
    c2rep = kb.sb(f"{tag}_c2rep", (64, 512), F32)
    ot1 = kb.sb(f"{tag}_ot1", (128, 512), F32)
    afm = kb.sb(f"{tag}_afm", (128, 8, 512), BF16)
    gsil = kb.sb(f"{tag}_gsil", (128, 1024), F32)
    x1t = [kb.sb(f"{tag}_x1t0", (128, 1024), F32)] * 2
    s1t = [kb.sb(f"{tag}_s1t0", (128, 1024), F32)] * 2
    outt = [kb.sb(f"{tag}_outt0", (128, 1024), F32)] * 2

    pS = kb.ps(f"{tag}_pS", (128, 2048), F32)
    pT = kb.ps(f"{tag}_pT", (128, 1024), BF16)
    pOp = kb.ps(f"{tag}_pOp", (65, 512), F32)
    pOn = kb.ps(f"{tag}_pOn", (65, 512), F32)
    pR = kb.ps(f"{tag}_pR", (64, 512), F32)

    mld = kb.dsem(f"{tag}_mld")
    wld = kb.dsem(f"{tag}_wld")
    wfb = kb.dsem(f"{tag}_wfb")
    wfl = kb.dsem(f"{tag}_wfl")
    qkld = kb.dsem(f"{tag}_qkld")
    xsld = kb.dsem(f"{tag}_xsld", 2)
    ost = kb.dsem(f"{tag}_ost", 2)

    def loadm(sy):
        kb.gate_wait(sy, gate)
        if deps is not None:
            for (p, n) in deps:
                sy.wait_ge(p.h, n)
        mld.inc(sy.dma_start(out=allmm[:, :], in_=allmm_dram))
        mld.inc(sy.dma_start(out=simall[:, :], in_=simall_dram))
        mld.inc(sy.dma_start(out=c09[:, :], in_=c09_dram))
    block.sync(loadm)
    MLD = mld.n

    def loadw(g):
        kb.gate_wait(g, gate)
        wld.inc(g.dma_start(out=wproj_sb[:, :, :],
                            in_=wproj_dram.rearrange("(k p) j -> p k j", p=128)))
    block.gpsimd(loadw)
    WLD = wld.n

    bpl = kb.dsem(f"{tag}_bpl")

    def loadbp(sy):
        kb.gate_wait(sy, gate)
        bpl.inc(sy.dma_start(out=bprep[:, :], in_=bprep_dram))
    block.sync(loadbp)
    MLD2 = bpl.n

    def thr_tr(te, w=MLD):
        te.wait_ge(mld.h, w)
        kb.gate_wait(te, gate)
        pe.inc(te.transpose(pS[0:2, 0:8], allmm[:, :], kb.identf[0:8, 0:8]))
    block.tensor(thr_tr)
    T1 = pe.n

    def thr_red(v, w=T1):
        v.wait_ge(pe.h, w)
        kb.gate_wait(v, gate)
        n = dv.inc(v.tensor_copy(tiny2[:, :], pS[0:2, 0:8]))
        v.wait_ge(dv.h, n)
        dv.inc(v.tensor_reduce(out=tiny3[:, :], in_=tiny2[:, :], axis=AX.X, op=ALU.max))
    block.vector(thr_red)
    T2 = dv.n

    def thr_tr2(te, w=T2):
        te.wait_ge(dv.h, w)
        pe.inc(te.transpose(pS[0:1, 0:2], tiny3[:, :], kb.identf[0:2, 0:2]))
    block.tensor(thr_tr2)
    T3 = pe.n

    def thr_fin(v, w=T3):
        v.wait_ge(pe.h, w)
        n = dv.inc(v.tensor_copy(thr1[:, :], pS[0:1, 0:2]))
        v.wait_ge(dv.h, n)
        n = dv.inc(v.tensor_tensor(out=thr1[:, :], in0=thr1[:, :], in1=c09[:, :],
                                   op=ALU.mult))
        v.wait_ge(dv.h, n)
        dv.inc(v.tensor_reduce(out=thr0[:, :], in_=thr1[:, :], axis=AX.X, op=ALU.add))
    block.vector(thr_fin)
    T4 = dv.n

    def thr_repmm(te, w=T4):
        te.wait_ge(dv.h, w)
        pe.inc(te.matmul(pS[:, 0:1], kb.ones128[:, :], thr0[:, :], start=True, stop=True))
    block.tensor(thr_repmm)
    T5 = pe.n

    def masks(v, w=T5):
        v.wait_ge(pe.h, w)
        n = dv.inc(v.tensor_copy(thr_rep[:, :], pS[:, 0:1]))
        v.wait_ge(dv.h, n)
        n = dv.inc(v.tensor_scalar(out=mask_pos[:, :], in0=simall[:, :],
                                   scalar1=thr_rep[:, :], scalar2=None, op0=ALU.is_gt))
        v.wait_ge(dv.h, n)
        n = dv.inc(v.tensor_scalar(out=mask_neg[:, :], in0=mask_pos[:, :],
                                   scalar1=-1.0, scalar2=1.0, op0=ALU.mult, op1=ALU.add))
        v.wait_ge(dv.h, n)
        v.tensor_copy(w1m[:, :], mask_pos[:, :])
        n = dv.inc(v.tensor_copy(w2m[:, :], mask_neg[:, :]))
        v.wait_ge(dv.h, n)
        for col in (0, 16):
            v.memset(mask_pos[0:5, col:col + 1], 1.0)
            v.memset(mask_neg[0:5, col:col + 1], 1.0)
            v.memset(w1m[0:5, col:col + 1], 0.5)
            n = dv.inc(v.memset(w2m[0:5, col:col + 1], 0.5))
    block.vector(masks)
    MSK = dv.n

    wf_re_at = 0
    for wi, wm in enumerate([w1m, w2m]):
        def w_tr(te, wm=wm, wi=wi, w=MSK):
            te.wait_ge(dv.h, w if wi == 0 else dv.n)
            pe.inc(te.transpose(pS[0:32, 0:128], wm[:, :], kb.identf[:, :]))
        block.tensor(w_tr)
        WT = pe.n

        def w_ev(v, w=WT, wi=wi):
            v.wait_ge(pe.h, w)
            if wi == 1:
                v.wait_ge(wfb.h, wfb.n)
            dv.inc(v.tensor_copy(wtb[:, :], pS[0:32, 0:128]))
        block.vector(w_ev)
        WE = dv.n

        def w_st(sy, wi=wi, w=WE):
            sy.wait_ge(dv.h, w)
            wfb.inc(sy.dma_start(out=wflat_dram[wi, :, :], in_=wtb[:, :]))
        block.sync(w_st)
        WS = wfb.n

    WFB = wfb.n

    out_done = []
    oscb = [0, 0]
    prev_sxp = [0]
    prev_str = [0]
    tev_prev = [0]
    fad_last = [0]
    for seq in range(2):
        def loadqkv(sy, seq=seq):
            if seq == 0:
                kb.gate_wait(sy, gate)
            if seq >= 1:
                sy.wait_ge(pe.h, pe.n)
                sy.wait_ge(dv.h, dv.n)
            if deps is not None:
                for (p, n) in deps:
                    sy.wait_ge(p.h, n)
            qkld.inc(sy.dma_start(out=q_sb[:, :, :],
                                  in_=qfm_dram[seq, :, :, :].rearrange("c p n -> p c n")))
            qkld.inc(sy.dma_start(out=k_sb[:, :, :],
                                  in_=kfm_dram[seq, :, :, :].rearrange("c p n -> p c n")))
            qkld.inc(sy.dma_start(out=v_sb[:, :, :],
                                  in_=v_dram[seq, :, :].rearrange("(c p) j -> p c j", p=128)))
        block.sync(loadqkv)
        QKL = qkld.n

        def v_masks(v, seq=seq, w=QKL, wm=MSK):
            v.wait_ge(qkld.h, w)
            v.wait_ge(dv.h, wm)
            for c in range(16):
                n = dv.inc(v.tensor_scalar(out=vpos[:, c, :], in0=v_sb[:, c, :],
                                           scalar1=mask_pos[:, 16 * seq + c:16 * seq + c + 1],
                                           scalar2=None, op0=ALU.mult))
            v.wait_ge(dv.h, n)
            for c in range(16):
                n = dv.inc(v.tensor_tensor(out=v_sb[:, c, :], in0=v_sb[:, c, :],
                                           in1=vpos[:, c, :], op=ALU.subtract))
            v.wait_ge(dv.h, n)
            dv.inc(v.tensor_copy(v_sb[0:5, 0, :], vpos[0:5, 0, :]))
        block.vector(v_masks)
        VMK = dv.n

        for qb in range(4):
            def wf_load(sy, seq=seq, qb=qb, w=WFB):
                sy.wait_ge(wfb.h, w)
                sy.wait_ge(dv.h, dv.n)
                base = 128 * (16 * seq + 4 * qb)
                wfl.inc(sy.dma_start(
                    out=wf1[0:1, :],
                    in_=wflat_dram[0, base // 128:base // 128 + 4, :].rearrange("a b -> (a b)")))
                wfl.inc(sy.dma_start(
                    out=wf2[0:1, :],
                    in_=wflat_dram[1, base // 128:base // 128 + 4, :].rearrange("a b -> (a b)")))
            block.sync(wf_load)
            WFL = wfl.n

            for h in range(16):
                hp = 64 * (h % 2)
                hc = h // 2
                SXP_at = [0] * 4
                STR_at = [0] * 4
                TEV_at = [0] * 4

                def emit_smm(qt):
                    qcol = 512 * qb + 128 * qt

                    def s_mm(te, hp=hp, hc=hc, qcol=qcol, w=VMK,
                             first=(qb == 0 and h == 0 and qt == 0),
                             wexp=(SXP_at[qt - 1] if qt >= 1 else prev_sxp[0]),
                             wfad=fad_last[0]):
                        if first:
                            te.wait_ge(dv.h, w)
                        if wexp:
                            te.wait_ge(ac.h, wexp)
                        if h == 0 and qt == 0 and wfad:
                            te.wait_ge(dv.h, wfad)
                        for j in range(4):
                            mm = te.matmul(pS[:, 512 * j:512 * (j + 1)],
                                           q_sb[hp:hp + 64, hc, qcol:qcol + 128],
                                           k_sb[hp:hp + 64, hc, 512 * j:512 * (j + 1)],
                                           start=True, stop=True)
                        pe.inc(mm)
                    block.tensor(s_mm)
                    return pe.n

                def emit_max_exp(qt, wsmm):
                    par = qt % 2

                    def s_max(v, w=wsmm, par=par):
                        v.wait_ge(pe.h, w)
                        for j in range(4):
                            r = v.tensor_reduce(out=nm4[:, 4 * par + j:4 * par + j + 1],
                                                in_=pS[:, 512 * j:512 * (j + 1)],
                                                axis=AX.X, op=ALU.max)
                        n = dv.inc(r)
                        v.wait_ge(dv.h, n)
                        dv.inc(v.tensor_reduce(out=nmx[:, par:par + 1],
                                               in_=nm4[:, 4 * par:4 * par + 4], axis=AX.X,
                                               op=ALU.max, negate=True))
                    block.vector(s_max)
                    smx = dv.n

                    def s_exp(sc, w=smx, par=par,
                              wstr=(STR_at[qt - 1] if qt >= 1 else prev_str[0])):
                        sc.wait_ge(dv.h, w)
                        if wstr:
                            sc.wait_ge(pe.h, wstr)
                        for j in range(4):
                            a = sc.activation(out=phat[:, 512 * j:512 * (j + 1)],
                                              in_=pS[:, 512 * j:512 * (j + 1)],
                                              func=AF.Exp, bias=nmx[:, par:par + 1], scale=1.0)
                        ac.inc(a)
                    block.scalar(s_exp)
                    SXP_at[qt] = ac.n

                def emit_tr(qt):
                    for rr in range(2):
                        def s_tr(te, w=SXP_at[qt], rr=rr, wtev=tev_prev[0]):
                            te.wait_ge(ac.h, w)
                            if wtev:
                                te.wait_ge(dv.h, wtev)
                            for c in range(8):
                                tr = te.transpose(pT[:, 128 * c:128 * (c + 1)],
                                                  phat[:, 128 * (8 * rr + c):128 * (8 * rr + c + 1)],
                                                  kb.ident[:, :])
                            pe.inc(tr)
                        block.tensor(s_tr)
                        STR = pe.n

                        def s_tev(v, w=STR, qt=qt, rr=rr):
                            v.wait_ge(pe.h, w)
                            dv.inc(v.tensor_copy(
                                ptblk[:, 8 * rr:8 * rr + 8, 128 * qt:128 * (qt + 1)],
                                pT.ap().rearrange("p (c n) -> p c n", n=128)[:, 0:8, :]))
                        block.vector(s_tev)
                        tev_prev[0] = dv.n
                    if not STR_at[qt]:
                        STR_at[qt] = pe.n
                    TEV_at[qt] = dv.n

                w0 = emit_smm(0)
                emit_max_exp(0, w0)
                for qt in range(1, 4):
                    w = emit_smm(qt)
                    STR_at[qt - 1] = pe.n + 2
                    emit_max_exp(qt, w)
                    emit_tr(qt - 1)
                    assert pe.n == STR_at[qt - 1]
                emit_tr(3)
                prev_sxp[0] = SXP_at[3]
                prev_str[0] = STR_at[3]
                TEV = TEV_at[3]

                def pv_mm(te, h=h, w=TEV):
                    te.wait_ge(dv.h, w)
                    for kc in range(16):
                        te.matmul(pOp[:, :], vpos[:, kc, 65 * h:65 * h + 65],
                                  ptblk[:, kc, :], start=(kc == 0), stop=(kc == 15))
                    for kc in range(16):
                        mm = te.matmul(pOn[:, :], v_sb[:, kc, 65 * h:65 * h + 65],
                                       ptblk[:, kc, :], start=(kc == 0), stop=(kc == 15))
                    pe.inc(mm)
                block.tensor(pv_mm)
                PVM = pe.n

                def pv_recip(v, w=PVM, wfl_n=WFL, first=(h == 0)):
                    v.wait_ge(pe.h, w)
                    if first:
                        v.wait_ge(wfl.h, wfl_n)
                    v.reciprocal(c1, pOp[64:65, :])
                    n = dv.inc(v.reciprocal(c2, pOn[64:65, :]))
                    v.wait_ge(dv.h, n)
                    v.tensor_tensor(out=c1, in0=c1, in1=wf1[0:1, :], op=ALU.mult)
                    dv.inc(v.tensor_tensor(out=c2, in0=c2, in1=wf2[0:1, :], op=ALU.mult))
                block.vector(pv_recip)
                RCP = dv.n

                def pv_rep1(te, w=RCP):
                    te.wait_ge(dv.h, w)
                    pe.inc(te.matmul(pR[:, :], kb.ones128[0:1, 0:64], c1,
                                     start=True, stop=True))
                block.tensor(pv_rep1)
                REP1 = pe.n

                def pv_rev1(v, w=REP1):
                    v.wait_ge(pe.h, w)
                    dv.inc(v.tensor_copy(c1rep[:, :], pR[:, :]))
                block.vector(pv_rev1)
                REV1 = dv.n

                def pv_rep2(te, w=REV1):
                    te.wait_ge(dv.h, w)
                    pe.inc(te.matmul(pR[:, :], kb.ones128[0:1, 0:64], c2,
                                     start=True, stop=True))
                block.tensor(pv_rep2)
                REP2 = pe.n

                def pv_comb(v, w=REP2, hp=hp, hc=hc):
                    v.wait_ge(pe.h, w)
                    n0 = dv.inc(v.tensor_copy(c2rep[:, :], pR[:, :]))
                    v.wait_ge(dv.h, n0)
                    v.tensor_tensor(out=ot1[hp:hp + 64, :], in0=pOp[0:64, :],
                                    in1=c1rep[:, :], op=ALU.mult)
                    n = dv.inc(v.tensor_tensor(out=afm[hp:hp + 64, hc, :], in0=pOn[0:64, :],
                                               in1=c2rep[:, :], op=ALU.mult))
                    v.wait_ge(dv.h, n)
                    dv.inc(v.tensor_tensor(out=afm[hp:hp + 64, hc, :],
                                           in0=afm[hp:hp + 64, hc, :],
                                           in1=ot1[hp:hp + 64, :], op=ALU.add))
                block.vector(pv_comb)
            HEADS_DONE = dv.n

            for qt in range(4):
                t = 16 * seq + 4 * qb + qt
                bb = qt % 2

                def wp_mm(te, qt=qt, w=HEADS_DONE, wld_n=WLD,
                          first=(seq == 0 and qb == 0 and qt == 0),
                          wfad=fad_last[0]):
                    te.wait_ge(dv.h, max(w, wfad))
                    if first:
                        te.wait_ge(wld.h, wld_n)
                    for nb in range(2):
                        for kc in range(8):
                            mm = te.matmul(pS[:, 512 * nb:512 * (nb + 1)],
                                           afm[:, kc, 128 * qt:128 * (qt + 1)],
                                           wproj_sb[:, kc, 512 * nb:512 * (nb + 1)],
                                           start=(kc == 0), stop=(kc == 7))
                    pe.inc(mm)
                block.tensor(wp_mm)
                WPM = pe.n

                def loadxs(sy, t=t, bb=bb, first=(seq == 0 and qb == 0 and qt == 0)):
                    if first:
                        kb.gate_wait(sy, gate)
                        if x1s1_done is not None:
                            for (p, n) in x1s1_done:
                                sy.wait_ge(p.h, n)
                    sy.wait_ge(dv.h, max(oscb))
                    xsld[bb].inc(sy.dma_start(out=x1t[bb][:, :],
                                              in_=x1_dram[128 * t:128 * (t + 1), :]))
                    xsld[bb].inc(sy.dma_start(out=s1t[bb][:, :],
                                              in_=s1_dram[128 * t:128 * (t + 1), :]))
                block.sync(loadxs)
                XSL = xsld[bb].n

                def f_add(v, w=WPM, wb=MLD2):
                    v.wait_ge(pe.h, w)
                    v.wait_ge(bpl.h, wb)
                    v.tensor_tensor(out=gsil[:, 0:512], in0=pS[:, 0:512],
                                    in1=bprep[:, 0:512], op=ALU.add)
                    dv.inc(v.tensor_tensor(out=gsil[:, 512:1024], in0=pS[:, 512:1024],
                                           in1=bprep[:, 512:1024], op=ALU.add))
                block.vector(f_add)
                FAD = dv.n
                fad_last[0] = FAD

                def g_silu(sc, w=FAD):
                    sc.wait_ge(dv.h, w)
                    ac.inc(sc.activation(out=gsil[:, :], in_=gsil[:, :], func=AF.Silu))
                block.scalar(g_silu)
                GS = ac.n

                def comb(v, bb=bb, w=GS, wxs=XSL):
                    v.wait_ge(ac.h, w)
                    v.wait_ge(xsld[bb].h, wxs)
                    v.wait_ge(ost[0].h, ost[0].n)
                    v.wait_ge(ost[1].h, ost[1].n)
                    n = dv.inc(v.tensor_tensor(out=outt[bb][:, :], in0=gsil[:, :],
                                               in1=s1t[bb][:, :], op=ALU.mult))
                    v.wait_ge(dv.h, n)
                    dv.inc(v.tensor_tensor(out=outt[bb][:, :], in0=outt[bb][:, :],
                                           in1=x1t[bb][:, :], op=ALU.add))
                block.vector(comb)
                CB = dv.n
                oscb[bb] = CB

                def o_st(sy, t=t, bb=bb, w=CB):
                    sy.wait_ge(dv.h, w)
                    ost[bb].inc(sy.dma_start(out=out_dram[128 * t:128 * (t + 1), :],
                                             in_=outt[bb][:, :]))
                block.sync(o_st)
                out_done.append((ost[bb], ost[bb].n))

    return {"mld": [mld], "wld": [wld], "bpl": [bpl], "wfb": [wfb], "wfl": [wfl],
            "qkld": [qkld], "xsld": xsld, "ost": ost, "out_done": out_done}


def build_l1(dev=()):
    nc = bass.Bass("TRN2", target_bir_lowering=False, debug=False)

    def dram(name, shape, dtype):
        kind = "ExternalOutput" if name in dev else "Internal"
        return nc.dram_tensor(name, list(shape), dtype, kind=kind).ap()

    x0 = nc.dram_tensor("x0", [TOK, DM], F32, kind="ExternalInput").ap()
    ident_d = nc.dram_tensor("c_ident", [128, 128], BF16, kind="ExternalInput").ap()
    identf_d = nc.dram_tensor("c_identf", [128, 128], F32, kind="ExternalInput").ap()
    ones_d = nc.dram_tensor("c_ones", [1, 128], F32, kind="ExternalInput").ap()
    cos_d = nc.dram_tensor("c_cos", [SEQ, DM // 2], F32, kind="ExternalInput").ap()
    sin_d = nc.dram_tensor("c_sin", [SEQ, DM // 2], F32, kind="ExternalInput").ap()

    W0 = {}
    for nm, shp in [("wdq", (1024, 512)), ("wdkv", (1024, 512)), ("wkr", (1024, 1024)),
                    ("wuq", (512, 1024)), ("wqr", (512, 1024)), ("wuk", (512, 1024)),
                    ("wuv", (512, 1024)), ("bdq", (128, 4)), ("bdkv", (128, 4)),
                    ("brep_uq", (128, 1024)), ("brep_qr", (128, 1024)),
                    ("brep_uk", (128, 1024)), ("brep_kr", (128, 1024)),
                    ("brep_uv", (128, 1024))]:
        W0[nm] = nc.dram_tensor("w0_" + nm, list(shp), F32, kind="ExternalInput").ap()

    wo0 = nc.dram_tensor("w0_wo", [1024, 1024], F32, kind="ExternalInput").ap()
    borep0 = nc.dram_tensor("w0_brep_o", [128, 1024], F32, kind="ExternalInput").ap()

    xnfm0 = dram("xnfm0", (8, 128, TOK), BF16)
    xntm0 = dram("xntm0", (TOK, DM), F32) if "xntm0" in dev else None
    qfm0 = dram("qfm0", (2, 8, 128, SEQ), BF16)
    kfm0 = dram("kfm0", (2, 8, 128, SEQ), BF16)
    v0 = dram("v0", (2, SEQ, 1040), BF16)
    s0 = dram("s0", (TOK, DM), F32)
    wgT = nc.dram_tensor("w0_wgT", [1024, 3072], I32, kind="ExternalInput").ap()
    wuT = nc.dram_tensor("w0_wuT", [1024, 3072], I32, kind="ExternalInput").ap()
    wdT = nc.dram_tensor("w0_wdT", [3072, 1024], I32, kind="ExternalInput").ap()
    x1 = dram("x1", (TOK, DM), F32)
    gind_d = nc.dram_tensor("c_gind", [128, 4], F32, kind="ExternalInput").ap()
    e2_d = nc.dram_tensor("c_e2", [4, 128], F32, kind="ExternalInput").ap()
    gcols_d = nc.dram_tensor("w1_gcols", [128, 8], F32, kind="ExternalInput").ap()
    bcols_d = nc.dram_tensor("w1_bcols", [128, 8], F32, kind="ExternalInput").ap()
    xnfm1 = dram("xnfm1", (8, 128, TOK), BF16)
    xn1tm = dram("xn1tm", (TOK, DM), F32)
    abdev = dram("abdev", (128, 32), F32) if "abdev" in dev else None
    w1qkv = nc.dram_tensor("w1_qkv", [1024, 3072], F32, kind="ExternalInput").ap()
    w1bq = nc.dram_tensor("w1_bq", [128, 8], F32, kind="ExternalInput").ap()
    w1bk = nc.dram_tensor("w1_bk", [128, 8], F32, kind="ExternalInput").ap()
    w1bvrep = nc.dram_tensor("w1_bvrep", [128, 1024], F32, kind="ExternalInput").ap()
    w1o = nc.dram_tensor("w1_wo", [1024, 1024], F32, kind="ExternalInput").ap()
    w1borep = nc.dram_tensor("w1_brep_o", [128, 1024], F32, kind="ExternalInput").ap()
    qfm1 = dram("qfm1", (2, 8, 128, SEQ), BF16)
    kfm1 = dram("kfm1", (2, 8, 128, SEQ), BF16)
    v1 = dram("v1", (2, SEQ, 1040), BF16)
    s1 = dram("s1", (TOK, DM), F32)
    w2qkv = nc.dram_tensor("w2_qkv", [1024, 3072], F32, kind="ExternalInput").ap()
    qgrep_d = nc.dram_tensor("w2_qgrep", [128, 1024], F32, kind="ExternalInput").ap()
    qfm2 = dram("qfm2", (2, 8, 128, SEQ), BF16)
    kfm2 = dram("kfm2", (2, 8, 128, SEQ), BF16)
    v2 = dram("v2", (2, SEQ, 1040), BF16)
    simall = dram("simall", (128, 32), F32)
    minmax = dram("minmax", (1, 2), F32)
    allmm_d = dram("allmm", (8, 2), F32)
    c09_d = nc.dram_tensor("c_c09", [1, 2], F32, kind="ExternalInput").ap()
    wproj_d = nc.dram_tensor("w2_proj", [1024, 1024], F32, kind="ExternalInput").ap()
    bprep_d = nc.dram_tensor("w2_bprep", [128, 1024], F32, kind="ExternalInput").ap()
    wflat = dram("wflat", (2, 32, 128), BF16)
    out_d = nc.dram_tensor("outp", [TOK, DM], F32, kind="ExternalOutput").ap()

    with ExitStack() as ctx:
        block = ctx.enter_context(nc.Block())
        kb = KB(nc, ctx, block)
        kb.ident = kb.sb("identsb", (128, 128), BF16)
        kb.identf = kb.sb("identfsb", (128, 128), F32)
        kb.ones128 = kb.sb("ones128sb", (1, 128), F32)
        kb.epsln = kb.sb("epsln", (128, 1), F32)
        kb.epsgn = kb.sb("epsgn", (128, 1), F32)
        kb.epsz = kb.sb("epsz", (128, 1), F32)
        identld = kb.dsem("identld")

        def load_const(s):
            identld.inc(s.dma_start(out=kb.ident[:, :], in_=ident_d))
            identld.inc(s.dma_start(out=kb.identf[:, :], in_=identf_d))
            identld.inc(s.dma_start(out=kb.ones128[:, :], in_=ones_d))
        block.sync(load_const)
        iw = identld.n

        def init_consts(v):
            v.memset(kb.epsln[:, :], LN_EPS)
            v.memset(kb.epsz[:, :], 0.0)
            kb.dv.inc(v.memset(kb.epsgn[:, :], GN_EPS))
        block.vector(init_consts)
        cw = kb.dv.n

        def wident(t):
            t.wait_ge(identld.h, iw)
        block.tensor(wident)

        def wconst(sc):
            sc.wait_ge(kb.dv.h, cw)
        block.scalar(wconst)

        allsems = []
        with kb.phase():
            lnres = ph_ln(kb, x0, xnfm0, xntm_dram=xntm0, tag="0")
        allsems.append(lnres)
        with kb.phase():
            p0res = ph_proj0(kb, xnfm0, cos_d, sin_d, W0, qfm0, kfm0, v0,
                             fm_done=lnres["fm_done"], gate=kb.token(), tag="p0")
        allsems.append(p0res)
        with kb.phase():
            a0res = ph_attn(kb, qfm0, kfm0, v0, wo0, borep0, s0,
                            resid_dram=None, deps=p0res["qkv_done"],
                            gate=kb.token(), tag="a0")
        allsems.append(a0res)
        with kb.phase():
            f0res = ph_ffn0(kb, xnfm0, wgT, wuT, wdT, s0, x0, x1,
                            fm_done=lnres["fm_done"], s_done=a0res["s_done"],
                            gate=kb.token(), tag="f0")
        allsems.append(f0res)
        A_gn = kb.sb("A_gn", (128, 16), F32)
        B_gn = kb.sb("B_gn", (128, 16), F32)
        with kb.phase():
            ln1res = ph_ln(kb, x1, xnfm1, xntm_dram=xn1tm, tag="1",
                           gate=kb.token(), x_done=f0res["x1_done"])
        allsems.append(ln1res)
        with kb.phase():
            gnres = ph_gn(kb, xnfm1, gind_d, e2_d, gcols_d, bcols_d, A_gn, B_gn,
                          fm_done=ln1res["fm_done"], gate=kb.token(), tag="gn")
        allsems.append(gnres)
        with kb.phase():
            p1res = ph_proj1(kb, xnfm1, w1qkv, w1bq, w1bk, w1bvrep, A_gn, B_gn,
                             qfm1, kfm1, v1, fm_done=ln1res["fm_done"],
                             ab_done=gnres["ab_done"], gate=kb.token(), tag="p1")
        allsems.append(p1res)
        with kb.phase():
            a1res = ph_attn(kb, qfm1, kfm1, v1, w1o, w1borep, s1,
                            resid_dram=xn1tm, deps=p1res["qkv_done"],
                            gate=kb.token(), tag="a1")
        allsems.append(a1res)
        with kb.phase():
            p2res = ph_proj2(kb, xnfm1, w2qkv, qgrep_d, qfm2, kfm2, v2,
                             simall, minmax, fm_done=ln1res["fm_done"],
                             gate=kb.token(), tag="p2")
        allsems.append(p2res)

        rsem = kb._prod("rsem")
        mstp = p2res["mst"][0]
        mstn = mstp.n

        def bcast(g):
            g.wait_ge(mstp.h, mstn)
            rsem.inc(g.collective_compute(
                "AllGather", ALU.bypass,
                replica_groups=[list(range(NCORES))],
                ins=[minmax.opt()], outs=[allmm_d.opt()]))
        block.gpsimd(bcast)
        allsems.append({"rsem": [rsem]})

        with kb.phase():
            mres = ph_maskattn(kb, allmm_d, simall, qfm2, kfm2, v2,
                               c09_d, wflat, wproj_d, bprep_d, x1, s1, out_d,
                               deps=[(rsem, 1)], gate=kb.token(), tag="mf")
        allsems.append(mres)
        if abdev is not None:
            abst = kb.dsem("abst")

            def ab_store(sy):
                sy.wait_ge(kb.dv.h, gnres["ab_done"])
                abst.inc(sy.dma_start(out=abdev[:, 0:16], in_=A_gn[:, :]))
                abst.inc(sy.dma_start(out=abdev[:, 16:32], in_=B_gn[:, :]))
            block.sync(ab_store)
            allsems.append({"abst": [abst]})

        def fin(s):
            for sems in allsems:
                for key, grp in sems.items():
                    if key.endswith("_done"):
                        continue
                    for p in (grp if isinstance(grp, list) else [grp]):
                        s.wait_ge(p.h, p.n)
        block.sync(fin)
    return nc


def shard_inputs(inputs):
    c = host_constants()
    I = inputs
    f32 = lambda k: np.ascontiguousarray(np.asarray(I[k], dtype=np.float32))
    rep = lambda k: np.ascontiguousarray(np.tile(np.asarray(I[k], dtype=np.float32)[None, :], (128, 1)))
    shared = {
        "c_ident": c["c_ident"], "c_cos": c["c_cos"], "c_sin": c["c_sin"],
        "w0_wdq": f32("L0_attn_Wd_q_w"), "w0_wdkv": f32("L0_attn_Wd_kv_w"),
        "w0_wkr": f32("L0_attn_Wk_r_w"), "w0_wuq": f32("L0_attn_Wu_q_w"),
        "w0_wqr": f32("L0_attn_Wq_r_w"), "w0_wuk": f32("L0_attn_Wu_k_w"),
        "w0_wuv": f32("L0_attn_Wu_v_w"),
        "w0_bdq": np.ascontiguousarray(f32("L0_attn_Wd_q_b").reshape(4, 128).T),
        "w0_bdkv": np.ascontiguousarray(f32("L0_attn_Wd_kv_b").reshape(4, 128).T),
        "w0_brep_uq": rep("L0_attn_Wu_q_b"), "w0_brep_qr": rep("L0_attn_Wq_r_b"),
        "w0_brep_uk": rep("L0_attn_Wu_k_b"), "w0_brep_kr": rep("L0_attn_Wk_r_b"),
        "w0_brep_uv": rep("L0_attn_Wu_v_b"),
        "w0_wo": f32("L0_attn_Wo_w"), "w0_brep_o": rep("L0_attn_Wo_b"),
        "w0_wgT": np.ascontiguousarray(np.asarray(I["L0_ffn_w_gate"]).T),
        "w0_wuT": np.ascontiguousarray(np.asarray(I["L0_ffn_w_up"]).T),
        "w0_wdT": np.ascontiguousarray(np.asarray(I["L0_ffn_w_down"]).T),
        "c_gind": c["c_gind"], "c_e2": c["c_e2"],
        "c_identf": c["c_identf"], "c_ones": c["c_ones"],
        "w1_gcols": np.ascontiguousarray(f32("L1_attn_gn_gamma").reshape(8, 128).T),
        "w1_bcols": np.ascontiguousarray(f32("L1_attn_gn_beta").reshape(8, 128).T),
        "w1_qkv": f32("L1_attn_W_qkv"),
        "w1_bq": np.ascontiguousarray(f32("L1_attn_b_qkv")[0:1024].reshape(8, 128).T),
        "w1_bk": np.ascontiguousarray(f32("L1_attn_b_qkv")[1024:2048].reshape(8, 128).T),
        "w1_bvrep": np.ascontiguousarray(np.tile(f32("L1_attn_b_qkv")[None, 2048:3072], (128, 1))),
        "w1_wo": f32("L1_attn_W_o"), "w1_brep_o": rep("L1_attn_b_o"),
        "w2_qkv": f32("L1_ffn_W_qkv"),
        "w2_qgrep": np.ascontiguousarray(np.tile(
            np.asarray(I["L1_ffn_g_info"], np.float32)[0, 0][None, :], (128, 1))),
    }
    x = np.ascontiguousarray(np.asarray(inputs["L0_attn_x"], dtype=np.float32))
    maps = []
    for core in range(NCORES):
        m = dict(shared)
        m["x0"] = np.ascontiguousarray(x[2 * core:2 * core + 2].reshape(TOK, DM))
        maps.append(m)
    return maps


CROSS = ("qfm2", "kfm2", "v2", "simall", "minmax", "x1", "s1")


def build_l2(dev=()):
    nc = bass.Bass("TRN2", target_bir_lowering=False, debug=False)

    def dram(name, shape, dtype):
        kind = "ExternalOutput" if name in dev else "Internal"
        return nc.dram_tensor(name, list(shape), dtype, kind=kind).ap()

    ident_d = nc.dram_tensor("c_ident", [128, 128], BF16, kind="ExternalInput").ap()
    identf_d = nc.dram_tensor("c_identf", [128, 128], F32, kind="ExternalInput").ap()
    ones_d = nc.dram_tensor("c_ones", [1, 128], F32, kind="ExternalInput").ap()
    c09_d = nc.dram_tensor("c_c09", [1, 2], F32, kind="ExternalInput").ap()
    allmm_d = nc.dram_tensor("allmm", [8, 2], F32, kind="ExternalInput").ap()
    simall_d = nc.dram_tensor("simall", [128, 32], F32, kind="ExternalInput").ap()
    qfm2 = nc.dram_tensor("qfm2", [2, 8, 128, SEQ], BF16, kind="ExternalInput").ap()
    kfm2 = nc.dram_tensor("kfm2", [2, 8, 128, SEQ], BF16, kind="ExternalInput").ap()
    v2 = nc.dram_tensor("v2", [2, SEQ, 1040], BF16, kind="ExternalInput").ap()
    x1 = nc.dram_tensor("x1", [TOK, DM], F32, kind="ExternalInput").ap()
    s1 = nc.dram_tensor("s1", [TOK, DM], F32, kind="ExternalInput").ap()
    wproj_d = nc.dram_tensor("w2_proj", [1024, 1024], F32, kind="ExternalInput").ap()
    bprep_d = nc.dram_tensor("w2_bprep", [128, 1024], F32, kind="ExternalInput").ap()
    out_d = nc.dram_tensor("outp", [TOK, DM], F32, kind="ExternalOutput").ap()
    wflat = dram("wflat", (2, 32, 128), BF16)

    with ExitStack() as ctx:
        block = ctx.enter_context(nc.Block())
        kb = KB(nc, ctx, block)
        kb.ident = kb.sb("identsb", (128, 128), BF16)
        kb.identf = kb.sb("identfsb", (128, 128), F32)
        kb.ones128 = kb.sb("ones128sb", (1, 128), F32)
        identld = kb.dsem("identld")

        def load_const(s):
            identld.inc(s.dma_start(out=kb.ident[:, :], in_=ident_d))
            identld.inc(s.dma_start(out=kb.identf[:, :], in_=identf_d))
            identld.inc(s.dma_start(out=kb.ones128[:, :], in_=ones_d))
        block.sync(load_const)
        iw = identld.n

        def wident(t):
            t.wait_ge(identld.h, iw)
        block.tensor(wident)

        with kb.phase():
            mres = ph_maskattn(kb, allmm_d, simall_d, qfm2, kfm2, v2,
                               c09_d, wflat, wproj_d, bprep_d, x1, s1, out_d,
                               tag="mf")

        def fin(s):
            for key, grp in mres.items():
                if key.endswith("_done"):
                    continue
                for p in (grp if isinstance(grp, list) else [grp]):
                    s.wait_ge(p.h, p.n)
        block.sync(fin)
    return nc


def l2_shared(inputs):
    c = host_constants()
    I = inputs
    f32 = lambda k: np.ascontiguousarray(np.asarray(I[k], dtype=np.float32))
    rep = lambda k: np.ascontiguousarray(np.tile(np.asarray(I[k], dtype=np.float32)[None, :], (128, 1)))
    return {
        "c_ident": c["c_ident"], "c_identf": c["c_identf"], "c_ones": c["c_ones"],
        "c_c09": np.array([[0.9, -0.1]], dtype=np.float32),
        "w2_proj": f32("L1_ffn_W_proj"), "w2_bprep": rep("L1_ffn_b_proj"),
    }


def kernel(**inputs):
    maps = shard_inputs(inputs)
    sh2 = l2_shared(inputs)
    for m in maps:
        for k in ("c_c09", "w2_proj", "w2_bprep"):
            m[k] = sh2[k]
    nc1 = build_l1()
    res = run_bass_kernel_spmd(nc1, maps, core_ids=list(range(NCORES)))
    out = np.concatenate(
        [np.asarray(res.results[c]["outp"], dtype=np.float32).reshape(2, SEQ, DM)
         for c in range(NCORES)], axis=0)
    return out
